# version 01
# baseline (speedup 1.0000x reference)
import numpy as np
import ml_dtypes
import concourse.bass as bass
import concourse.mybir as mybir
from concourse.bass_core_rust import IndirectOffsetOnAxis
from concourse.bass_utils import run_bass_kernel_spmd
from contextlib import ExitStack

F32 = mybir.dt.float32
BF16 = mybir.dt.bfloat16
U8 = mybir.dt.uint8
U32 = mybir.dt.uint32
I32 = mybir.dt.int32
AF = mybir.ActivationFunctionType
ALU = mybir.AluOpType
AXX = mybir.AxisListType.X

NH = 16
NTC = 8
NDC = 16
CEXP = 20.0
CINV = 1.0 / float(np.sqrt(128.0))
SLOPES = [1.0 / 2 ** (k / 2.0) for k in range(16)]
EPS = 1e-5
N_SUB = 4
DEV_STAGE = 0
DEV_KEYS = 0


def qbase(parity, slot):
    return 512 * parity if slot == 0 else 512 * (3 - parity)


def global_blocks(parity):
    return (0, 3) if parity == 0 else (1, 2)


def kc_owner(kc):
    g = kc // 4
    if g in (0, 3):
        return 0, (kc if g == 0 else kc - 8)
    return 1, kc - 4


def attn_chunks(layer):
    out = []
    for head in range(NH):
        slope = SLOPES[head]
        for slot in range(2):
            kcmax = 7 if slot == 0 else 15
            for kc in range(kcmax + 1):
                contrib = False
                full_both = True
                for p in (0, 1):
                    dq = qbase(p, slot) - 128 * kc
                    valid = dq >= -384
                    far = (layer == 0) and slope * max(0, dq - 127) > 120.0
                    if valid and not far:
                        contrib = True
                    if dq < 127:
                        full_both = False
                if contrib:
                    out.append((head, slot, kc, not full_both))
    return out



class Sem:
    def __init__(self, nc, name):
        self.h = nc.alloc_semaphore(name=name)
        self.n = 0


class KB:
    ENGS = ("pe", "act", "dve", "pool", "sp")

    def __init__(self, nc):
        self.nc = nc
        self.ops = {e: [] for e in self.ENGS}
        self.sem = {s: Sem(nc, "s_" + s)
                    for s in ("sdma", "gdma", "pe", "act", "dve", "pool", "cc")}
        self.ctx = ExitStack()
        self._sbuf_bytes = 0
        self.psum = [self.ctx.enter_context(
            nc.psum_tensor(f"pb{i}", [128, 512], F32)) for i in range(8)]
        self.bank_war = [[] for _ in range(8)]
        self.bank_rdy = [0] * 8

    def sb(self, name, shape, dtype):
        self._sbuf_bytes += int(np.prod(shape[1:])) * mybir.dt.size(dtype)
        return self.ctx.enter_context(self.nc.sbuf_tensor(name, shape, dtype))

    def _rec(self, eng, fn, sem=None, v=1):
        s = self.sem[sem] if sem else None
        if s is not None:
            s.n += v

            def f(e, fn=fn, s=s, v=v):
                fn(e).then_inc(s.h, v)
        else:
            def f(e, fn=fn):
                fn(e)
        self.ops[eng].append(f)
        return s.n if s is not None else None

    def pe(self, fn, sem=None, v=1):
        return self._rec("pe", fn, sem, v)

    def act(self, fn, sem="act", v=1):
        return self._rec("act", fn, sem, v)

    def dve(self, fn, sem="dve", v=1):
        return self._rec("dve", fn, sem, v)

    def pool(self, fn, sem="pool", v=1):
        return self._rec("pool", fn, sem, v)

    def sp(self, fn, sem=None, v=1):
        return self._rec("sp", fn, sem, v)

    def wait(self, eng, sem, n=None):
        s = self.sem[sem]
        n = s.n if n is None else n
        assert n <= s.n, f"wait on future count {sem} {n} > {s.n}"
        if n <= 0:
            return
        self.ops[eng].append(lambda e, s=s, n=n: e.wait_ge(s.h, n))

    def dma(self, q, out, in_, **kw):
        sem = "sdma" if q == "sp" else "gdma"
        return self._rec(q, lambda e: e.dma_start(out=out, in_=in_, **kw), sem, 16)

    def mm_group(self, bank, mms, tail=None, out=None):
        for dep in self.bank_war[bank]:
            self.wait("pe", *dep)
        self.bank_war[bank] = []
        seq = list(mms) + list(tail or [])
        n = len(seq)
        pt = out if out is not None else self.psum[bank][:, :]
        for i, (l, r) in enumerate(seq):
            last = i == n - 1
            self.pe(lambda e, l=l, r=r, s=(i == 0), t=last, pt=pt:
                    e.matmul(pt, l, r, start=s, stop=t),
                    sem="pe" if last else None)
        self.bank_rdy[bank] = self.sem["pe"].n
        return self.bank_rdy[bank]

    def finalize(self, out_waits):
        nc, ops, sems = self.nc, self.ops, self.sem
        with nc.Block() as block:

            @block.tensor
            def _(e):
                for f in ops["pe"]:
                    f(e)

            @block.scalar
            def _(e):
                for f in ops["act"]:
                    f(e)

            @block.vector
            def _(e):
                for f in ops["dve"]:
                    f(e)

            @block.gpsimd
            def _(e):
                for f in ops["pool"]:
                    f(e)

            @block.sync
            def _(e):
                for f in ops["sp"]:
                    f(e)
                for s, n in out_waits:
                    e.wait_ge(sems[s].h, n)



def _tile_qk(wqkv):
    out = np.empty((32, 128, 2048), dtype=np.float32)
    w = wqkv.reshape(16, 128, 48, 128)
    for j in range(32):
        out[j] = w[:, :, j, :].transpose(1, 0, 2).reshape(128, 2048)
    return out


def _tile_v(wv):
    w = wv.reshape(16, 128, 4, 512)
    return np.ascontiguousarray(w.transpose(2, 1, 0, 3).reshape(4, 128, 8192))


def _tile_wo(wo):
    w = wo.reshape(16, 128, 4, 512)
    return np.ascontiguousarray(w.transpose(1, 0, 2, 3).reshape(128, 32768))


def _tile_lhs64(w, transpose):
    if transpose:
        v = w.reshape(64, 128, 16, 128)
        return np.ascontiguousarray(v.transpose(0, 3, 2, 1).reshape(64, 128, 2048))
    v = w.reshape(16, 128, 64, 128)
    return np.ascontiguousarray(v.transpose(2, 1, 0, 3).reshape(64, 128, 2048))


def _tile_down(w, transpose):
    if transpose:
        v = w.reshape(2048, 64, 128)
        return np.ascontiguousarray(v.transpose(1, 2, 0))
    return np.ascontiguousarray(w.reshape(64, 128, 2048))


def _int32_to_u8(w):
    return np.ascontiguousarray(w.view(np.uint8).reshape(*w.shape, 4)[..., 0])


def _w01_strip(parity, slot):
    kcmax = 7 if slot == 0 else 15
    k0 = qbase(parity, slot) - 128 * kcmax
    i = np.arange(128)[:, None]
    x = np.arange(1408)[None, :]
    return ((i - x) <= k0).astype(np.uint8)


def _l0_bias_table(parity, chunks):
    cols = []
    i = np.arange(128)
    for (head, slot, kc, _m) in chunks:
        s = SLOPES[head]
        cols.append(s * (128 * kc + i - qbase(parity, slot)) - CEXP)
    return np.stack(cols, axis=1).astype(np.float32)


def _routing_offsets(core):
    b, h = core // 2, core % 2
    offs = []
    for side in range(2):
        for blk in range(2):
            s0 = global_blocks(h)[blk] * 512
            n0 = b * 2048 + s0
            m0 = n0
            prod_batch = (m0 % 4096 * 0) + (m0 // 4096) + (2 if side == 1 else 0)
            sp = (m0 % 4096) // 2
            blk_p = sp // 512
            prod_parity = 1 if blk_p in (1, 2) else 0
            prod_core = 2 * (prod_batch % 2) + prod_parity if False else None
            pb_in_t = m0 // 4096
            prod_core = 2 * (pb_in_t + (2 if side == 1 else 0) * 0) + prod_parity
            prod_core = 2 * (pb_in_t + (2 if side == 1 else 0)) + prod_parity
            if prod_parity == 0:
                tloc = sp if blk_p == 0 else sp - 1024
            else:
                tloc = sp - 512
            offs.append((prod_core * 2048 + 2 * tloc) * 64)
    return np.array(offs + [0] * 4, dtype=np.uint32).reshape(1, 8)


def stage_inputs(inputs):
    f32 = np.float32
    g = {}
    wqkv = np.asarray(inputs["L0_attn_Wqkv"], f32)
    g["l0qk_t"] = _tile_qk(wqkv)
    g["l0v_t"] = _tile_v(np.ascontiguousarray(wqkv[:, 4096:6144]))
    g["l0wo_t"] = _tile_wo(np.asarray(inputs["L0_attn_Wo"], f32))
    bqkv = np.asarray(inputs["L0_attn_bqkv"], f32)
    g["l0bqk"] = np.ascontiguousarray(bqkv[:4096].reshape(32, 128).T)
    g["l0bv"] = np.ascontiguousarray(bqkv[4096:].reshape(1, 2048))
    g["l0bo"] = np.ascontiguousarray(
        np.asarray(inputs["L0_attn_bo"], f32).reshape(1, 2048))
    g["l0g_t"] = _tile_lhs64(_int32_to_u8(np.asarray(inputs["L0_ffn_gate_w"])), True)
    g["l0u_t"] = _tile_lhs64(_int32_to_u8(np.asarray(inputs["L0_ffn_up_w"])), True)
    g["l0d_t"] = _tile_down(_int32_to_u8(np.asarray(inputs["L0_ffn_down_w"])), True)
    g["l0gs"] = np.ascontiguousarray(
        np.asarray(inputs["L0_ffn_gate_scale"], f32).reshape(64, 128).T)
    g["l0us"] = np.ascontiguousarray(
        np.asarray(inputs["L0_ffn_up_scale"], f32).reshape(64, 128).T)
    g["l0ds"] = np.ascontiguousarray(
        np.asarray(inputs["L0_ffn_down_scale"], f32).reshape(1, 2048))

    wq = np.asarray(inputs["L1_attn_W_Q"], f32)
    wk = np.asarray(inputs["L1_attn_W_K"], f32)
    wv = np.asarray(inputs["L1_attn_W_V"], f32)
    g["l1q_t"] = np.ascontiguousarray(
        wq.transpose(0, 2, 1).reshape(16, 16, 128, 128)
        .transpose(0, 2, 1, 3).reshape(16, 128, 2048))
    g["l1k_t"] = np.ascontiguousarray(
        wk.transpose(0, 2, 1).reshape(16, 16, 128, 128)
        .transpose(0, 2, 1, 3).reshape(16, 128, 2048))
    wv2 = np.ascontiguousarray(wv.transpose(2, 0, 1).reshape(2048, 2048))
    g["l1v_t"] = _tile_v(wv2)
    g["l1wo_t"] = _tile_wo(
        np.ascontiguousarray(np.asarray(inputs["L1_attn_W_O"], f32).T))
    g["l1g_t"] = _tile_lhs64(np.asarray(inputs["L1_ffn_gate_w"], f32), False)
    g["l1u_t"] = _tile_lhs64(np.asarray(inputs["L1_ffn_up_w"], f32), False)
    g["l1d_t"] = _tile_down(np.asarray(inputs["L1_ffn_down_w"], f32), False)
    wqr = np.asarray(inputs["L1_ffn_W_q"], f32)
    g["l1wq_t"] = np.ascontiguousarray(
        wqr.reshape(16, 128, 128).transpose(1, 0, 2).reshape(128, 2048))

    g["l1de"] = np.ascontiguousarray(np.asarray(inputs["L1_ffn_down_embed"], f32))
    g["l1ue"] = np.ascontiguousarray(np.asarray(inputs["L1_ffn_up_embed"], f32))

    x = np.asarray(inputs["L0_attn_x"], f32)
    chunks0 = attn_chunks(0)
    in_maps = []
    for c in range(8):
        b, h = c // 2, c % 2
        m = dict(g)
        m["x0"] = np.ascontiguousarray(np.concatenate(
            [x[b, blk * 512:(blk + 1) * 512] for blk in global_blocks(h)], 0))
        m["w01a"] = np.ascontiguousarray(_w01_strip(h, 0))
        m["w01b"] = np.ascontiguousarray(_w01_strip(h, 1))
        m["l0bias"] = _l0_bias_table(h, chunks0)
        m["cidx"] = _routing_offsets(c)
        m["l1keys"] = np.ascontiguousarray(
            np.asarray(inputs["L1_ffn_keys"], np.float32)[c // 4])
        in_maps.append(m)
    return in_maps


def unstage_output(res_list):
    out = np.empty((4, 2048, 2048), dtype=np.float32)
    for c in range(8):
        b, h = c // 2, c % 2
        y = res_list[c]["out"]
        for i, blk in enumerate(global_blocks(h)):
            out[b, blk * 512:(blk + 1) * 512] = y[i * 512:(i + 1) * 512]
    return out



def build_program(n_sub=None):
    if n_sub is None:
        n_sub = N_SUB
    nc = bass.Bass("TRN2", target_bir_lowering=False)
    K = KB(nc)
    t = {}

    def din(name, shape, dtype):
        t[name] = nc.dram_tensor(name, list(shape), dtype, kind="ExternalInput")

    din("x0", (1024, 2048), F32)
    din("l0qk_t", (32, 128, 2048), F32)
    din("l0v_t", (4, 128, 8192), F32)
    din("l0wo_t", (128, 32768), F32)
    din("l0bqk", (128, 32), F32)
    din("l0bv", (1, 2048), F32)
    din("l0bo", (1, 2048), F32)
    din("l0g_t", (64, 128, 2048), U8)
    din("l0u_t", (64, 128, 2048), U8)
    din("l0d_t", (64, 128, 2048), U8)
    din("l0gs", (128, 64), F32)
    din("l0us", (128, 64), F32)
    din("l0ds", (1, 2048), F32)
    din("l1q_t", (16, 128, 2048), F32)
    din("l1k_t", (16, 128, 2048), F32)
    din("l1v_t", (4, 128, 8192), F32)
    din("l1wo_t", (128, 32768), F32)
    din("l1g_t", (64, 128, 2048), F32)
    din("l1u_t", (64, 128, 2048), F32)
    din("l1d_t", (64, 128, 2048), F32)
    din("l1wq_t", (128, 2048), F32)
    din("l1keys", (64, 64), F32)
    din("l1de", (4096, 2048), F32)
    din("l1ue", (4096, 2048), F32)
    din("w01a", (128, 1408), U8)
    din("w01b", (128, 1408), U8)
    din("l0bias", (128, len(attn_chunks(0))), F32)
    din("cidx", (1, 8), U32)
    t["out"] = nc.dram_tensor("out", [1024, 2048], F32, kind="ExternalOutput")

    dbg = "ExternalOutput" if DEV_STAGE == 1 else "Internal"
    if DEV_STAGE in (3, 4, 5, 6, 7, 8, 9, 10, 12, 13):
        t["dbg_rt"] = nc.dram_tensor("dbg_rt", [16384, 64], F32,
                                     kind="ExternalOutput")
        t["dbg_e"] = nc.dram_tensor("dbg_e", [1024, 2048], F32,
                                    kind="ExternalOutput")
    if DEV_STAGE > 0:
        t["dbg_z"] = nc.dram_tensor("dbg_z", [128, 16384], BF16,
                                    kind="ExternalOutput")
        t["dbg_ln"] = nc.dram_tensor("dbg_ln", [128, 16384], BF16,
                                     kind="ExternalOutput")
    t["x_scr"] = nc.dram_tensor("x_scr", [1024, 2048], F32)
    t["h_scr"] = nc.dram_tensor("h_scr", [64, 128, 1024], BF16)
    t["e_scr"] = nc.dram_tensor("e_scr", [1024, 2048], F32)
    t["qdram"] = nc.dram_tensor("qdram", [16, 2, 128, 512], BF16, kind=dbg)
    t["cc_kv"] = nc.dram_tensor("cc_kv", [256, 16384], BF16, kind=dbg)
    t["kv_all"] = nc.dram_tensor("kv_all", [512, 16384], BF16)
    t["cc_rt"] = nc.dram_tensor("cc_rt", [2048, 64], F32)
    t["rt_all"] = nc.dram_tensor("rt_all", [16384, 64], F32)

    sb = {}
    sb["A4L"] = K.sb("A4L", [128, 16384], BF16)
    sb["A8"] = K.sb("A8", [128, 32768], BF16)
    sb["XCH"] = K.sb("XCH", [128, 2 * 2048], F32)
    sb["LNTM"] = K.sb("LNTM", [128, 2048], BF16)
    sb["FTC"] = K.sb("FTC", [128, 2048], F32)
    sb["STAT"] = K.sb("STAT", [128, 64], F32)
    sb["CONSTS"] = K.sb("CONSTS", [128, 2], F32)
    sb["WT"] = K.sb("WT", [128, 4096], BF16)
    sb["WT2"] = K.sb("WT2", [128, 2 * 2048], BF16)
    sb["ZERO"] = K.sb("ZERO", [128, 512], BF16)
    sb["BQK"] = K.sb("BQK", [128, 32], F32)
    sb["ONEC"] = K.sb("ONEC", [128, 1], BF16)
    sb["IDN"] = K.sb("IDN", [128, 128], BF16)
    sb["GS"] = K.sb("GS", [128, 64], F32)
    sb["US"] = K.sb("US", [128, 64], F32)
    sb["KEYS"] = K.sb("KEYS", [128, 128], BF16)
    sb["IOTA8"] = K.sb("IOTA8", [128, 8], F32)
    sb["DEB"] = K.sb("DEB", [128, 2 * 2048], BF16)
    sb["UEB"] = K.sb("UEB", [128, 2 * 2048], BF16)
    sb["CIDX"] = K.sb("CIDX", [1, 8], U32)
    sb["QR"] = K.sb("QR", [64, 2048], BF16)
    rows1 = K.sb("ROWS1", [128, 2176], F32)
    sb["BVROW"] = rows1[0:1, 0:2048]
    sb["ONE_A"] = rows1[0:1, 2048:2176]
    sb["BOROW"] = rows1[32:33, 0:2048]
    sb["ONE_B"] = rows1[32:33, 2048:2176]
    sb["DSROW"] = rows1[64:65, 0:2048]
    sb["ONE_C"] = rows1[64:65, 2048:2176]
    rows2 = K.sb("ROWS2", [128, 2560], F32)
    sb["SLOPE"] = rows2[0:1, 0:2048]
    sb["ALIROW"] = rows2[0:1, 2048:2560]
    sb["REC"] = rows2[32:33, 0:1024]
    sb["DEN"] = rows2[32:33, 1024:2048]
    sb["ONE_D"] = rows2[32:33, 2048:2176]
    sb["RSX"] = rows2[64:65, 0:1024]
    sb["RSH"] = rows2[64:65, 1024:2048]
    sb["ONE_E"] = rows2[64:65, 2048:2560]
    at = K.sb("AT", [128, 10240], U8)
    at_bf, at_f = at.ap().bitcast(BF16), at.ap().bitcast(F32)
    sb["QST"] = at_bf[:, 0:1024]
    sb["QT"] = at_bf[:, 1024:2048]
    sb["EB"] = at_bf[:, 2048:3072]
    sb["EM"] = at_bf[:, 3072:4096]
    sb["RD"] = at_f[:, 2048:2560]
    sb["XBF"] = at_bf[:, 0:2048]
    sb["RT"] = at_f[:, 1024:2048]
    sb["TK"] = at_f[:, 2048:2560]
    aw = K.sb("AW", [128, 8192], U8)
    aw_bf, aw_f = aw.ap().bitcast(BF16), aw.ap().bitcast(F32)
    aw_u = aw.ap().bitcast(U32)
    sb["W01A"] = aw.ap()[:, 0:1408]
    sb["W01B"] = aw.ap()[:, 1408:2816]
    sb["BIAS0"] = aw_f[:, 1408:1408 + len(attn_chunks(0))]
    sb["TKU"] = aw_u[:, 512:576]
    sb["EWID"] = aw_u[:, 576:584]
    sb["EWW"] = aw_f[:, 584:592]
    sb["RS4"] = aw_f[:, 592:608]
    ae = K.sb("AE", [128, 8192], U8)
    sb["DSB"] = ae.ap().bitcast(F32)
    sb["EACC"] = ae.ap().bitcast(F32)

    st = {"sb": sb, "t": t}
    emit_prologue(K, st)
    emit_attention(K, st, layer=0, x_src=t["x0"], x_dst=t["x_scr"],
                   final=(n_sub == 1))
    if n_sub >= 2:
        emit_ffn_l0(K, st, x_src=t["x_scr"], x_dst=t["x_scr"],
                    final=(n_sub == 2))
    if n_sub >= 3:
        emit_attention(K, st, layer=1, x_src=t["x_scr"], x_dst=t["x_scr"],
                       final=(n_sub == 3))
    if n_sub >= 4:
        emit_ffn_l1(K, st, x_src=t["x_scr"], x_dst=t["x_scr"], final=True)

    K.finalize(out_waits=[("sdma", K.sem["sdma"].n), ("gdma", K.sem["gdma"].n)])
    return nc



def emit_prologue(K, st):
    sb, t = st["sb"], st["t"]
    K.dma("sp", sb["W01A"], t["w01a"][:, :])
    K.dma("sp", sb["W01B"], t["w01b"][:, :])
    K.dma("sp", sb["BIAS0"], t["l0bias"][:, :])
    K.dma("sp", sb["BQK"][:, :], t["l0bqk"][:, :])
    K.dma("sp", sb["BVROW"], t["l0bv"][0:1, :])
    K.dma("sp", sb["BOROW"], t["l0bo"][0:1, :])
    K.dma("sp", sb["DSROW"], t["l0ds"][0:1, :])
    K.dma("sp", sb["GS"][:, :], t["l0gs"][:, :])
    K.dma("sp", sb["US"][:, :], t["l0us"][:, :])
    K.dma("sp", sb["CIDX"][:, :], t["cidx"][:, :])
    st["n_pro_sdma"] = K.sem["sdma"].n
    K.dma("pool", sb["KEYS"][0:64, 0:64], t["l1keys"][:, :])
    K.pool(lambda e: e.iota(sb["ALIROW"].bitcast(I32), [[1, 512]], base=0, channel_multiplier=0))
    for nm in ("ONE_A", "ONE_B", "ONE_C", "ONE_D", "ONE_E"):
        K.pool(lambda e, nm=nm: e.memset(sb[nm], 1.0))
    K.pool(lambda e: e.memset(sb["CONSTS"][:, 0:1], EPS))
    K.pool(lambda e: e.memset(sb["CONSTS"][:, 1:2], -CEXP))
    K.pool(lambda e: e.memset(sb["ONEC"][:, :], 1.0))
    K.pool(lambda e: e.memset(sb["ZERO"][:, :], 0.0))
    K.pool(lambda e: e.memset(sb["IDN"][:, :], 1.0))
    for h in range(16):
        K.pool(lambda e, h=h: e.memset(
            sb["SLOPE"][0:1, h * 128:(h + 1) * 128], SLOPES[h]))
    K.pool(lambda e: e.iota(sb["IOTA8"][:, :].bitcast(I32), [[1, 8]], base=0, channel_multiplier=0))
    K.wait("pool", "pool")
    K.pool(lambda e: e.affine_select(sb["LNTM"][:, 0:128], sb["IDN"][:, :],
                                     [[-1, 128]], ALU.is_ge, 0.0,
                                     base=0, channel_multiplier=1))
    K.wait("pool", "pool")
    K.pool(lambda e: e.affine_select(sb["IDN"][:, :], sb["LNTM"][:, 0:128],
                                     [[1, 128]], ALU.is_ge, 0.0,
                                     base=0, channel_multiplier=-1))
    npool = K.sem["pool"].n
    K.wait("dve", "pool", npool)
    K.dve(lambda e: e.tensor_scalar_mul(sb["ALIROW"],
                                        sb["ALIROW"].bitcast(I32), -1.0 / CINV))
    K.dve(lambda e: e.tensor_copy(sb["IOTA8"][:, :],
                                  sb["IOTA8"][:, :].bitcast(I32)))
    st["n_pro_dve"] = K.sem["dve"].n
    st["n_pro_pool"] = npool



def emit_ln_chunk(K, st, tc, src_ap, out_ap, save_stats=None):
    sb = st["sb"]
    S = sb["STAT"]
    ssum, sqs = S[:, tc:tc + 1], S[:, 8 + tc:9 + tc]
    mean, var = S[:, 16 + tc:17 + tc], S[:, 24 + tc:25 + tc]
    sd, rr = S[:, 32 + tc:33 + tc], S[:, 40 + tc:41 + tc]
    nmr, msq = S[:, 48 + tc:49 + tc], S[:, 56 + tc:57 + tc]
    K.dve(lambda e: e.reduce_sum(ssum, src_ap, axis=AXX))
    K.act(lambda e: e.activation(out=sb["LNTM"][:, 0:2048], in_=src_ap,
                                 func=AF.Square, accum_out=sqs))
    K.wait("dve", "dve")
    K.dve(lambda e: e.tensor_scalar_mul(mean, ssum, 1.0 / 2048.0))
    K.wait("dve", "dve")
    K.dve(lambda e: e.tensor_mul(msq, mean, mean))
    K.wait("dve", "act")
    K.wait("dve", "dve")
    K.dve(lambda e: e.scalar_tensor_tensor(var, sqs, 1.0 / 2048.0, msq,
                                           ALU.mult, ALU.subtract))
    K.wait("act", "dve")
    K.act(lambda e: e.activation(out=sd, in_=var, func=AF.Sqrt,
                                 bias=sb["CONSTS"][:, 0:1]))
    K.wait("dve", "act")
    K.dve(lambda e: e.reciprocal(rr, sd))
    K.wait("dve", "dve")
    K.dve(lambda e: e.tensor_scalar(nmr, mean, rr, -1.0, ALU.mult, ALU.mult))
    K.wait("dve", "dve")
    K.dve(lambda e: e.tensor_scalar(out_ap, src_ap, rr, nmr, ALU.mult, ALU.add))
    if save_stats is not None:
        name, off = save_stats
        K.wait("dve", "dve")
        K.dve(lambda e: e.tensor_copy(sb[name][:, off + tc:off + tc + 1], rr))
        K.dve(lambda e: e.tensor_copy(sb[name][:, off + 8 + tc:off + 9 + tc], nmr))
    return K.sem["dve"].n


def emit_ln_transpose(K, st, x_src, save_stats=None):
    sb = st["sb"]
    K.wait("act", "pool", st["n_pro_pool"])
    K.wait("pe", "pool", st["n_pro_pool"])
    K.wait("dve", "pool", st["n_pro_pool"])
    for tc in range(NTC):
        par = tc % 2
        xc = sb["XCH"][:, par * 2048:(par + 1) * 2048]
        lc = sb["LNTM"][:, 0:2048]
        K.wait("sp", "dve")
        K.wait("sp", "pe")
        K.dma("sp", xc, x_src[tc * 128:(tc + 1) * 128, :])
        K.wait("dve", "sdma")
        K.wait("act", "sdma")
        K.wait("dve", "pe")
        ndve = emit_ln_chunk(K, st, tc, xc, lc, save_stats)
        K.wait("pe", "dve", ndve)
        for half in range(2):
            bank = 6 + half
            for dep in K.bank_war[bank]:
                K.wait("pe", *dep)
            K.bank_war[bank] = []
            pb = K.psum[bank].ap().bitcast(BF16)
            for k in range(8):
                dc = half * 8 + k
                src = sb["LNTM"][:, dc * 128:(dc + 1) * 128]
                K.pe(lambda e, o=pb[:, k * 128:(k + 1) * 128], s=src:
                     e.transpose(o, s, sb["IDN"][:, :]),
                     sem="pe" if k == 7 else None)
            npe = K.sem["pe"].n
            K.wait("dve", "pe", npe)
            dst = bass.AP(sb["A4L"], (half * 8) * 1024 + tc * 128,
                          [[16384, 128], [1024, 8], [1, 128]])
            K.dve(lambda e, d=dst, s=pb: e.tensor_copy(d, s))
            K.bank_war[bank].append(("dve", K.sem["dve"].n))



def emit_attention(K, st, layer, x_src, x_dst, final):
    sb, t = st["sb"], st["t"]
    A4L, A8 = sb["A4L"], sb["A8"]

    emit_ln_transpose(K, st, x_src)
    nln = K.sem["dve"].n
    K.wait("pe", "dve", nln)

    wt_last = [0, 0]
    qst_war = [0, 0]
    for j in range(32):
        par = j % 2
        wtile = sb["WT2"][:, par * 2048:(par + 1) * 2048]
        K.wait("pool", "pe", wt_last[par])
        if layer == 0:
            K.dma("pool", wtile, t["l0qk_t"][j, :, :])
        else:
            src = t["l1q_t"] if j < 16 else t["l1k_t"]
            K.dma("pool", wtile, src[j % 16, :, :])
        ng = K.sem["gdma"].n
        K.wait("pe", "gdma", ng)
        for u in range(2):
            bank = (2 * j + u) % 4
            mms = [(wtile[:, dc * 128:(dc + 1) * 128],
                    A4L[:, dc * 1024 + u * 512:dc * 1024 + (u + 1) * 512])
                   for dc in range(NDC)]
            K.mm_group(bank, mms)
            K.wait("dve", "pe", K.bank_rdy[bank])
            if j < 16:
                par2 = (2 * j + u) % 2
                stg = sb["QST"][:, par2 * 512:(par2 + 1) * 512]
                K.wait("dve", "sdma", qst_war[par2])
                if layer == 0:
                    K.dve(lambda e, o=stg, p=K.psum[bank], bb=sb["BQK"][:, j:j + 1]:
                          e.tensor_scalar_add(o, p[:, :], bb))
                else:
                    K.dve(lambda e, o=stg, p=K.psum[bank]:
                          e.tensor_copy(o, p[:, :]))
                K.bank_war[bank].append(("dve", K.sem["dve"].n))
                K.wait("sp", "dve")
                K.dma("sp", t["qdram"][j, u, :, :], stg)
                qst_war[par2] = K.sem["sdma"].n
            else:
                jc = j - 16
                dst = bass.AP(A8, (4 * u) * 2048 + jc * 128,
                              [[32768, 128], [2048, 4], [1, 128]])
                if layer == 0:
                    K.dve(lambda e, o=dst, p=K.psum[bank], bb=sb["BQK"][:, j:j + 1]:
                          e.tensor_scalar_add(o, p[:, :], bb))
                else:
                    K.dve(lambda e, o=dst, p=K.psum[bank]:
                          e.tensor_copy(o, p[:, :]))
                K.bank_war[bank].append(("dve", K.sem["dve"].n))
        wt_last[par] = K.sem["pe"].n

    vsrc = t["l0v_t"] if layer == 0 else t["l1v_t"]
    for jv in range(4):
        if layer == 0:
            K.wait("pe", "sdma", st["n_pro_sdma"])
        for half in range(2):
            K.wait("pool", "pe", K.sem["pe"].n)
            K.dma("pool", sb["WT"][:, :],
                  vsrc[jv, :, half * 4096:(half + 1) * 4096])
            ng = K.sem["gdma"].n
            K.wait("pe", "gdma", ng)
            for tc in range(NTC):
                bank = tc
                if half == 0:
                    for dep in K.bank_war[bank]:
                        K.wait("pe", *dep)
                    K.bank_war[bank] = []
                pt = K.psum[bank][:, :]
                for dci in range(8):
                    dc = half * 8 + dci
                    l = A4L[:, dc * 1024 + tc * 128:dc * 1024 + (tc + 1) * 128]
                    r = sb["WT"][:, dci * 512:(dci + 1) * 512]
                    lastmm = (half == 1 and dci == 7 and layer != 0)
                    K.pe(lambda e, l=l, r=r, pt=pt,
                         st_=(half == 0 and dci == 0), tl=lastmm:
                         e.matmul(pt, l, r, start=st_, stop=tl),
                         sem="pe" if dci == 7 else None)
                if half == 1 and layer == 0:
                    K.pe(lambda e, pt=pt,
                         r=sb["BVROW"][0:1, jv * 512:(jv + 1) * 512]:
                         e.matmul(pt, sb["ONE_A"], r,
                                  start=False, stop=True), sem="pe")
                if half == 1:
                    K.bank_rdy[bank] = K.sem["pe"].n
                    dst = A8[:, 16384 + tc * 2048 + jv * 512:
                             16384 + tc * 2048 + (jv + 1) * 512]
                    K.wait("act", "pe", K.bank_rdy[bank])
                    K.act(lambda e, d=dst, p=K.psum[bank]:
                          e.activation(out=d, in_=p[:, :], func=AF.Copy))
                    K.bank_war[bank].append(("act", K.sem["act"].n))

    K.wait("sp", "act")
    K.wait("sp", "dve")
    K.dma("sp", t["cc_kv"][0:128, :], A8[:, 0:16384])
    K.dma("sp", t["cc_kv"][128:256, :], A8[:, 16384:32768])
    nsd = K.sem["sdma"].n
    if DEV_STAGE == 1:
        K.wait("sp", "dve")
        K.dma("sp", t["dbg_ln"][:, :], A4L[:, :])
        return
    K.wait("pool", "sdma", nsd)
    K.pool(lambda e: e.collective_compute(
        "AllGather", ALU.bypass,
        replica_groups=[[0, 1], [2, 3], [4, 5], [6, 7]],
        ins=[t["cc_kv"].ap()],
        outs=[t["kv_all"].ap()]), sem="cc")
    ncc = K.sem["cc"].n

    chunks = attn_chunks(layer)
    zslots = {}
    for ch in chunks:
        zslots.setdefault((ch[0], ch[1]), []).append(ch)
    bias_col = {ch: i for i, ch in enumerate(attn_chunks(0))}
    ebuf_war = [0, 0]

    for hg in range(2):
        K.wait("sp", "cc", ncc)
        K.wait("sp", "pe")
        for kc in range(16):
            r, lt = kc_owner(kc)
            base = lt * 2048 + hg * 1024
            K.dma("sp", A8[:, kc * 1024:(kc + 1) * 1024],
                  t["kv_all"][r * 256:r * 256 + 128, base:base + 1024])
            K.dma("sp", A8[:, 16384 + kc * 1024:16384 + (kc + 1) * 1024],
                  t["kv_all"][r * 256 + 128:r * 256 + 256, base:base + 1024])
        nkv = K.sem["sdma"].n
        K.wait("pe", "sdma", nkv)

        for head in range(hg * 8, hg * 8 + 8):
            for slot in range(2):
                lst = zslots.get((head, slot))
                if not lst:
                    continue
                hpar = (head * 2 + slot) % 2
                zbank, dbank, rbank = 2 + hpar, 0 + hpar, 4 + hpar
                qt = sb["QT"][:, hpar * 512:(hpar + 1) * 512]
                K.wait("sp", "pe")
                K.dma("sp", qt, t["qdram"][head, slot, :, :])
                K.wait("pe", "sdma")
                for bk in (zbank, dbank):
                    for dep in K.bank_war[bk]:
                        K.wait("pe", *dep)
                    K.bank_war[bk] = []
                first = True
                nch = len(lst)
                for ch_i, ch in enumerate(lst):
                    _, _, kc, masked = ch
                    spar = ch_i % 2
                    sbank = 6 + spar
                    for dep in K.bank_war[sbank]:
                        K.wait("pe", *dep)
                    K.bank_war[sbank] = []
                    koff = kc * 1024 + (head - hg * 8) * 128
                    ktile = A8[:, koff:koff + 128]
                    vtile = A8[:, 16384 + koff:16384 + koff + 128]
                    if layer == 0:
                        K.pe(lambda e, b=sbank, l=ktile, r=qt:
                             e.matmul(K.psum[b][:, :], l, r,
                                      start=True, stop=False))
                        K.pe(lambda e, b=sbank,
                             l=sb["SLOPE"][0:1, head * 128:(head + 1) * 128]:
                             e.matmul(K.psum[b][:, :], l, sb["ALIROW"],
                                      start=False, stop=True), sem="pe")
                    else:
                        K.pe(lambda e, b=sbank, l=ktile, r=qt:
                             e.matmul(K.psum[b][:, :], l, r,
                                      start=True, stop=True), sem="pe")
                    nsc = K.sem["pe"].n
                    eb = sb["EB"][:, spar * 512:(spar + 1) * 512]
                    K.wait("act", "pe", nsc)
                    K.wait("act", "pe", ebuf_war[spar])
                    K.wait("act", "dve")
                    if layer == 0:
                        bias = sb["BIAS0"][:, bias_col[ch]:bias_col[ch] + 1]
                        K.act(lambda e, o=eb, p=K.psum[sbank], bb=bias:
                              e.activation(out=o, in_=p[:, :], func=AF.Exp,
                                           bias=bb, scale=CINV))
                    else:
                        K.act(lambda e, o=eb, p=K.psum[sbank]:
                              e.activation(out=o, in_=p[:, :], func=AF.Exp,
                                           bias=sb["CONSTS"][:, 1:2],
                                           scale=CINV))
                    K.bank_war[sbank].append(("act", K.sem["act"].n))
                    if masked:
                        em = sb["EM"][:, spar * 512:(spar + 1) * 512]
                        strip = sb["W01A"] if slot == 0 else sb["W01B"]
                        kcm = 7 if slot == 0 else 15
                        off = (kcm - kc) * 128
                        K.wait("dve", "act")
                        K.wait("dve", "pe", ebuf_war[spar])
                        if layer == 0:
                            K.dve(lambda e, o=em, m=strip[:, off:off + 512], a=eb:
                                  e.select(o, m, a, sb["ZERO"][:, :]))
                        else:
                            K.dve(lambda e, o=em, m=strip[:, off:off + 512], a=eb:
                                  e.tensor_mul(o, a, m))
                        etile = em
                        K.wait("pe", "dve")
                    else:
                        etile = eb
                        K.wait("pe", "act")
                    K.pe(lambda e, b=dbank, r=etile, s=first, tl=(ch_i == nch - 1):
                         e.matmul(K.psum[b][0:1, :], sb["ONEC"][:, 0:1], r,
                                  start=s, stop=tl),
                         sem="pe" if ch_i == nch - 1 else None)
                    K.pe(lambda e, b=zbank, l=vtile, r=etile, s=first,
                         tl=(ch_i == nch - 1):
                         e.matmul(K.psum[b][:, :], l, r, start=s, stop=tl),
                         sem="pe" if ch_i == nch - 1 else None)
                    ebuf_war[spar] = K.sem["pe"].n
                    first = False
                nz = K.sem["pe"].n
                den = sb["DEN"][0:1, hpar * 512:(hpar + 1) * 512]
                rec = sb["REC"][0:1, hpar * 512:(hpar + 1) * 512]
                rdb = sb["RD"][:, 0:512]
                K.wait("dve", "pe", nz)
                K.dve(lambda e, o=den, p=K.psum[dbank]:
                      e.tensor_copy(o, p[0:1, :]))
                K.bank_war[dbank].append(("dve", K.sem["dve"].n))
                K.wait("dve", "dve")
                K.dve(lambda e, o=rec, i=den: e.reciprocal(o, i))
                nrec = K.sem["dve"].n
                for dep in K.bank_war[rbank]:
                    K.wait("pe", *dep)
                K.bank_war[rbank] = []
                K.wait("pe", "dve", nrec)
                K.pe(lambda e, b=rbank, r=rec:
                     e.matmul(K.psum[b][:, :], sb["ONE_D"], r,
                              start=True, stop=True), sem="pe")
                nrb = K.sem["pe"].n
                K.wait("act", "pe", nrb)
                K.wait("act", "dve")
                K.act(lambda e, o=rdb, p=K.psum[rbank]:
                      e.activation(out=o, in_=p[:, :], func=AF.Copy))
                K.bank_war[rbank].append(("act", K.sem["act"].n))
                zdst = A4L[:, head * 1024 + slot * 512:
                           head * 1024 + (slot + 1) * 512]
                K.wait("dve", "act")
                K.dve(lambda e, o=zdst, p=K.psum[zbank], r=rdb:
                      e.tensor_mul(o, p[:, :], r))
                K.bank_war[zbank].append(("dve", K.sem["dve"].n))

    if DEV_STAGE == 2:
        K.wait("sp", "dve")
        K.dma("sp", t["dbg_z"][:, :], A4L[:, :])
        K.wait("sp", "dve")
        K.dma("sp", t["dbg_ln"][:, :], A8[:, 0:16384])
        return
    K.wait("pool", "pe")
    wosrc = t["l0wo_t"] if layer == 0 else t["l1wo_t"]
    K.dma("pool", A8[:, :], wosrc[:, :])
    K.wait("pe", "gdma")
    K.wait("pe", "dve")
    emit_proj_ln_residual(
        K, st, x_src=x_src, x_dst=x_dst, final=final,
        lhs_of=lambda k, tc: A4L[:, k * 1024 + (tc // 4) * 512
                                 + (tc % 4) * 128:k * 1024 + (tc // 4) * 512
                                 + (tc % 4) * 128 + 128],
        rhs_of=lambda k, dc: A8[:, k * 2048 + dc * 512:k * 2048 + (dc + 1) * 512],
        nk=16, bias_row=("BOROW", 0) if layer == 0 else None,
        scale_bcast=False, extra_add=None)


def emit_proj_ln_residual(K, st, x_src, x_dst, final, lhs_of, rhs_of,
                          nk, bias_row, scale_bcast, extra_add):
    sb, t = st["sb"], st["t"]
    prev_res = 0
    for tc in range(NTC):
        for dc in range(4):
            bank = (tc * 4 + dc) % 2
            mms = [(lhs_of(k, tc), rhs_of(k, dc)) for k in range(nk)]
            tail = None
            if bias_row is not None:
                name, off = bias_row
                tail = [(sb["ONE_B"],
                         sb[name][0:1, off + dc * 512:off + (dc + 1) * 512])]
            K.mm_group(bank, mms, tail=tail)
            dst = sb["FTC"][:, dc * 512:(dc + 1) * 512]
            if scale_bcast:
                K.wait("dve", "pe", K.bank_rdy[bank])
                K.wait("dve", "dve", prev_res)
                K.dve(lambda e, o=dst, p=K.psum[bank],
                      s=sb["DSB"][:, dc * 512:(dc + 1) * 512]:
                      e.tensor_mul(o, p[:, :], s))
                K.bank_war[bank].append(("dve", K.sem["dve"].n))
            else:
                K.wait("act", "pe", K.bank_rdy[bank])
                K.wait("act", "dve", prev_res)
                K.act(lambda e, o=dst, p=K.psum[bank]:
                      e.activation(out=o, in_=p[:, :], func=AF.Copy))
                K.bank_war[bank].append(("act", K.sem["act"].n))
        nfa = K.sem["act"].n
        if extra_add is not None:
            K.wait("dve", "act", nfa)
            K.wait("dve", "dve")
            K.wait("dve", "pool")
            K.dve(lambda e, a=extra_add(tc):
                  e.tensor_add(sb["FTC"][:, :], sb["FTC"][:, :], a))
        emit_ln_residual_store(K, st, tc, sb["FTC"][:, :], nfa,
                               x_src, x_dst, final)
        prev_res = K.sem["dve"].n



def emit_ln_residual_store(K, st, tc, F_ap, nfa, x_src, x_dst, final):
    sb, t = st["sb"], st["t"]
    par = tc % 2
    xc = sb["XCH"][:, par * 2048:(par + 1) * 2048]
    K.wait("sp", "dve")
    K.dma("sp", xc, x_src[tc * 128:(tc + 1) * 128, :])
    nx = K.sem["sdma"].n
    K.wait("dve", "act", nfa)
    K.wait("dve", "dve")
    K.wait("act", "act", nfa)
    K.wait("act", "dve", K.sem["dve"].n)
    emit_ln_chunk(K, st, tc, F_ap, F_ap)
    K.wait("dve", "sdma", nx)
    K.wait("dve", "dve")
    K.dve(lambda e, xc=xc, f=F_ap: e.tensor_add(xc, xc, f))
    nxn = K.sem["dve"].n
    K.wait("sp", "dve", nxn)
    dst = t["out"] if final else x_dst
    K.dma("sp", dst[tc * 128:(tc + 1) * 128, :], xc)


def emit_ffn_l0(K, st, x_src, x_dst, final):
    sb, t = st["sb"], st["t"]
    A4L, A8 = sb["A4L"], sb["A8"]
    a4f = A4L.ap().bitcast(F32)

    emit_ln_transpose(K, st, x_src)
    nln = K.sem["dve"].n
    K.wait("pe", "dve", nln)

    K.wait("pe", "sdma", st["n_pro_sdma"])
    for dc in range(4):
        bank = dc % 2
        K.mm_group(bank, [(sb["ONE_C"], sb["DSROW"][0:1, dc * 512:(dc + 1) * 512])])
        K.wait("act", "pe", K.bank_rdy[bank])
        K.act(lambda e, o=sb["DSB"][:, dc * 512:(dc + 1) * 512], p=K.psum[bank]:
              e.activation(out=o, in_=p[:, :], func=AF.Copy))
        K.bank_war[bank].append(("act", K.sem["act"].n))
    for u in range(2):
        bank = 4 + u
        mms = [(sb["ONEC"][:, 0:1],
                A4L[:, dc * 1024 + u * 512:dc * 1024 + (u + 1) * 512])
               for dc in range(NDC)]
        K.mm_group(bank, mms, out=K.psum[bank][0:1, :])
        K.wait("dve", "pe", K.bank_rdy[bank])
        K.dve(lambda e, o=sb["RSX"][0:1, u * 512:(u + 1) * 512], p=K.psum[bank]:
              e.tensor_scalar_mul(o, p[0:1, :], -128.0))
        K.bank_war[bank].append(("dve", K.sem["dve"].n))
    nrsx = K.sem["dve"].n
    K.wait("pe", "dve", nrsx)

    ga = sb["QT"]
    sgt = sb["QST"]
    ua = sb["EB"]
    hst = sb["EM"]
    hst_war = 0
    for u in range(2):
        hbank = 2 + u
        for dep in K.bank_war[hbank]:
            K.wait("pe", *dep)
        K.bank_war[hbank] = []
        for ic in range(64):
            for gi, (wsrc, scl) in enumerate(
                    ((t["l0g_t"], sb["GS"]), (t["l0u_t"], sb["US"]))):
                wtile = sb["WT2"][:, gi * 2048:(gi + 1) * 2048]
                K.wait("pool", "pe")
                K.dma("pool", wtile, wsrc[ic, :, :])
                K.wait("pe", "gdma")
                bank = gi
                mms = [(wtile[:, dc * 128:(dc + 1) * 128],
                        A4L[:, dc * 1024 + u * 512:dc * 1024 + (u + 1) * 512])
                       for dc in range(NDC)]
                tail = [(sb["ONE_E"][0:1, 0:128],
                         sb["RSX"][0:1, u * 512:(u + 1) * 512])]
                K.mm_group(bank, mms, tail=tail)
                K.wait("dve", "pe", K.bank_rdy[bank])
                dvedst = sgt if gi == 0 else ua
                K.wait("dve", "act")
                K.dve(lambda e, o=dvedst[:, 0:512], p=K.psum[bank],
                      s=scl[:, ic:ic + 1]:
                      e.tensor_scalar_mul(o, p[:, :], s))
                K.bank_war[bank].append(("dve", K.sem["dve"].n))
            K.wait("act", "dve")
            K.act(lambda e: e.activation(out=ga[:, 0:512], in_=sgt[:, 0:512],
                                         func=AF.Silu))
            K.wait("dve", "act")
            K.wait("dve", "pe", hst_war)
            K.wait("dve", "sdma")
            K.dve(lambda e: e.tensor_mul(hst[:, 0:512], ga[:, 0:512],
                                         ua[:, 0:512]))
            nh = K.sem["dve"].n
            K.wait("pe", "dve", nh)
            K.pe(lambda e, b=hbank, s=(ic == 0), tl=(ic == 63):
                 e.matmul(K.psum[b][0:1, :], sb["ONEC"][:, 0:1], hst[:, 0:512],
                          start=s, stop=tl), sem="pe")
            hst_war = K.sem["pe"].n
            K.wait("sp", "dve", nh)
            K.dma("sp", t["h_scr"][ic, :, u * 512:(u + 1) * 512], hst[:, 0:512])
        K.wait("dve", "pe", K.sem["pe"].n)
        K.dve(lambda e, o=sb["RSH"][0:1, u * 512:(u + 1) * 512], p=K.psum[hbank]:
              e.tensor_scalar_mul(o, p[0:1, :], -128.0))
        K.bank_war[hbank].append(("dve", K.sem["dve"].n))
    nrsh = K.sem["dve"].n
    K.wait("pe", "dve", nrsh)

    for u in range(2):
        for dh in range(2):
            for bk in range(8):
                for dep in K.bank_war[bk]:
                    K.wait("pe", *dep)
                K.bank_war[bk] = []
            for ic in range(64):
                wpar = ic % 2
                wtile = sb["WT"][:, wpar * 1024:(wpar + 1) * 1024]
                hpar = ic % 2
                htile = (sb["QST"] if hpar == 0 else sb["QT"])
                K.wait("pool", "pe")
                K.dma("pool", wtile,
                      t["l0d_t"][ic, :, dh * 1024:(dh + 1) * 1024])
                K.wait("sp", "pe")
                K.dma("sp", htile[:, 0:512],
                      t["h_scr"][ic, :, u * 512:(u + 1) * 512])
                K.wait("pe", "gdma")
                K.wait("pe", "sdma")
                for tc4 in range(4):
                    for dco in range(2):
                        bank = tc4 * 2 + dco
                        l = htile[:, tc4 * 128:(tc4 + 1) * 128]
                        r = wtile[:, dco * 512:(dco + 1) * 512]
                        K.pe(lambda e, b=bank, l=l, r=r, s=(ic == 0):
                             e.matmul(K.psum[b][:, :], l, r,
                                      start=s, stop=False),
                             sem="pe" if (tc4 == 3 and dco == 1) else None)
            for tc4 in range(4):
                for dco in range(2):
                    bank = tc4 * 2 + dco
                    l = sb["RSH"][0:1, u * 512 + tc4 * 128:
                                  u * 512 + (tc4 + 1) * 128]
                    K.pe(lambda e, b=bank, l=l:
                         e.matmul(K.psum[b][:, :], l, sb["ONE_E"],
                                  start=False, stop=True), sem="pe")
                    K.bank_rdy[bank] = K.sem["pe"].n
            npeall = K.sem["pe"].n
            K.wait("dve", "pe", npeall)
            K.wait("dve", "act")
            for tc4 in range(4):
                for dco in range(2):
                    bank = tc4 * 2 + dco
                    doff = dh * 1024 + dco * 512
                    dst = a4f[:, tc4 * 2048 + doff:tc4 * 2048 + doff + 512]
                    K.dve(lambda e, o=dst, p=K.psum[bank],
                          s=sb["DSB"][:, doff:doff + 512]:
                          e.tensor_mul(o, p[:, :], s))
                    K.bank_war[bank].append(("dve", K.sem["dve"].n))
        nfa = K.sem["act"].n
        for tc4 in range(4):
            tc = u * 4 + tc4
            emit_ln_residual_store(K, st, tc,
                                   a4f[:, tc4 * 2048:(tc4 + 1) * 2048],
                                   nfa, x_src, x_dst, final)


def emit_ffn_l1(K, st, x_src, x_dst, final):
    sb, t = st["sb"], st["t"]
    nc = K.nc
    A4L = sb["A4L"]
    a4f = A4L.ap().bitcast(F32)

    emit_ln_transpose(K, st, x_src, save_stats=("RS4", 0))
    nln = K.sem["dve"].n
    K.wait("pe", "dve", nln)
    if DEV_STAGE == 7:
        K.wait("sp", "dve")
        K.dma("sp", t["dbg_rt"][0:128, 0:16], sb["RS4"])
        K.dma("sp", t["dbg_e"][0:128, 0:512],
              sb["A4L"][:, 0:1024].bitcast(F32))
        return

    wq = sb["WT2"][:, 0:2048]
    K.wait("pool", "pe")
    K.dma("pool", wq, t["l1wq_t"][:, :])
    K.wait("pe", "gdma")
    for u in range(2):
        for half in range(2):
            bank = u
            mms = [(wq[:, dc * 128 + half * 64:dc * 128 + half * 64 + 64],
                    A4L[:, dc * 1024 + u * 512:dc * 1024 + (u + 1) * 512])
                   for dc in range(NDC)]
            K.mm_group(bank, mms, out=K.psum[bank][0:64, :])
            K.wait("dve", "pe", K.bank_rdy[bank])
            K.dve(lambda e, o=sb["QR"][0:64, half * 1024 + u * 512:
                                       half * 1024 + (u + 1) * 512],
                  p=K.psum[bank]:
                  e.tensor_copy(o, p[0:64, :]))
            K.bank_war[bank].append(("dve", K.sem["dve"].n))
    nqr = K.sem["dve"].n
    if DEV_STAGE == 8:
        K.wait("sp", "dve", nqr)
        K.dma("sp", t["dbg_e"][0:128, 0:512], sb["QR"][:, :].bitcast(F32))
        return
    K.wait("pe", "dve", nqr)
    K.wait("pe", "gdma")
    for tc in range(NTC):
        bank = 2 + (tc % 2)
        for dep in K.bank_war[bank]:
            K.wait("pe", *dep)
        K.bank_war[bank] = []
        u, w = tc // 4, tc % 4
        for half in range(2):
            l = sb["QR"][0:64, half * 1024 + tc * 128:
                         half * 1024 + (tc + 1) * 128]
            r = sb["KEYS"][0:64, 0:64]
            K.pe(lambda e, b=bank, l=l, r=r, h=half:
                 e.matmul(K.psum[b][:, h * 64:(h + 1) * 64], l, r,
                          start=True, stop=True), sem="pe")
        nsc = K.sem["pe"].n
        if DEV_STAGE == 10:
            K.bank_war[bank].append(("pe", nsc))
            continue
        K.wait("dve", "pe", nsc)
        K.wait("dve", "sdma")
        K.dve(lambda e, o=sb["RD"][:, 0:128], p=K.psum[bank]:
              e.tensor_copy(o, p[:, 0:128]))
        K.bank_war[bank].append(("dve", K.sem["dve"].n))
        if DEV_STAGE == 9:
            continue
        K.wait("sp", "dve")
        K.dma("sp", bass.AP(t["cc_rt"], tc * 16384, [[128, 128], [1, 128]]),
              sb["RD"][:, 0:128])
    nrt = K.sem["sdma"].n
    if DEV_STAGE == 10:
        K.wait("sp", "pe")
        K.dma("sp", t["dbg_e"][0:128, 0:512], sb["QR"][:, :].bitcast(F32))
        return
    if DEV_STAGE == 9:
        K.wait("sp", "dve")
        K.dma("sp", t["dbg_e"][0:128, 0:128], sb["RD"][:, 0:128])
        return
    if DEV_STAGE == 5:
        K.wait("sp", "sdma", nrt)
        K.dma("sp", t["dbg_rt"][0:2048, :], t["cc_rt"][:, :])
        return
    K.wait("pool", "sdma", nrt)
    K.pool(lambda e: e.collective_compute(
        "AllGather", ALU.bypass,
        replica_groups=[[0, 1, 2, 3, 4, 5, 6, 7]],
        ins=[t["cc_rt"].ap()],
        outs=[t["rt_all"].ap()]), sem="cc")
    ncc = K.sem["cc"].n
    if DEV_STAGE == 6:
        K.wait("sp", "cc", ncc)
        K.dma("sp", t["dbg_rt"][:, :], t["rt_all"][:, :])
        return

    regs = [nc.gpsimd.alloc_register(f"rtoff{i}") for i in range(4)]
    K.wait("pool", "cc", ncc)
    for side in range(2):
        for blk in range(2):
            i = side * 2 + blk
            K.pool(lambda e, r=regs[i], c=sb["CIDX"][0:1, i:i + 1]:
                   e.reg_load(r, c), sem=None)
            src = bass.AP(t["rt_all"], regs[i], [[64, 128], [8192, 4], [1, 64]])
            K.dma("pool", sb["RT"][:, i * 256:(i + 1) * 256], src)
    nrta = K.sem["gdma"].n

    if DEV_STAGE == 3:
        K.wait("sp", "sdma", nrta)
        K.wait("sp", "gdma")
        K.dma("sp", t["dbg_rt"][:, :], t["rt_all"][:, :])
        K.dma("sp", t["dbg_e"][0:128, 0:1024], sb["RT"][:, :])
        return
    TK, TKU = sb["TK"], sb["TKU"]
    IO8 = sb["IOTA8"][:, :]
    deb_war = [0, 0]
    for tc in range(NTC):
        blk, w = tc // 4, tc % 4
        t0 = sb["RT"][:, blk * 256 + w * 64:blk * 256 + (w + 1) * 64]
        t1 = sb["RT"][:, 512 + blk * 256 + w * 64:512 + blk * 256 + (w + 1) * 64]
        K.wait("dve", "gdma", nrta)
        sx, sy = TK[:, 0:8], TK[:, 8:16]
        ixf, iyf = TK[:, 16:24], TK[:, 24:32]
        ix64 = TK[:, 32:40]
        alls, alli = TK[:, 64:128], TK[:, 128:192]
        s8 = TK[:, 192:200]
        i_f, j_f = TK[:, 200:208], TK[:, 208:216]
        seli, selj = TK[:, 216:224], TK[:, 224:232]
        neg0, ssum, srec = TK[:, 232:233], TK[:, 233:234], TK[:, 234:235]
        ex8, wts, ew8 = TK[:, 240:248], TK[:, 248:256], TK[:, 256:264]
        ixu, iyu, pku = TKU[:, 0:8], TKU[:, 8:16], TKU[:, 16:24]
        iu, ju = TKU[:, 24:32], TKU[:, 32:40]
        K.dve(lambda e, o=sx, i=t0: e.max(out=o, in_=i))
        K.wait("dve", "dve")
        K.dve(lambda e, o=ixu, m=sx, i=t0: e.max_index(out=o, in_max=m, in_values=i))
        K.dve(lambda e, o=sy, i=t1: e.max(out=o, in_=i))
        K.wait("dve", "dve")
        K.dve(lambda e, o=iyu, m=sy, i=t1: e.max_index(out=o, in_max=m, in_values=i))
        K.wait("dve", "dve")
        K.dve(lambda e, o=ixf, i=ixu: e.tensor_copy(o, i))
        K.dve(lambda e, o=iyf, i=iyu: e.tensor_copy(o, i))
        K.wait("dve", "dve")
        K.dve(lambda e, o=ix64, i=ixf: e.tensor_scalar_mul(o, i, 64.0))
        K.wait("dve", "dve")
        for i8 in range(8):
            K.dve(lambda e, o=alls[:, i8 * 8:(i8 + 1) * 8], i=sy,
                  sc=sx[:, i8:i8 + 1]:
                  e.tensor_scalar_add(o, i, sc))
            K.dve(lambda e, o=alli[:, i8 * 8:(i8 + 1) * 8], i=iyf,
                  sc=ix64[:, i8:i8 + 1]:
                  e.tensor_scalar_add(o, i, sc))
        K.wait("dve", "dve")
        K.dve(lambda e, o=s8, i=alls: e.max(out=o, in_=i))
        K.wait("dve", "dve")
        K.dve(lambda e, o=pku, m=s8, i=alls: e.max_index(out=o, in_max=m, in_values=i))
        K.wait("dve", "dve")
        K.dve(lambda e, o=iu, i=pku: e.tensor_single_scalar(o, i, 3, ALU.logical_shift_right))
        K.dve(lambda e, o=ju, i=pku: e.tensor_single_scalar(o, i, 7, ALU.bitwise_and))
        K.wait("dve", "dve")
        K.dve(lambda e, o=i_f, i=iu: e.tensor_copy(o, i))
        K.dve(lambda e, o=j_f, i=ju: e.tensor_copy(o, i))
        K.wait("dve", "dve")
        for kk in range(8):
            K.dve(lambda e, o=TK[:, 448:456], i=IO8, sc=i_f[:, kk:kk + 1]:
                  e.tensor_single_scalar(o, i, sc, ALU.is_equal))
            K.wait("dve", "dve")
            K.dve(lambda e, m=TK[:, 448:456], v=ix64, o=TK[:, 456:464],
                  a=seli[:, kk:kk + 1]:
                  e.scalar_tensor_tensor(o, m, 1.0, v, ALU.bypass, ALU.mult,
                                         accum_out=a))
            K.dve(lambda e, o=TK[:, 448:456], i=IO8, sc=j_f[:, kk:kk + 1]:
                  e.tensor_single_scalar(o, i, sc, ALU.is_equal))
            K.wait("dve", "dve")
            K.dve(lambda e, m=TK[:, 448:456], v=iyf, o=TK[:, 456:464],
                  a=selj[:, kk:kk + 1]:
                  e.scalar_tensor_tensor(o, m, 1.0, v, ALU.bypass, ALU.mult,
                                         accum_out=a))
            K.wait("dve", "dve")
        K.dve(lambda e, o=TK[:, 264:272], a=seli, b=selj: e.tensor_add(o, a, b))
        K.wait("dve", "dve")
        K.dve(lambda e, o=sb["EWID"], i=TK[:, 264:272]: e.tensor_copy(o, i))
        nid = K.sem["dve"].n
        K.dve(lambda e, o=neg0, i=s8[:, 0:1]: e.tensor_scalar_mul(o, i, -1.0))
        K.wait("act", "dve")
        K.act(lambda e, o=ex8, i=s8, b=neg0: e.activation(
            out=o, in_=i, func=AF.Exp, bias=b))
        K.wait("dve", "act")
        K.dve(lambda e, o=ssum, i=ex8: e.reduce_sum(o, i, axis=AXX))
        K.wait("dve", "dve")
        K.dve(lambda e, o=srec, i=ssum: e.reciprocal(o, i))
        K.wait("dve", "dve")
        K.dve(lambda e, o=wts, i=ex8, sc=srec: e.tensor_scalar_mul(o, i, sc))
        par = tc % 2
        xc = sb["XCH"][:, par * 2048:(par + 1) * 2048]
        K.wait("sp", "dve")
        K.dma("sp", xc, x_src[tc * 128:(tc + 1) * 128, :])
        nx = K.sem["sdma"].n
        K.wait("dve", "sdma", nx)
        K.dve(lambda e, xc=xc, r=sb["RS4"][:, tc:tc + 1],
              m=sb["RS4"][:, 8 + tc:9 + tc]:
              e.tensor_scalar(sb["XBF"], xc, r, m, ALU.mult, ALU.add))
        nxbf = K.sem["dve"].n
        K.wait("pool", "dve", nid)
        for k8 in range(8):
            gpar = k8 % 2
            de = sb["DEB"][:, gpar * 2048:(gpar + 1) * 2048]
            ue = sb["UEB"][:, gpar * 2048:(gpar + 1) * 2048]
            K.wait("pool", "dve", deb_war[gpar])
            K.pool(lambda e, o=de, ix=sb["EWID"][:, k8:k8 + 1]:
                   e.indirect_dma_start(o, None, t["l1de"][:, :],
                                        IndirectOffsetOnAxis(ap=ix, axis=0)),
                   sem="gdma", v=16)
            K.pool(lambda e, o=ue, ix=sb["EWID"][:, k8:k8 + 1]:
                   e.indirect_dma_start(o, None, t["l1ue"][:, :],
                                        IndirectOffsetOnAxis(ap=ix, axis=0)),
                   sem="gdma", v=16)
            ng = K.sem["gdma"].n
            K.wait("dve", "gdma", ng)
            K.wait("dve", "dve", nxbf)
            K.dve(lambda e, de=de, a=ew8[:, k8:k8 + 1]:
                  e.scalar_tensor_tensor(sb["LNTM"][:, 0:2048], de, 1.0,
                                         sb["XBF"], ALU.bypass, ALU.mult,
                                         accum_out=a))
            K.wait("act", "dve")
            K.act(lambda e, o=TK[:, 272 + k8:273 + k8], i=ew8[:, k8:k8 + 1]:
                  e.activation(out=o, in_=i, func=AF.Silu))
            K.wait("dve", "act")
            K.dve(lambda e, o=TK[:, 280 + k8:281 + k8],
                  a=TK[:, 272 + k8:273 + k8], b=wts[:, k8:k8 + 1]:
                  e.tensor_mul(o, a, b))
            K.wait("dve", "dve")
            if k8 == 0:
                K.dve(lambda e, ue=ue, sc=TK[:, 280:281]:
                      e.tensor_scalar_mul(sb["EACC"][:, 0:2048], ue, sc))
            else:
                K.dve(lambda e, ue=ue, sc=TK[:, 280 + k8:281 + k8]:
                      e.scalar_tensor_tensor(sb["EACC"][:, 0:2048], ue, sc,
                                             sb["EACC"][:, 0:2048], ALU.mult,
                                             ALU.add))
            deb_war[gpar] = K.sem["dve"].n
        nea = K.sem["dve"].n
        K.wait("sp", "dve", nea)
        K.dma("sp", t["e_scr"][tc * 128:(tc + 1) * 128, :], sb["EACC"][:, 0:2048])

    if DEV_STAGE == 4:
        K.wait("sp", "sdma")
        K.dma("sp", t["dbg_e"][:, :], t["e_scr"][:, :])
        K.dma("sp", t["dbg_rt"][:, :], t["rt_all"][:, :])
        return
    hst_war = 0
    for u in range(2):
        for ic in range(64):
            for gi, wsrc in enumerate((t["l1g_t"], t["l1u_t"])):
                wtile = sb["WT2"][:, gi * 2048:(gi + 1) * 2048]
                K.wait("pool", "pe")
                K.dma("pool", wtile, wsrc[ic, :, :])
                K.wait("pe", "gdma")
                bank = gi
                mms = [(wtile[:, dc * 128:(dc + 1) * 128],
                        A4L[:, dc * 1024 + u * 512:dc * 1024 + (u + 1) * 512])
                       for dc in range(NDC)]
                K.mm_group(bank, mms)
                K.wait("dve", "pe", K.bank_rdy[bank])
                dvedst = sb["QST"] if gi == 0 else sb["EB"]
                K.wait("dve", "act")
                K.dve(lambda e, o=dvedst[:, 0:512], p=K.psum[bank]:
                      e.tensor_copy(o, p[:, :]))
                K.bank_war[bank].append(("dve", K.sem["dve"].n))
            K.wait("act", "dve")
            K.act(lambda e: e.activation(out=sb["QT"][:, 0:512],
                                         in_=sb["QST"][:, 0:512], func=AF.Silu))
            K.wait("dve", "act")
            K.wait("dve", "sdma")
            K.dve(lambda e: e.tensor_mul(sb["EM"][:, 0:512], sb["QT"][:, 0:512],
                                         sb["EB"][:, 0:512]))
            nh = K.sem["dve"].n
            K.wait("sp", "dve", nh)
            K.dma("sp", t["h_scr"][ic, :, u * 512:(u + 1) * 512],
                  sb["EM"][:, 0:512])

    if DEV_STAGE == 12:
        K.wait("sp", "sdma")
        K.wait("sp", "gdma")
        K.dma("sp", t["dbg_e"].ap().bitcast(BF16)[0:1024, 0:4096],
              t["h_scr"][0:32, :, :])
        return
    for u in range(2):
        for dh in range(2):
            for bk in range(8):
                for dep in K.bank_war[bk]:
                    K.wait("pe", *dep)
                K.bank_war[bk] = []
            for ic in range(64):
                wpar = ic % 2
                wtile = sb["WT"][:, wpar * 1024:(wpar + 1) * 1024]
                htile = (sb["QST"] if ic % 2 == 0 else sb["QT"])
                K.wait("pool", "pe")
                K.dma("pool", wtile,
                      t["l1d_t"][ic, :, dh * 1024:(dh + 1) * 1024])
                K.wait("sp", "pe")
                K.dma("sp", htile[:, 0:512],
                      t["h_scr"][ic, :, u * 512:(u + 1) * 512])
                K.wait("pe", "gdma")
                K.wait("pe", "sdma")
                for tc4 in range(4):
                    for dco in range(2):
                        bank = tc4 * 2 + dco
                        l = htile[:, tc4 * 128:(tc4 + 1) * 128]
                        r = wtile[:, dco * 512:(dco + 1) * 512]
                        K.pe(lambda e, b=bank, l=l, r=r, s=(ic == 0),
                             tl=(ic == 63):
                             e.matmul(K.psum[b][:, :], l, r,
                                      start=s, stop=tl),
                             sem="pe" if (tc4 == 3 and dco == 1) else None)
            for tc4 in range(4):
                for dco in range(2):
                    K.bank_rdy[tc4 * 2 + dco] = K.sem["pe"].n
            npeall = K.sem["pe"].n
            K.wait("dve", "pe", npeall)
            K.wait("dve", "act")
            for tc4 in range(4):
                for dco in range(2):
                    bank = tc4 * 2 + dco
                    doff = dh * 1024 + dco * 512
                    dst = a4f[:, tc4 * 2048 + doff:tc4 * 2048 + doff + 512]
                    K.dve(lambda e, o=dst, p=K.psum[bank]:
                          e.tensor_copy(o, p[:, :]))
                    K.bank_war[bank].append(("dve", K.sem["dve"].n))
        nfa = K.sem["act"].n
        for tc4 in range(4):
            tc = u * 4 + tc4
            K.wait("sp", "dve")
            K.dma("sp", sb["EACC"][:, 0:2048],
                  t["e_scr"][tc * 128:(tc + 1) * 128, :])
            nel = K.sem["sdma"].n
            K.wait("dve", "sdma", nel)
            K.wait("dve", "dve")
            fap = a4f[:, tc4 * 2048:(tc4 + 1) * 2048]
            K.dve(lambda e, f=fap: e.tensor_add(sb["FTC"][:, :], f,
                                                sb["EACC"][:, 0:2048]))
            fap = sb["FTC"][:, :]
            if DEV_STAGE == 13:
                nf = K.sem["dve"].n
                K.wait("sp", "dve", nf)
                K.dma("sp", t["dbg_e"][tc * 128:(tc + 1) * 128, :], fap)
                continue
            emit_ln_residual_store(K, st, tc, fap, nfa, x_src, x_dst, final)
        if DEV_STAGE == 13 and u == 1:
            return



def kernel(**inputs):
    in_maps = stage_inputs(inputs)
    nc = build_program()
    res = run_bass_kernel_spmd(nc, in_maps, core_ids=list(range(8)))
    return unstage_output(res.results)


# version 03
# speedup vs baseline: 1.2949x; 1.2949x over previous
import numpy as np
import ml_dtypes
import concourse.bass as bass
import concourse.mybir as mybir
from concourse.bass_core_rust import IndirectOffsetOnAxis
from concourse.bass_utils import run_bass_kernel_spmd
from contextlib import ExitStack

F32 = mybir.dt.float32
BF16 = mybir.dt.bfloat16
U8 = mybir.dt.uint8
U32 = mybir.dt.uint32
I32 = mybir.dt.int32
AF = mybir.ActivationFunctionType
ALU = mybir.AluOpType
AXX = mybir.AxisListType.X

NH = 16
NTC = 8
NDC = 16
CEXP = 20.0
CINV = 1.0 / float(np.sqrt(128.0))
SLOPES = [1.0 / 2 ** (k / 2.0) for k in range(16)]
EPS = 1e-5
N_SUB = 4
DEV_STAGE = 0
DEV_KEYS = 0


def qbase(parity, slot):
    return 512 * parity if slot == 0 else 512 * (3 - parity)


def global_blocks(parity):
    return (0, 3) if parity == 0 else (1, 2)


def kc_owner(kc):
    g = kc // 4
    if g in (0, 3):
        return 0, (kc if g == 0 else kc - 8)
    return 1, kc - 4


def attn_chunks(layer):
    out = []
    for head in range(NH):
        slope = SLOPES[head]
        for slot in range(2):
            kcmax = 7 if slot == 0 else 15
            for kc in range(kcmax + 1):
                contrib = False
                full_both = True
                for p in (0, 1):
                    dq = qbase(p, slot) - 128 * kc
                    valid = dq >= -384
                    far = (layer == 0) and slope * max(0, dq - 127) > 120.0
                    if valid and not far:
                        contrib = True
                    if dq < 127:
                        full_both = False
                if contrib:
                    out.append((head, slot, kc, not full_both))
    return out



class Sem:
    def __init__(self, nc, name):
        self.h = nc.alloc_semaphore(name=name)
        self.n = 0


class KB:
    ENGS = ("pe", "act", "dve", "pool", "sp")

    def __init__(self, nc):
        self.nc = nc
        self.ops = {e: [] for e in self.ENGS}
        self.sem = {s: Sem(nc, "s_" + s)
                    for s in ("sdma", "gdma", "pe", "act", "dve", "pool", "cc")}
        self.ctx = ExitStack()
        self._sbuf_bytes = 0
        self.psum = [self.ctx.enter_context(
            nc.psum_tensor(f"pb{i}", [128, 512], F32)) for i in range(8)]
        self.bank_war = [[] for _ in range(8)]
        self.bank_rdy = [0] * 8

    def sb(self, name, shape, dtype):
        self._sbuf_bytes += int(np.prod(shape[1:])) * mybir.dt.size(dtype)
        return self.ctx.enter_context(self.nc.sbuf_tensor(name, shape, dtype))

    def _rec(self, eng, fn, sem=None, v=1):
        s = self.sem[sem] if sem else None
        if s is not None:
            s.n += v

            def f(e, fn=fn, s=s, v=v):
                fn(e).then_inc(s.h, v)
        else:
            def f(e, fn=fn):
                fn(e)
        self.ops[eng].append(f)
        return s.n if s is not None else None

    def pe(self, fn, sem=None, v=1):
        return self._rec("pe", fn, sem, v)

    def act(self, fn, sem="act", v=1):
        return self._rec("act", fn, sem, v)

    def dve(self, fn, sem="dve", v=1):
        return self._rec("dve", fn, sem, v)

    def pool(self, fn, sem="pool", v=1):
        return self._rec("pool", fn, sem, v)

    def sp(self, fn, sem=None, v=1):
        return self._rec("sp", fn, sem, v)

    def wait(self, eng, sem, n=None):
        s = self.sem[sem]
        n = s.n if n is None else n
        assert n <= s.n, f"wait on future count {sem} {n} > {s.n}"
        if n <= 0:
            return
        self.ops[eng].append(lambda e, s=s, n=n: e.wait_ge(s.h, n))

    def dma(self, q, out, in_, **kw):
        sem = "sdma" if q == "sp" else "gdma"
        return self._rec(q, lambda e: e.dma_start(out=out, in_=in_, **kw), sem, 16)

    def mm_group(self, bank, mms, tail=None, out=None):
        for dep in self.bank_war[bank]:
            self.wait("pe", *dep)
        self.bank_war[bank] = []
        seq = list(mms) + list(tail or [])
        n = len(seq)
        pt = out if out is not None else self.psum[bank][:, :]
        for i, (l, r) in enumerate(seq):
            last = i == n - 1
            self.pe(lambda e, l=l, r=r, s=(i == 0), t=last, pt=pt:
                    e.matmul(pt, l, r, start=s, stop=t),
                    sem="pe" if last else None)
        self.bank_rdy[bank] = self.sem["pe"].n
        return self.bank_rdy[bank]

    def finalize(self, out_waits):
        nc, ops, sems = self.nc, self.ops, self.sem
        with nc.Block() as block:

            @block.tensor
            def _(e):
                for f in ops["pe"]:
                    f(e)

            @block.scalar
            def _(e):
                for f in ops["act"]:
                    f(e)

            @block.vector
            def _(e):
                for f in ops["dve"]:
                    f(e)

            @block.gpsimd
            def _(e):
                for f in ops["pool"]:
                    f(e)

            @block.sync
            def _(e):
                for f in ops["sp"]:
                    f(e)
                for s, n in out_waits:
                    e.wait_ge(sems[s].h, n)



def _tile_qk(wqkv):
    out = np.empty((32, 128, 2048), dtype=np.float32)
    w = wqkv.reshape(16, 128, 48, 128)
    for j in range(32):
        out[j] = w[:, :, j, :].transpose(1, 0, 2).reshape(128, 2048)
    return out


def _tile_v(wv):
    w = wv.reshape(16, 128, 4, 512)
    return np.ascontiguousarray(w.transpose(2, 1, 0, 3).reshape(4, 128, 8192))


def _tile_wo(wo):
    w = wo.reshape(16, 128, 4, 512)
    return np.ascontiguousarray(w.transpose(1, 0, 2, 3).reshape(128, 32768))


def _tile_lhs64(w, transpose):
    if transpose:
        v = w.reshape(64, 128, 16, 128)
        return np.ascontiguousarray(v.transpose(0, 3, 2, 1).reshape(64, 128, 2048))
    v = w.reshape(16, 128, 64, 128)
    return np.ascontiguousarray(v.transpose(2, 1, 0, 3).reshape(64, 128, 2048))


def _tile_down(w, transpose):
    if transpose:
        v = w.reshape(2048, 64, 128)
        return np.ascontiguousarray(v.transpose(1, 2, 0))
    return np.ascontiguousarray(w.reshape(64, 128, 2048))


def _int32_to_u8(w):
    return np.ascontiguousarray(w.view(np.uint8).reshape(*w.shape, 4)[..., 0])


def _w01_strip(parity, slot):
    kcmax = 7 if slot == 0 else 15
    k0 = qbase(parity, slot) - 128 * kcmax
    i = np.arange(128)[:, None]
    x = np.arange(1408)[None, :]
    return ((i - x) <= k0).astype(np.uint8)


def _l0_bias_table(parity, chunks):
    cols = []
    i = np.arange(128)
    for (head, slot, kc, _m) in chunks:
        s = SLOPES[head]
        cols.append(s * (128 * kc + i - qbase(parity, slot)) - CEXP)
    return np.stack(cols, axis=1).astype(np.float32)


def _routing_offsets(core):
    b, h = core // 2, core % 2
    offs = []
    for side in range(2):
        for blk in range(2):
            s0 = global_blocks(h)[blk] * 512
            n0 = b * 2048 + s0
            m0 = n0
            prod_batch = (m0 % 4096 * 0) + (m0 // 4096) + (2 if side == 1 else 0)
            sp = (m0 % 4096) // 2
            blk_p = sp // 512
            prod_parity = 1 if blk_p in (1, 2) else 0
            prod_core = 2 * (prod_batch % 2) + prod_parity if False else None
            pb_in_t = m0 // 4096
            prod_core = 2 * (pb_in_t + (2 if side == 1 else 0) * 0) + prod_parity
            prod_core = 2 * (pb_in_t + (2 if side == 1 else 0)) + prod_parity
            if prod_parity == 0:
                tloc = sp if blk_p == 0 else sp - 1024
            else:
                tloc = sp - 512
            offs.append((prod_core * 2048 + 2 * tloc) * 64)
    return np.array(offs + [0] * 4, dtype=np.uint32).reshape(1, 8)


def stage_inputs(inputs):
    f32 = np.float32
    g = {}
    wqkv = np.asarray(inputs["L0_attn_Wqkv"], f32)
    g["l0qk_t"] = _tile_qk(wqkv)
    g["l0v_t"] = _tile_v(np.ascontiguousarray(wqkv[:, 4096:6144]))
    g["l0wo_t"] = _tile_wo(np.asarray(inputs["L0_attn_Wo"], f32))
    bqkv = np.asarray(inputs["L0_attn_bqkv"], f32)
    g["l0bqk"] = np.ascontiguousarray(bqkv[:4096].reshape(32, 128).T)
    g["l0bv"] = np.ascontiguousarray(bqkv[4096:].reshape(1, 2048))
    g["l0bo"] = np.ascontiguousarray(
        np.asarray(inputs["L0_attn_bo"], f32).reshape(1, 2048))
    g["l0g_t"] = _tile_lhs64(_int32_to_u8(np.asarray(inputs["L0_ffn_gate_w"])), True)
    g["l0u_t"] = _tile_lhs64(_int32_to_u8(np.asarray(inputs["L0_ffn_up_w"])), True)
    g["l0d_t"] = _tile_down(_int32_to_u8(np.asarray(inputs["L0_ffn_down_w"])), True)
    g["l0gs"] = np.ascontiguousarray(
        np.asarray(inputs["L0_ffn_gate_scale"], f32).reshape(64, 128).T)
    g["l0us"] = np.ascontiguousarray(
        np.asarray(inputs["L0_ffn_up_scale"], f32).reshape(64, 128).T)
    g["l0ds"] = np.ascontiguousarray(
        np.asarray(inputs["L0_ffn_down_scale"], f32).reshape(1, 2048))

    wq = np.asarray(inputs["L1_attn_W_Q"], f32)
    wk = np.asarray(inputs["L1_attn_W_K"], f32)
    wv = np.asarray(inputs["L1_attn_W_V"], f32)
    g["l1q_t"] = np.ascontiguousarray(
        wq.transpose(0, 2, 1).reshape(16, 16, 128, 128)
        .transpose(0, 2, 1, 3).reshape(16, 128, 2048))
    g["l1k_t"] = np.ascontiguousarray(
        wk.transpose(0, 2, 1).reshape(16, 16, 128, 128)
        .transpose(0, 2, 1, 3).reshape(16, 128, 2048))
    wv2 = np.ascontiguousarray(wv.transpose(2, 0, 1).reshape(2048, 2048))
    g["l1v_t"] = _tile_v(wv2)
    g["l1wo_t"] = _tile_wo(
        np.ascontiguousarray(np.asarray(inputs["L1_attn_W_O"], f32).T))
    g["l1g_t"] = _tile_lhs64(np.asarray(inputs["L1_ffn_gate_w"], f32), False)
    g["l1u_t"] = _tile_lhs64(np.asarray(inputs["L1_ffn_up_w"], f32), False)
    g["l1d_t"] = _tile_down(np.asarray(inputs["L1_ffn_down_w"], f32), False)
    wqr = np.asarray(inputs["L1_ffn_W_q"], f32)
    g["l1wq_t"] = np.ascontiguousarray(
        wqr.reshape(16, 128, 128).transpose(1, 0, 2).reshape(128, 2048))

    g["l1de"] = np.ascontiguousarray(np.asarray(inputs["L1_ffn_down_embed"], f32))
    g["l1ue"] = np.ascontiguousarray(np.asarray(inputs["L1_ffn_up_embed"], f32))

    x = np.asarray(inputs["L0_attn_x"], f32)
    chunks0 = attn_chunks(0)
    in_maps = []
    for c in range(8):
        b, h = c // 2, c % 2
        m = dict(g)
        m["x0"] = np.ascontiguousarray(np.concatenate(
            [x[b, blk * 512:(blk + 1) * 512] for blk in global_blocks(h)], 0))
        m["w01a"] = np.ascontiguousarray(_w01_strip(h, 0))
        m["w01b"] = np.ascontiguousarray(_w01_strip(h, 1))
        m["l0bias"] = _l0_bias_table(h, chunks0)
        m["cidx"] = _routing_offsets(c)
        m["l1keys"] = np.ascontiguousarray(
            np.asarray(inputs["L1_ffn_keys"], np.float32)[c // 4])
        in_maps.append(m)
    return in_maps


def unstage_output(res_list):
    out = np.empty((4, 2048, 2048), dtype=np.float32)
    for c in range(8):
        b, h = c // 2, c % 2
        y = res_list[c]["out"]
        for i, blk in enumerate(global_blocks(h)):
            out[b, blk * 512:(blk + 1) * 512] = y[i * 512:(i + 1) * 512]
    return out



def build_program(n_sub=None):
    if n_sub is None:
        n_sub = N_SUB
    nc = bass.Bass("TRN2", target_bir_lowering=False)
    K = KB(nc)
    t = {}

    def din(name, shape, dtype):
        t[name] = nc.dram_tensor(name, list(shape), dtype, kind="ExternalInput")

    din("x0", (1024, 2048), F32)
    din("l0qk_t", (32, 128, 2048), F32)
    din("l0v_t", (4, 128, 8192), F32)
    din("l0wo_t", (128, 32768), F32)
    din("l0bqk", (128, 32), F32)
    din("l0bv", (1, 2048), F32)
    din("l0bo", (1, 2048), F32)
    din("l0g_t", (64, 128, 2048), U8)
    din("l0u_t", (64, 128, 2048), U8)
    din("l0d_t", (64, 128, 2048), U8)
    din("l0gs", (128, 64), F32)
    din("l0us", (128, 64), F32)
    din("l0ds", (1, 2048), F32)
    din("l1q_t", (16, 128, 2048), F32)
    din("l1k_t", (16, 128, 2048), F32)
    din("l1v_t", (4, 128, 8192), F32)
    din("l1wo_t", (128, 32768), F32)
    din("l1g_t", (64, 128, 2048), F32)
    din("l1u_t", (64, 128, 2048), F32)
    din("l1d_t", (64, 128, 2048), F32)
    din("l1wq_t", (128, 2048), F32)
    din("l1keys", (64, 64), F32)
    din("l1de", (4096, 2048), F32)
    din("l1ue", (4096, 2048), F32)
    din("w01a", (128, 1408), U8)
    din("w01b", (128, 1408), U8)
    din("l0bias", (128, len(attn_chunks(0))), F32)
    din("cidx", (1, 8), U32)
    t["out"] = nc.dram_tensor("out", [1024, 2048], F32, kind="ExternalOutput")

    dbg = "ExternalOutput" if DEV_STAGE == 1 else "Internal"
    if DEV_STAGE in (3, 4, 5, 6, 7, 8, 9, 10, 12, 13):
        t["dbg_rt"] = nc.dram_tensor("dbg_rt", [16384, 64], F32,
                                     kind="ExternalOutput")
        t["dbg_e"] = nc.dram_tensor("dbg_e", [1024, 2048], F32,
                                    kind="ExternalOutput")
    if DEV_STAGE > 0:
        t["dbg_z"] = nc.dram_tensor("dbg_z", [128, 16384], BF16,
                                    kind="ExternalOutput")
        t["dbg_ln"] = nc.dram_tensor("dbg_ln", [128, 16384], BF16,
                                     kind="ExternalOutput")
    t["x_scr"] = nc.dram_tensor("x_scr", [1024, 2048], F32)
    t["h_scr"] = nc.dram_tensor("h_scr", [64, 128, 1024], BF16)
    t["e_scr"] = nc.dram_tensor("e_scr", [1024, 2048], F32)
    t["qdram"] = nc.dram_tensor("qdram", [16, 2, 128, 512], BF16, kind=dbg)
    t["cc_kv"] = nc.dram_tensor("cc_kv", [256, 16384], BF16, kind=dbg)
    t["kv_all"] = nc.dram_tensor("kv_all", [512, 16384], BF16)
    t["cc_rt"] = nc.dram_tensor("cc_rt", [2048, 64], F32)
    t["rt_all"] = nc.dram_tensor("rt_all", [16384, 64], F32)

    sb = {}
    sb["A4L"] = K.sb("A4L", [128, 16384], BF16)
    sb["A8"] = K.sb("A8", [128, 32768], BF16)
    sb["XCH"] = K.sb("XCH", [128, 2 * 2048], F32)
    sb["LNTM"] = K.sb("LNTM", [128, 2048], BF16)
    sb["FTC"] = K.sb("FTC", [128, 2048], F32)
    sb["STAT"] = K.sb("STAT", [128, 64], F32)
    sb["CONSTS"] = K.sb("CONSTS", [128, 2], F32)
    sb["WT"] = K.sb("WT", [128, 4096], BF16)
    sb["WT2"] = K.sb("WT2", [128, 2 * 2048], BF16)
    sb["ZERO"] = K.sb("ZERO", [128, 512], BF16)
    sb["BQK"] = K.sb("BQK", [128, 32], F32)
    sb["ONEC"] = K.sb("ONEC", [128, 1], BF16)
    sb["IDN"] = K.sb("IDN", [128, 128], BF16)
    sb["GS"] = K.sb("GS", [128, 64], F32)
    sb["US"] = K.sb("US", [128, 64], F32)
    sb["KEYS"] = K.sb("KEYS", [128, 128], BF16)
    sb["IOTA8"] = K.sb("IOTA8", [128, 8], F32)
    sb["DEB"] = K.sb("DEB", [128, 2 * 2048], BF16)
    sb["UEB"] = K.sb("UEB", [128, 2 * 2048], BF16)
    sb["CIDX"] = K.sb("CIDX", [1, 8], U32)
    sb["QR"] = K.sb("QR", [64, 2048], BF16)
    rows1 = K.sb("ROWS1", [128, 2176], F32)
    sb["BVROW"] = rows1[0:1, 0:2048]
    sb["ONE_A"] = rows1[0:1, 2048:2176]
    sb["BOROW"] = rows1[32:33, 0:2048]
    sb["ONE_B"] = rows1[32:33, 2048:2176]
    sb["DSROW"] = rows1[64:65, 0:2048]
    sb["ONE_C"] = rows1[64:65, 2048:2176]
    rows2 = K.sb("ROWS2", [128, 2560], F32)
    sb["SLOPE"] = rows2[0:1, 0:2048]
    sb["ALIROW"] = rows2[0:1, 2048:2560]
    sb["REC"] = rows2[32:33, 0:1024]
    sb["DEN"] = rows2[32:33, 1024:2048]
    sb["ONE_D"] = rows2[32:33, 2048:2176]
    sb["RSX"] = rows2[64:65, 0:1024]
    sb["RSH"] = rows2[64:65, 1024:2048]
    sb["ONE_E"] = rows2[64:65, 2048:2560]
    at = K.sb("AT", [128, 10240], U8)
    at_bf, at_f = at.ap().bitcast(BF16), at.ap().bitcast(F32)
    sb["QST"] = at_bf[:, 0:1024]
    sb["QT"] = at_bf[:, 1024:2048]
    sb["EB"] = at_bf[:, 2048:3072]
    sb["EM"] = at_bf[:, 3072:4096]
    sb["RD"] = at_f[:, 2048:2560]
    sb["XBF"] = at_bf[:, 0:2048]
    sb["RT"] = at_f[:, 1024:2048]
    sb["TK"] = at_f[:, 2048:2560]
    aw = K.sb("AW", [128, 8192], U8)
    aw_bf, aw_f = aw.ap().bitcast(BF16), aw.ap().bitcast(F32)
    aw_u = aw.ap().bitcast(U32)
    sb["W01A"] = aw.ap()[:, 0:1408]
    sb["W01B"] = aw.ap()[:, 1408:2816]
    sb["BIAS0"] = aw_f[:, 1408:1408 + len(attn_chunks(0))]
    sb["TKU"] = aw_u[:, 512:576]
    sb["EWID"] = aw_u[:, 576:584]
    sb["EWW"] = aw_f[:, 584:592]
    sb["RS4"] = aw_f[:, 592:608]
    ae = K.sb("AE", [128, 8192], U8)
    sb["DSB"] = ae.ap().bitcast(F32)
    sb["EACC"] = ae.ap().bitcast(F32)

    st = {"sb": sb, "t": t}
    emit_prologue(K, st)
    emit_attention(K, st, layer=0, x_src=t["x0"], x_dst=t["x_scr"],
                   final=(n_sub == 1))
    if n_sub >= 2:
        emit_ffn_l0(K, st, x_src=t["x_scr"], x_dst=t["x_scr"],
                    final=(n_sub == 2))
    if n_sub >= 3:
        emit_attention(K, st, layer=1, x_src=t["x_scr"], x_dst=t["x_scr"],
                       final=(n_sub == 3))
    if n_sub >= 4:
        emit_ffn_l1(K, st, x_src=t["x_scr"], x_dst=t["x_scr"], final=True)

    K.finalize(out_waits=[("sdma", K.sem["sdma"].n), ("gdma", K.sem["gdma"].n)])
    return nc



def emit_prologue(K, st):
    sb, t = st["sb"], st["t"]
    K.dma("sp", sb["W01A"], t["w01a"][:, :])
    K.dma("sp", sb["W01B"], t["w01b"][:, :])
    K.dma("sp", sb["BIAS0"], t["l0bias"][:, :])
    K.dma("sp", sb["BQK"][:, :], t["l0bqk"][:, :])
    K.dma("sp", sb["BVROW"], t["l0bv"][0:1, :])
    K.dma("sp", sb["BOROW"], t["l0bo"][0:1, :])
    K.dma("sp", sb["DSROW"], t["l0ds"][0:1, :])
    K.dma("sp", sb["GS"][:, :], t["l0gs"][:, :])
    K.dma("sp", sb["US"][:, :], t["l0us"][:, :])
    K.dma("sp", sb["CIDX"][:, :], t["cidx"][:, :])
    st["n_pro_sdma"] = K.sem["sdma"].n
    K.dma("pool", sb["KEYS"][0:64, 0:64], t["l1keys"][:, :])
    K.pool(lambda e: e.iota(sb["ALIROW"].bitcast(I32), [[1, 512]], base=0, channel_multiplier=0))
    for nm in ("ONE_A", "ONE_B", "ONE_C", "ONE_D", "ONE_E"):
        K.pool(lambda e, nm=nm: e.memset(sb[nm], 1.0))
    K.pool(lambda e: e.memset(sb["CONSTS"][:, 0:1], EPS))
    K.pool(lambda e: e.memset(sb["CONSTS"][:, 1:2], -CEXP))
    K.pool(lambda e: e.memset(sb["ONEC"][:, :], 1.0))
    K.pool(lambda e: e.memset(sb["ZERO"][:, :], 0.0))
    K.pool(lambda e: e.memset(sb["IDN"][:, :], 1.0))
    for h in range(16):
        K.pool(lambda e, h=h: e.memset(
            sb["SLOPE"][0:1, h * 128:(h + 1) * 128], SLOPES[h]))
    K.pool(lambda e: e.iota(sb["IOTA8"][:, :].bitcast(I32), [[1, 8]], base=0, channel_multiplier=0))
    K.wait("pool", "pool")
    K.pool(lambda e: e.affine_select(sb["LNTM"][:, 0:128], sb["IDN"][:, :],
                                     [[-1, 128]], ALU.is_ge, 0.0,
                                     base=0, channel_multiplier=1))
    K.wait("pool", "pool")
    K.pool(lambda e: e.affine_select(sb["IDN"][:, :], sb["LNTM"][:, 0:128],
                                     [[1, 128]], ALU.is_ge, 0.0,
                                     base=0, channel_multiplier=-1))
    npool = K.sem["pool"].n
    K.wait("dve", "pool", npool)
    K.dve(lambda e: e.tensor_scalar_mul(sb["ALIROW"],
                                        sb["ALIROW"].bitcast(I32), -1.0 / CINV))
    K.dve(lambda e: e.tensor_copy(sb["IOTA8"][:, :],
                                  sb["IOTA8"][:, :].bitcast(I32)))
    st["n_pro_dve"] = K.sem["dve"].n
    st["n_pro_pool"] = npool



def emit_ln_chunk(K, st, tc, src_ap, out_ap, save_stats=None):
    sb = st["sb"]
    S = sb["STAT"]
    ssum, sqs = S[:, tc:tc + 1], S[:, 8 + tc:9 + tc]
    mean, var = S[:, 16 + tc:17 + tc], S[:, 24 + tc:25 + tc]
    sd, rr = S[:, 32 + tc:33 + tc], S[:, 40 + tc:41 + tc]
    nmr, msq = S[:, 48 + tc:49 + tc], S[:, 56 + tc:57 + tc]
    K.dve(lambda e: e.reduce_sum(ssum, src_ap, axis=AXX))
    K.act(lambda e: e.activation(out=sb["LNTM"][:, 0:2048], in_=src_ap,
                                 func=AF.Square, accum_out=sqs))
    K.wait("dve", "dve")
    K.dve(lambda e: e.tensor_scalar_mul(mean, ssum, 1.0 / 2048.0))
    K.wait("dve", "dve")
    K.dve(lambda e: e.tensor_mul(msq, mean, mean))
    K.wait("dve", "act")
    K.wait("dve", "dve")
    K.dve(lambda e: e.scalar_tensor_tensor(var, sqs, 1.0 / 2048.0, msq,
                                           ALU.mult, ALU.subtract))
    K.wait("act", "dve")
    K.act(lambda e: e.activation(out=sd, in_=var, func=AF.Sqrt,
                                 bias=sb["CONSTS"][:, 0:1]))
    K.wait("dve", "act")
    K.dve(lambda e: e.reciprocal(rr, sd))
    K.wait("dve", "dve")
    K.dve(lambda e: e.tensor_scalar(nmr, mean, rr, -1.0, ALU.mult, ALU.mult))
    K.wait("dve", "dve")
    K.dve(lambda e: e.tensor_scalar(out_ap, src_ap, rr, nmr, ALU.mult, ALU.add))
    if save_stats is not None:
        name, off = save_stats
        K.wait("dve", "dve")
        K.dve(lambda e: e.tensor_copy(sb[name][:, off + tc:off + tc + 1], rr))
        K.dve(lambda e: e.tensor_copy(sb[name][:, off + 8 + tc:off + 9 + tc], nmr))
    return K.sem["dve"].n


def emit_ln_transpose(K, st, x_src, save_stats=None):
    sb = st["sb"]
    K.wait("act", "pool", st["n_pro_pool"])
    K.wait("pe", "pool", st["n_pro_pool"])
    K.wait("dve", "pool", st["n_pro_pool"])
    for tc in range(NTC):
        par = tc % 2
        xc = sb["XCH"][:, par * 2048:(par + 1) * 2048]
        lc = sb["LNTM"][:, 0:2048]
        K.wait("sp", "dve")
        K.wait("sp", "pe")
        K.dma("sp", xc, x_src[tc * 128:(tc + 1) * 128, :])
        K.wait("dve", "sdma")
        K.wait("act", "sdma")
        K.wait("dve", "pe")
        ndve = emit_ln_chunk(K, st, tc, xc, lc, save_stats)
        K.wait("pe", "dve", ndve)
        for half in range(2):
            bank = 6 + half
            for dep in K.bank_war[bank]:
                K.wait("pe", *dep)
            K.bank_war[bank] = []
            pb = K.psum[bank].ap().bitcast(BF16)
            for k in range(8):
                dc = half * 8 + k
                src = sb["LNTM"][:, dc * 128:(dc + 1) * 128]
                K.pe(lambda e, o=pb[:, k * 128:(k + 1) * 128], s=src:
                     e.transpose(o, s, sb["IDN"][:, :]),
                     sem="pe" if k == 7 else None)
            npe = K.sem["pe"].n
            K.wait("dve", "pe", npe)
            dst = bass.AP(sb["A4L"], (half * 8) * 1024 + tc * 128,
                          [[16384, 128], [1024, 8], [1, 128]])
            K.dve(lambda e, d=dst, s=pb: e.tensor_copy(d, s))
            K.bank_war[bank].append(("dve", K.sem["dve"].n))



def emit_attention(K, st, layer, x_src, x_dst, final):
    sb, t = st["sb"], st["t"]
    A4L, A8 = sb["A4L"], sb["A8"]

    emit_ln_transpose(K, st, x_src)
    nln = K.sem["dve"].n
    K.wait("pe", "dve", nln)

    wt_last = [0, 0]
    qst_war = [0, 0]
    for j in range(32):
        par = j % 2
        wtile = sb["WT2"][:, par * 2048:(par + 1) * 2048]
        K.wait("pool", "pe", wt_last[par])
        if layer == 0:
            K.dma("pool", wtile, t["l0qk_t"][j, :, :])
        else:
            src = t["l1q_t"] if j < 16 else t["l1k_t"]
            K.dma("pool", wtile, src[j % 16, :, :])
        ng = K.sem["gdma"].n
        K.wait("pe", "gdma", ng)
        for u in range(2):
            bank = (2 * j + u) % 4
            mms = [(wtile[:, dc * 128:(dc + 1) * 128],
                    A4L[:, dc * 1024 + u * 512:dc * 1024 + (u + 1) * 512])
                   for dc in range(NDC)]
            K.mm_group(bank, mms)
            K.wait("dve", "pe", K.bank_rdy[bank])
            if j < 16:
                par2 = (2 * j + u) % 2
                stg = sb["QST"][:, par2 * 512:(par2 + 1) * 512]
                K.wait("dve", "sdma", qst_war[par2])
                if layer == 0:
                    K.dve(lambda e, o=stg, p=K.psum[bank], bb=sb["BQK"][:, j:j + 1]:
                          e.tensor_scalar_add(o, p[:, :], bb))
                else:
                    K.dve(lambda e, o=stg, p=K.psum[bank]:
                          e.tensor_copy(o, p[:, :]))
                K.bank_war[bank].append(("dve", K.sem["dve"].n))
                K.wait("sp", "dve")
                K.dma("sp", t["qdram"][j, u, :, :], stg)
                qst_war[par2] = K.sem["sdma"].n
            else:
                jc = j - 16
                dst = bass.AP(A8, (4 * u) * 2048 + jc * 128,
                              [[32768, 128], [2048, 4], [1, 128]])
                if layer == 0:
                    K.dve(lambda e, o=dst, p=K.psum[bank], bb=sb["BQK"][:, j:j + 1]:
                          e.tensor_scalar_add(o, p[:, :], bb))
                else:
                    K.dve(lambda e, o=dst, p=K.psum[bank]:
                          e.tensor_copy(o, p[:, :]))
                K.bank_war[bank].append(("dve", K.sem["dve"].n))
        wt_last[par] = K.sem["pe"].n

    vsrc = t["l0v_t"] if layer == 0 else t["l1v_t"]
    for jv in range(4):
        if layer == 0:
            K.wait("pe", "sdma", st["n_pro_sdma"])
        for half in range(2):
            K.wait("pool", "pe", K.sem["pe"].n)
            K.dma("pool", sb["WT"][:, :],
                  vsrc[jv, :, half * 4096:(half + 1) * 4096])
            ng = K.sem["gdma"].n
            K.wait("pe", "gdma", ng)
            for tc in range(NTC):
                bank = tc
                if half == 0:
                    for dep in K.bank_war[bank]:
                        K.wait("pe", *dep)
                    K.bank_war[bank] = []
                pt = K.psum[bank][:, :]
                for dci in range(8):
                    dc = half * 8 + dci
                    l = A4L[:, dc * 1024 + tc * 128:dc * 1024 + (tc + 1) * 128]
                    r = sb["WT"][:, dci * 512:(dci + 1) * 512]
                    lastmm = (half == 1 and dci == 7 and layer != 0)
                    K.pe(lambda e, l=l, r=r, pt=pt,
                         st_=(half == 0 and dci == 0), tl=lastmm:
                         e.matmul(pt, l, r, start=st_, stop=tl),
                         sem="pe" if dci == 7 else None)
                if half == 1 and layer == 0:
                    K.pe(lambda e, pt=pt,
                         r=sb["BVROW"][0:1, jv * 512:(jv + 1) * 512]:
                         e.matmul(pt, sb["ONE_A"], r,
                                  start=False, stop=True), sem="pe")
                if half == 1:
                    K.bank_rdy[bank] = K.sem["pe"].n
                    dst = A8[:, 16384 + tc * 2048 + jv * 512:
                             16384 + tc * 2048 + (jv + 1) * 512]
                    K.wait("act", "pe", K.bank_rdy[bank])
                    K.act(lambda e, d=dst, p=K.psum[bank]:
                          e.activation(out=d, in_=p[:, :], func=AF.Copy))
                    K.bank_war[bank].append(("act", K.sem["act"].n))

    K.wait("sp", "act")
    K.wait("sp", "dve")
    K.dma("sp", t["cc_kv"][0:128, :], A8[:, 0:16384])
    K.dma("sp", t["cc_kv"][128:256, :], A8[:, 16384:32768])
    nsd = K.sem["sdma"].n
    if DEV_STAGE == 1:
        K.wait("sp", "dve")
        K.dma("sp", t["dbg_ln"][:, :], A4L[:, :])
        return
    K.wait("pool", "sdma", nsd)
    K.pool(lambda e: e.collective_compute(
        "AllGather", ALU.bypass,
        replica_groups=[[0, 1], [2, 3], [4, 5], [6, 7]],
        ins=[t["cc_kv"].ap()],
        outs=[t["kv_all"].ap()]), sem="cc")
    ncc = K.sem["cc"].n

    chunks = attn_chunks(layer)
    zslots = {}
    for ch in chunks:
        zslots.setdefault((ch[0], ch[1]), []).append(ch)
    bias_col = {ch: i for i, ch in enumerate(attn_chunks(0))}
    ebuf_war = [0, 0]

    for hg in range(2):
        K.wait("sp", "cc", ncc)
        K.wait("sp", "pe")
        for kc in range(16):
            r, lt = kc_owner(kc)
            base = lt * 2048 + hg * 1024
            K.dma("sp", A8[:, kc * 1024:(kc + 1) * 1024],
                  t["kv_all"][r * 256:r * 256 + 128, base:base + 1024])
            K.dma("sp", A8[:, 16384 + kc * 1024:16384 + (kc + 1) * 1024],
                  t["kv_all"][r * 256 + 128:r * 256 + 256, base:base + 1024])
        nkv = K.sem["sdma"].n
        K.wait("pe", "sdma", nkv)

        for head in range(hg * 8, hg * 8 + 8):
            for slot in range(2):
                lst = zslots.get((head, slot))
                if not lst:
                    continue
                hpar = (head * 2 + slot) % 2
                zbank, dbank, rbank = 2 + hpar, 0 + hpar, 4 + hpar
                qt = sb["QT"][:, hpar * 512:(hpar + 1) * 512]
                K.wait("sp", "pe")
                K.dma("sp", qt, t["qdram"][head, slot, :, :])
                K.wait("pe", "sdma")
                for bk in (zbank, dbank):
                    for dep in K.bank_war[bk]:
                        K.wait("pe", *dep)
                    K.bank_war[bk] = []
                first = True
                nch = len(lst)
                for ch_i, ch in enumerate(lst):
                    _, _, kc, masked = ch
                    spar = ch_i % 2
                    sbank = 6 + spar
                    for dep in K.bank_war[sbank]:
                        K.wait("pe", *dep)
                    K.bank_war[sbank] = []
                    koff = kc * 1024 + (head - hg * 8) * 128
                    ktile = A8[:, koff:koff + 128]
                    vtile = A8[:, 16384 + koff:16384 + koff + 128]
                    if layer == 0:
                        K.pe(lambda e, b=sbank, l=ktile, r=qt:
                             e.matmul(K.psum[b][:, :], l, r,
                                      start=True, stop=False))
                        K.pe(lambda e, b=sbank,
                             l=sb["SLOPE"][0:1, head * 128:(head + 1) * 128]:
                             e.matmul(K.psum[b][:, :], l, sb["ALIROW"],
                                      start=False, stop=True), sem="pe")
                    else:
                        K.pe(lambda e, b=sbank, l=ktile, r=qt:
                             e.matmul(K.psum[b][:, :], l, r,
                                      start=True, stop=True), sem="pe")
                    nsc = K.sem["pe"].n
                    eb = sb["EB"][:, spar * 512:(spar + 1) * 512]
                    K.wait("act", "pe", nsc)
                    K.wait("act", "pe", ebuf_war[spar])
                    K.wait("act", "dve")
                    if layer == 0:
                        bias = sb["BIAS0"][:, bias_col[ch]:bias_col[ch] + 1]
                        K.act(lambda e, o=eb, p=K.psum[sbank], bb=bias:
                              e.activation(out=o, in_=p[:, :], func=AF.Exp,
                                           bias=bb, scale=CINV))
                    else:
                        K.act(lambda e, o=eb, p=K.psum[sbank]:
                              e.activation(out=o, in_=p[:, :], func=AF.Exp,
                                           bias=sb["CONSTS"][:, 1:2],
                                           scale=CINV))
                    K.bank_war[sbank].append(("act", K.sem["act"].n))
                    if masked:
                        em = sb["EM"][:, spar * 512:(spar + 1) * 512]
                        strip = sb["W01A"] if slot == 0 else sb["W01B"]
                        kcm = 7 if slot == 0 else 15
                        off = (kcm - kc) * 128
                        K.wait("dve", "act")
                        K.wait("dve", "pe", ebuf_war[spar])
                        if layer == 0:
                            K.dve(lambda e, o=em, m=strip[:, off:off + 512], a=eb:
                                  e.select(o, m, a, sb["ZERO"][:, :]))
                        else:
                            K.dve(lambda e, o=em, m=strip[:, off:off + 512], a=eb:
                                  e.tensor_mul(o, a, m))
                        etile = em
                        K.wait("pe", "dve")
                    else:
                        etile = eb
                        K.wait("pe", "act")
                    K.pe(lambda e, b=dbank, r=etile, s=first, tl=(ch_i == nch - 1):
                         e.matmul(K.psum[b][0:1, :], sb["ONEC"][:, 0:1], r,
                                  start=s, stop=tl),
                         sem="pe" if ch_i == nch - 1 else None)
                    K.pe(lambda e, b=zbank, l=vtile, r=etile, s=first,
                         tl=(ch_i == nch - 1):
                         e.matmul(K.psum[b][:, :], l, r, start=s, stop=tl),
                         sem="pe" if ch_i == nch - 1 else None)
                    ebuf_war[spar] = K.sem["pe"].n
                    first = False
                nz = K.sem["pe"].n
                den = sb["DEN"][0:1, hpar * 512:(hpar + 1) * 512]
                rec = sb["REC"][0:1, hpar * 512:(hpar + 1) * 512]
                rdb = sb["RD"][:, 0:512]
                K.wait("dve", "pe", nz)
                K.dve(lambda e, o=den, p=K.psum[dbank]:
                      e.tensor_copy(o, p[0:1, :]))
                K.bank_war[dbank].append(("dve", K.sem["dve"].n))
                K.wait("dve", "dve")
                K.dve(lambda e, o=rec, i=den: e.reciprocal(o, i))
                nrec = K.sem["dve"].n
                for dep in K.bank_war[rbank]:
                    K.wait("pe", *dep)
                K.bank_war[rbank] = []
                K.wait("pe", "dve", nrec)
                K.pe(lambda e, b=rbank, r=rec:
                     e.matmul(K.psum[b][:, :], sb["ONE_D"], r,
                              start=True, stop=True), sem="pe")
                nrb = K.sem["pe"].n
                K.wait("act", "pe", nrb)
                K.wait("act", "dve")
                K.act(lambda e, o=rdb, p=K.psum[rbank]:
                      e.activation(out=o, in_=p[:, :], func=AF.Copy))
                K.bank_war[rbank].append(("act", K.sem["act"].n))
                zdst = A4L[:, head * 1024 + slot * 512:
                           head * 1024 + (slot + 1) * 512]
                K.wait("dve", "act")
                K.dve(lambda e, o=zdst, p=K.psum[zbank], r=rdb:
                      e.tensor_mul(o, p[:, :], r))
                K.bank_war[zbank].append(("dve", K.sem["dve"].n))

    if DEV_STAGE == 2:
        K.wait("sp", "dve")
        K.dma("sp", t["dbg_z"][:, :], A4L[:, :])
        K.wait("sp", "dve")
        K.dma("sp", t["dbg_ln"][:, :], A8[:, 0:16384])
        return
    K.wait("pool", "pe")
    wosrc = t["l0wo_t"] if layer == 0 else t["l1wo_t"]
    K.dma("pool", A8[:, :], wosrc[:, :])
    K.wait("pe", "gdma")
    K.wait("pe", "dve")
    emit_proj_ln_residual(
        K, st, x_src=x_src, x_dst=x_dst, final=final,
        lhs_of=lambda k, tc: A4L[:, k * 1024 + (tc // 4) * 512
                                 + (tc % 4) * 128:k * 1024 + (tc // 4) * 512
                                 + (tc % 4) * 128 + 128],
        rhs_of=lambda k, dc: A8[:, k * 2048 + dc * 512:k * 2048 + (dc + 1) * 512],
        nk=16, bias_row=("BOROW", 0) if layer == 0 else None,
        scale_bcast=False, extra_add=None)


def emit_proj_ln_residual(K, st, x_src, x_dst, final, lhs_of, rhs_of,
                          nk, bias_row, scale_bcast, extra_add):
    sb, t = st["sb"], st["t"]
    prev_res = 0
    for tc in range(NTC):
        for dc in range(4):
            bank = (tc * 4 + dc) % 2
            mms = [(lhs_of(k, tc), rhs_of(k, dc)) for k in range(nk)]
            tail = None
            if bias_row is not None:
                name, off = bias_row
                tail = [(sb["ONE_B"],
                         sb[name][0:1, off + dc * 512:off + (dc + 1) * 512])]
            K.mm_group(bank, mms, tail=tail)
            dst = sb["FTC"][:, dc * 512:(dc + 1) * 512]
            if scale_bcast:
                K.wait("dve", "pe", K.bank_rdy[bank])
                K.wait("dve", "dve", prev_res)
                K.dve(lambda e, o=dst, p=K.psum[bank],
                      s=sb["DSB"][:, dc * 512:(dc + 1) * 512]:
                      e.tensor_mul(o, p[:, :], s))
                K.bank_war[bank].append(("dve", K.sem["dve"].n))
            else:
                K.wait("act", "pe", K.bank_rdy[bank])
                K.wait("act", "dve", prev_res)
                K.act(lambda e, o=dst, p=K.psum[bank]:
                      e.activation(out=o, in_=p[:, :], func=AF.Copy))
                K.bank_war[bank].append(("act", K.sem["act"].n))
        nfa = K.sem["act"].n
        if extra_add is not None:
            K.wait("dve", "act", nfa)
            K.wait("dve", "dve")
            K.wait("dve", "pool")
            K.dve(lambda e, a=extra_add(tc):
                  e.tensor_add(sb["FTC"][:, :], sb["FTC"][:, :], a))
        emit_ln_residual_store(K, st, tc, sb["FTC"][:, :], nfa,
                               x_src, x_dst, final)
        prev_res = K.sem["dve"].n



def emit_ln_residual_store(K, st, tc, F_ap, nfa, x_src, x_dst, final):
    sb, t = st["sb"], st["t"]
    par = tc % 2
    xc = sb["XCH"][:, par * 2048:(par + 1) * 2048]
    K.wait("sp", "dve")
    K.dma("sp", xc, x_src[tc * 128:(tc + 1) * 128, :])
    nx = K.sem["sdma"].n
    K.wait("dve", "act", nfa)
    K.wait("dve", "dve")
    K.wait("act", "act", nfa)
    K.wait("act", "dve", K.sem["dve"].n)
    emit_ln_chunk(K, st, tc, F_ap, F_ap)
    K.wait("dve", "sdma", nx)
    K.wait("dve", "dve")
    K.dve(lambda e, xc=xc, f=F_ap: e.tensor_add(xc, xc, f))
    nxn = K.sem["dve"].n
    K.wait("sp", "dve", nxn)
    dst = t["out"] if final else x_dst
    K.dma("sp", dst[tc * 128:(tc + 1) * 128, :], xc)


def emit_ffn_l0(K, st, x_src, x_dst, final):
    sb, t = st["sb"], st["t"]
    A4L, A8 = sb["A4L"], sb["A8"]
    a4f = A4L.ap().bitcast(F32)

    emit_ln_transpose(K, st, x_src)
    nln = K.sem["dve"].n
    K.wait("pe", "dve", nln)

    K.wait("pe", "sdma", st["n_pro_sdma"])
    for dc in range(4):
        bank = dc % 2
        K.mm_group(bank, [(sb["ONE_C"], sb["DSROW"][0:1, dc * 512:(dc + 1) * 512])])
        K.wait("act", "pe", K.bank_rdy[bank])
        K.act(lambda e, o=sb["DSB"][:, dc * 512:(dc + 1) * 512], p=K.psum[bank]:
              e.activation(out=o, in_=p[:, :], func=AF.Copy))
        K.bank_war[bank].append(("act", K.sem["act"].n))
    for u in range(2):
        bank = 4 + u
        mms = [(sb["ONEC"][:, 0:1],
                A4L[:, dc * 1024 + u * 512:dc * 1024 + (u + 1) * 512])
               for dc in range(NDC)]
        K.mm_group(bank, mms, out=K.psum[bank][0:1, :])
        K.wait("dve", "pe", K.bank_rdy[bank])
        K.dve(lambda e, o=sb["RSX"][0:1, u * 512:(u + 1) * 512], p=K.psum[bank]:
              e.tensor_scalar_mul(o, p[0:1, :], -128.0))
        K.bank_war[bank].append(("dve", K.sem["dve"].n))
    nrsx = K.sem["dve"].n
    K.wait("pe", "dve", nrsx)

    ga = sb["QT"]
    sgt = sb["QST"]
    ua = sb["EB"]
    hst = sb["EM"]
    hst_war = 0
    wslots = [sb["WT2"][:, 0:2048], sb["WT2"][:, 2048:4096],
              sb["WT"][:, 0:2048], sb["WT"][:, 2048:4096]]
    wslot_last = [K.sem["pe"].n] * 4
    for u in range(2):
        hbank = 2 + u
        for dep in K.bank_war[hbank]:
            K.wait("pe", *dep)
        K.bank_war[hbank] = []
        for ic in range(64):
            for gi, (wsrc, scl) in enumerate(
                    ((t["l0g_t"], sb["GS"]), (t["l0u_t"], sb["US"]))):
                si = (2 * ic + gi) % 4
                wtile = wslots[si]
                K.wait("pool", "pe", wslot_last[si])
                K.dma("pool", wtile, wsrc[ic, :, :])
                K.wait("pe", "gdma")
                bank = gi
                mms = [(wtile[:, dc * 128:(dc + 1) * 128],
                        A4L[:, dc * 1024 + u * 512:dc * 1024 + (u + 1) * 512])
                       for dc in range(NDC)]
                tail = [(sb["ONE_E"][0:1, 0:128],
                         sb["RSX"][0:1, u * 512:(u + 1) * 512])]
                K.mm_group(bank, mms, tail=tail)
                K.wait("dve", "pe", K.bank_rdy[bank])
                dvedst = sgt if gi == 0 else ua
                K.wait("dve", "act")
                K.dve(lambda e, o=dvedst[:, 0:512], p=K.psum[bank],
                      s=scl[:, ic:ic + 1]:
                      e.tensor_scalar_mul(o, p[:, :], s))
                K.bank_war[bank].append(("dve", K.sem["dve"].n))
                wslot_last[si] = K.sem["pe"].n
            K.wait("act", "dve")
            K.act(lambda e: e.activation(out=ga[:, 0:512], in_=sgt[:, 0:512],
                                         func=AF.Silu))
            K.wait("dve", "act")
            K.wait("dve", "pe", hst_war)
            K.wait("dve", "sdma")
            K.dve(lambda e: e.tensor_mul(hst[:, 0:512], ga[:, 0:512],
                                         ua[:, 0:512]))
            nh = K.sem["dve"].n
            K.wait("pe", "dve", nh)
            K.pe(lambda e, b=hbank, s=(ic == 0), tl=(ic == 63):
                 e.matmul(K.psum[b][0:1, :], sb["ONEC"][:, 0:1], hst[:, 0:512],
                          start=s, stop=tl), sem="pe")
            hst_war = K.sem["pe"].n
            K.wait("sp", "dve", nh)
            K.dma("sp", t["h_scr"][ic, :, u * 512:(u + 1) * 512], hst[:, 0:512])
        K.wait("dve", "pe", K.sem["pe"].n)
        K.dve(lambda e, o=sb["RSH"][0:1, u * 512:(u + 1) * 512], p=K.psum[hbank]:
              e.tensor_scalar_mul(o, p[0:1, :], -128.0))
        K.bank_war[hbank].append(("dve", K.sem["dve"].n))
    nrsh = K.sem["dve"].n
    K.wait("pe", "dve", nrsh)

    K.wait("sp", "dve")
    K.wait("sp", "act")
    K.wait("pool", "dve")
    K.wait("pool", "act")
    p2_last = [K.sem["pe"].n] * 2
    for u in range(2):
        for dh in range(2):
            for bk in range(8):
                for dep in K.bank_war[bk]:
                    K.wait("pe", *dep)
                K.bank_war[bk] = []
            for ic in range(64):
                wpar = ic % 2
                wtile = sb["WT"][:, wpar * 1024:(wpar + 1) * 1024]
                hpar = ic % 2
                htile = (sb["QST"] if hpar == 0 else sb["QT"])
                K.wait("pool", "pe", p2_last[wpar])
                K.dma("pool", wtile,
                      t["l0d_t"][ic, :, dh * 1024:(dh + 1) * 1024])
                K.wait("sp", "pe", p2_last[hpar])
                K.dma("sp", htile[:, 0:512],
                      t["h_scr"][ic, :, u * 512:(u + 1) * 512])
                K.wait("pe", "gdma")
                K.wait("pe", "sdma")
                for tc4 in range(4):
                    for dco in range(2):
                        bank = tc4 * 2 + dco
                        l = htile[:, tc4 * 128:(tc4 + 1) * 128]
                        r = wtile[:, dco * 512:(dco + 1) * 512]
                        K.pe(lambda e, b=bank, l=l, r=r, s=(ic == 0):
                             e.matmul(K.psum[b][:, :], l, r,
                                      start=s, stop=False),
                             sem="pe" if (tc4 == 3 and dco == 1) else None)
                p2_last[ic % 2] = K.sem["pe"].n
            for tc4 in range(4):
                for dco in range(2):
                    bank = tc4 * 2 + dco
                    l = sb["RSH"][0:1, u * 512 + tc4 * 128:
                                  u * 512 + (tc4 + 1) * 128]
                    K.pe(lambda e, b=bank, l=l:
                         e.matmul(K.psum[b][:, :], l, sb["ONE_E"],
                                  start=False, stop=True), sem="pe")
                    K.bank_rdy[bank] = K.sem["pe"].n
            npeall = K.sem["pe"].n
            K.wait("dve", "pe", npeall)
            K.wait("dve", "act")
            for tc4 in range(4):
                for dco in range(2):
                    bank = tc4 * 2 + dco
                    doff = dh * 1024 + dco * 512
                    dst = a4f[:, tc4 * 2048 + doff:tc4 * 2048 + doff + 512]
                    K.dve(lambda e, o=dst, p=K.psum[bank],
                          s=sb["DSB"][:, doff:doff + 512]:
                          e.tensor_mul(o, p[:, :], s))
                    K.bank_war[bank].append(("dve", K.sem["dve"].n))
        nfa = K.sem["act"].n
        for tc4 in range(4):
            tc = u * 4 + tc4
            emit_ln_residual_store(K, st, tc,
                                   a4f[:, tc4 * 2048:(tc4 + 1) * 2048],
                                   nfa, x_src, x_dst, final)


def emit_ffn_l1(K, st, x_src, x_dst, final):
    sb, t = st["sb"], st["t"]
    nc = K.nc
    A4L = sb["A4L"]
    a4f = A4L.ap().bitcast(F32)

    emit_ln_transpose(K, st, x_src, save_stats=("RS4", 0))
    nln = K.sem["dve"].n
    K.wait("pe", "dve", nln)
    if DEV_STAGE == 7:
        K.wait("sp", "dve")
        K.dma("sp", t["dbg_rt"][0:128, 0:16], sb["RS4"])
        K.dma("sp", t["dbg_e"][0:128, 0:512],
              sb["A4L"][:, 0:1024].bitcast(F32))
        return

    wq = sb["WT2"][:, 0:2048]
    K.wait("pool", "pe")
    K.dma("pool", wq, t["l1wq_t"][:, :])
    K.wait("pe", "gdma")
    for u in range(2):
        for half in range(2):
            bank = u
            mms = [(wq[:, dc * 128 + half * 64:dc * 128 + half * 64 + 64],
                    A4L[:, dc * 1024 + u * 512:dc * 1024 + (u + 1) * 512])
                   for dc in range(NDC)]
            K.mm_group(bank, mms, out=K.psum[bank][0:64, :])
            K.wait("dve", "pe", K.bank_rdy[bank])
            K.dve(lambda e, o=sb["QR"][0:64, half * 1024 + u * 512:
                                       half * 1024 + (u + 1) * 512],
                  p=K.psum[bank]:
                  e.tensor_copy(o, p[0:64, :]))
            K.bank_war[bank].append(("dve", K.sem["dve"].n))
    nqr = K.sem["dve"].n
    if DEV_STAGE == 8:
        K.wait("sp", "dve", nqr)
        K.dma("sp", t["dbg_e"][0:128, 0:512], sb["QR"][:, :].bitcast(F32))
        return
    K.wait("pe", "dve", nqr)
    K.wait("pe", "gdma")
    for tc in range(NTC):
        bank = 2 + (tc % 2)
        for dep in K.bank_war[bank]:
            K.wait("pe", *dep)
        K.bank_war[bank] = []
        u, w = tc // 4, tc % 4
        for half in range(2):
            l = sb["QR"][0:64, half * 1024 + tc * 128:
                         half * 1024 + (tc + 1) * 128]
            r = sb["KEYS"][0:64, 0:64]
            K.pe(lambda e, b=bank, l=l, r=r, h=half:
                 e.matmul(K.psum[b][:, h * 64:(h + 1) * 64], l, r,
                          start=True, stop=True), sem="pe")
        nsc = K.sem["pe"].n
        if DEV_STAGE == 10:
            K.bank_war[bank].append(("pe", nsc))
            continue
        K.wait("dve", "pe", nsc)
        K.wait("dve", "sdma")
        K.dve(lambda e, o=sb["RD"][:, 0:128], p=K.psum[bank]:
              e.tensor_copy(o, p[:, 0:128]))
        K.bank_war[bank].append(("dve", K.sem["dve"].n))
        if DEV_STAGE == 9:
            continue
        K.wait("sp", "dve")
        K.dma("sp", bass.AP(t["cc_rt"], tc * 16384, [[128, 128], [1, 128]]),
              sb["RD"][:, 0:128])
    nrt = K.sem["sdma"].n
    if DEV_STAGE == 10:
        K.wait("sp", "pe")
        K.dma("sp", t["dbg_e"][0:128, 0:512], sb["QR"][:, :].bitcast(F32))
        return
    if DEV_STAGE == 9:
        K.wait("sp", "dve")
        K.dma("sp", t["dbg_e"][0:128, 0:128], sb["RD"][:, 0:128])
        return
    if DEV_STAGE == 5:
        K.wait("sp", "sdma", nrt)
        K.dma("sp", t["dbg_rt"][0:2048, :], t["cc_rt"][:, :])
        return
    K.wait("pool", "sdma", nrt)
    K.pool(lambda e: e.collective_compute(
        "AllGather", ALU.bypass,
        replica_groups=[[0, 1, 2, 3, 4, 5, 6, 7]],
        ins=[t["cc_rt"].ap()],
        outs=[t["rt_all"].ap()]), sem="cc")
    ncc = K.sem["cc"].n
    if DEV_STAGE == 6:
        K.wait("sp", "cc", ncc)
        K.dma("sp", t["dbg_rt"][:, :], t["rt_all"][:, :])
        return

    regs = [nc.gpsimd.alloc_register(f"rtoff{i}") for i in range(4)]
    K.wait("pool", "cc", ncc)
    for side in range(2):
        for blk in range(2):
            i = side * 2 + blk
            K.pool(lambda e, r=regs[i], c=sb["CIDX"][0:1, i:i + 1]:
                   e.reg_load(r, c), sem=None)
            src = bass.AP(t["rt_all"], regs[i], [[64, 128], [8192, 4], [1, 64]])
            K.dma("pool", sb["RT"][:, i * 256:(i + 1) * 256], src)
    nrta = K.sem["gdma"].n

    if DEV_STAGE == 3:
        K.wait("sp", "sdma", nrta)
        K.wait("sp", "gdma")
        K.dma("sp", t["dbg_rt"][:, :], t["rt_all"][:, :])
        K.dma("sp", t["dbg_e"][0:128, 0:1024], sb["RT"][:, :])
        return
    TK, TKU = sb["TK"], sb["TKU"]
    IO8 = sb["IOTA8"][:, :]
    deb_war = [0, 0]
    for tc in range(NTC):
        blk, w = tc // 4, tc % 4
        t0 = sb["RT"][:, blk * 256 + w * 64:blk * 256 + (w + 1) * 64]
        t1 = sb["RT"][:, 512 + blk * 256 + w * 64:512 + blk * 256 + (w + 1) * 64]
        K.wait("dve", "gdma", nrta)
        sx, sy = TK[:, 0:8], TK[:, 8:16]
        ixf, iyf = TK[:, 16:24], TK[:, 24:32]
        ix64 = TK[:, 32:40]
        alls, alli = TK[:, 64:128], TK[:, 128:192]
        s8 = TK[:, 192:200]
        i_f, j_f = TK[:, 200:208], TK[:, 208:216]
        seli, selj = TK[:, 216:224], TK[:, 224:232]
        neg0, ssum, srec = TK[:, 232:233], TK[:, 233:234], TK[:, 234:235]
        ex8, wts, ew8 = TK[:, 240:248], TK[:, 248:256], TK[:, 256:264]
        ixu, iyu, pku = TKU[:, 0:8], TKU[:, 8:16], TKU[:, 16:24]
        iu, ju = TKU[:, 24:32], TKU[:, 32:40]
        K.dve(lambda e, o=sx, i=t0: e.max(out=o, in_=i))
        K.wait("dve", "dve")
        K.dve(lambda e, o=ixu, m=sx, i=t0: e.max_index(out=o, in_max=m, in_values=i))
        K.dve(lambda e, o=sy, i=t1: e.max(out=o, in_=i))
        K.wait("dve", "dve")
        K.dve(lambda e, o=iyu, m=sy, i=t1: e.max_index(out=o, in_max=m, in_values=i))
        K.wait("dve", "dve")
        K.dve(lambda e, o=ixf, i=ixu: e.tensor_copy(o, i))
        K.dve(lambda e, o=iyf, i=iyu: e.tensor_copy(o, i))
        K.wait("dve", "dve")
        K.dve(lambda e, o=ix64, i=ixf: e.tensor_scalar_mul(o, i, 64.0))
        K.wait("dve", "dve")
        for i8 in range(8):
            K.dve(lambda e, o=alls[:, i8 * 8:(i8 + 1) * 8], i=sy,
                  sc=sx[:, i8:i8 + 1]:
                  e.tensor_scalar_add(o, i, sc))
            K.dve(lambda e, o=alli[:, i8 * 8:(i8 + 1) * 8], i=iyf,
                  sc=ix64[:, i8:i8 + 1]:
                  e.tensor_scalar_add(o, i, sc))
        K.wait("dve", "dve")
        K.dve(lambda e, o=s8, i=alls: e.max(out=o, in_=i))
        K.wait("dve", "dve")
        K.dve(lambda e, o=pku, m=s8, i=alls: e.max_index(out=o, in_max=m, in_values=i))
        K.wait("dve", "dve")
        K.dve(lambda e, o=iu, i=pku: e.tensor_single_scalar(o, i, 3, ALU.logical_shift_right))
        K.dve(lambda e, o=ju, i=pku: e.tensor_single_scalar(o, i, 7, ALU.bitwise_and))
        K.wait("dve", "dve")
        K.dve(lambda e, o=i_f, i=iu: e.tensor_copy(o, i))
        K.dve(lambda e, o=j_f, i=ju: e.tensor_copy(o, i))
        K.wait("dve", "dve")
        for kk in range(8):
            K.dve(lambda e, o=TK[:, 448:456], i=IO8, sc=i_f[:, kk:kk + 1]:
                  e.tensor_single_scalar(o, i, sc, ALU.is_equal))
            K.wait("dve", "dve")
            K.dve(lambda e, m=TK[:, 448:456], v=ix64, o=TK[:, 456:464],
                  a=seli[:, kk:kk + 1]:
                  e.scalar_tensor_tensor(o, m, 1.0, v, ALU.bypass, ALU.mult,
                                         accum_out=a))
            K.dve(lambda e, o=TK[:, 448:456], i=IO8, sc=j_f[:, kk:kk + 1]:
                  e.tensor_single_scalar(o, i, sc, ALU.is_equal))
            K.wait("dve", "dve")
            K.dve(lambda e, m=TK[:, 448:456], v=iyf, o=TK[:, 456:464],
                  a=selj[:, kk:kk + 1]:
                  e.scalar_tensor_tensor(o, m, 1.0, v, ALU.bypass, ALU.mult,
                                         accum_out=a))
            K.wait("dve", "dve")
        K.dve(lambda e, o=TK[:, 264:272], a=seli, b=selj: e.tensor_add(o, a, b))
        K.wait("dve", "dve")
        K.dve(lambda e, o=sb["EWID"], i=TK[:, 264:272]: e.tensor_copy(o, i))
        nid = K.sem["dve"].n
        K.dve(lambda e, o=neg0, i=s8[:, 0:1]: e.tensor_scalar_mul(o, i, -1.0))
        K.wait("act", "dve")
        K.act(lambda e, o=ex8, i=s8, b=neg0: e.activation(
            out=o, in_=i, func=AF.Exp, bias=b))
        K.wait("dve", "act")
        K.dve(lambda e, o=ssum, i=ex8: e.reduce_sum(o, i, axis=AXX))
        K.wait("dve", "dve")
        K.dve(lambda e, o=srec, i=ssum: e.reciprocal(o, i))
        K.wait("dve", "dve")
        K.dve(lambda e, o=wts, i=ex8, sc=srec: e.tensor_scalar_mul(o, i, sc))
        par = tc % 2
        xc = sb["XCH"][:, par * 2048:(par + 1) * 2048]
        K.wait("sp", "dve")
        K.dma("sp", xc, x_src[tc * 128:(tc + 1) * 128, :])
        nx = K.sem["sdma"].n
        K.wait("dve", "sdma", nx)
        K.dve(lambda e, xc=xc, r=sb["RS4"][:, tc:tc + 1],
              m=sb["RS4"][:, 8 + tc:9 + tc]:
              e.tensor_scalar(sb["XBF"], xc, r, m, ALU.mult, ALU.add))
        nxbf = K.sem["dve"].n
        K.wait("pool", "dve", nid)
        for k8 in range(8):
            gpar = k8 % 2
            de = sb["DEB"][:, gpar * 2048:(gpar + 1) * 2048]
            ue = sb["UEB"][:, gpar * 2048:(gpar + 1) * 2048]
            K.wait("pool", "dve", deb_war[gpar])
            K.pool(lambda e, o=de, ix=sb["EWID"][:, k8:k8 + 1]:
                   e.indirect_dma_start(o, None, t["l1de"][:, :],
                                        IndirectOffsetOnAxis(ap=ix, axis=0)),
                   sem="gdma", v=16)
            K.pool(lambda e, o=ue, ix=sb["EWID"][:, k8:k8 + 1]:
                   e.indirect_dma_start(o, None, t["l1ue"][:, :],
                                        IndirectOffsetOnAxis(ap=ix, axis=0)),
                   sem="gdma", v=16)
            ng = K.sem["gdma"].n
            K.wait("dve", "gdma", ng)
            K.wait("dve", "dve", nxbf)
            K.dve(lambda e, de=de, a=ew8[:, k8:k8 + 1]:
                  e.scalar_tensor_tensor(sb["LNTM"][:, 0:2048], de, 1.0,
                                         sb["XBF"], ALU.bypass, ALU.mult,
                                         accum_out=a))
            K.wait("act", "dve")
            K.act(lambda e, o=TK[:, 272 + k8:273 + k8], i=ew8[:, k8:k8 + 1]:
                  e.activation(out=o, in_=i, func=AF.Silu))
            K.wait("dve", "act")
            K.dve(lambda e, o=TK[:, 280 + k8:281 + k8],
                  a=TK[:, 272 + k8:273 + k8], b=wts[:, k8:k8 + 1]:
                  e.tensor_mul(o, a, b))
            K.wait("dve", "dve")
            if k8 == 0:
                K.dve(lambda e, ue=ue, sc=TK[:, 280:281]:
                      e.tensor_scalar_mul(sb["EACC"][:, 0:2048], ue, sc))
            else:
                K.dve(lambda e, ue=ue, sc=TK[:, 280 + k8:281 + k8]:
                      e.scalar_tensor_tensor(sb["EACC"][:, 0:2048], ue, sc,
                                             sb["EACC"][:, 0:2048], ALU.mult,
                                             ALU.add))
            deb_war[gpar] = K.sem["dve"].n
        nea = K.sem["dve"].n
        K.wait("sp", "dve", nea)
        K.dma("sp", t["e_scr"][tc * 128:(tc + 1) * 128, :], sb["EACC"][:, 0:2048])

    if DEV_STAGE == 4:
        K.wait("sp", "sdma")
        K.dma("sp", t["dbg_e"][:, :], t["e_scr"][:, :])
        K.dma("sp", t["dbg_rt"][:, :], t["rt_all"][:, :])
        return
    hst_war = 0
    wslots = [sb["WT2"][:, 0:2048], sb["WT2"][:, 2048:4096],
              sb["WT"][:, 0:2048], sb["WT"][:, 2048:4096]]
    wslot_last = [K.sem["pe"].n] * 4
    for u in range(2):
        for ic in range(64):
            for gi, wsrc in enumerate((t["l1g_t"], t["l1u_t"])):
                si = (2 * ic + gi) % 4
                wtile = wslots[si]
                K.wait("pool", "pe", wslot_last[si])
                K.dma("pool", wtile, wsrc[ic, :, :])
                K.wait("pe", "gdma")
                bank = gi
                mms = [(wtile[:, dc * 128:(dc + 1) * 128],
                        A4L[:, dc * 1024 + u * 512:dc * 1024 + (u + 1) * 512])
                       for dc in range(NDC)]
                K.mm_group(bank, mms)
                K.wait("dve", "pe", K.bank_rdy[bank])
                dvedst = sb["QST"] if gi == 0 else sb["EB"]
                K.wait("dve", "act")
                K.dve(lambda e, o=dvedst[:, 0:512], p=K.psum[bank]:
                      e.tensor_copy(o, p[:, :]))
                K.bank_war[bank].append(("dve", K.sem["dve"].n))
                wslot_last[si] = K.sem["pe"].n
            K.wait("act", "dve")
            K.act(lambda e: e.activation(out=sb["QT"][:, 0:512],
                                         in_=sb["QST"][:, 0:512], func=AF.Silu))
            K.wait("dve", "act")
            K.wait("dve", "sdma")
            K.dve(lambda e: e.tensor_mul(sb["EM"][:, 0:512], sb["QT"][:, 0:512],
                                         sb["EB"][:, 0:512]))
            nh = K.sem["dve"].n
            K.wait("sp", "dve", nh)
            K.dma("sp", t["h_scr"][ic, :, u * 512:(u + 1) * 512],
                  sb["EM"][:, 0:512])

    if DEV_STAGE == 12:
        K.wait("sp", "sdma")
        K.wait("sp", "gdma")
        K.dma("sp", t["dbg_e"].ap().bitcast(BF16)[0:1024, 0:4096],
              t["h_scr"][0:32, :, :])
        return
    K.wait("sp", "dve")
    K.wait("sp", "act")
    K.wait("pool", "dve")
    K.wait("pool", "act")
    p2_last = [K.sem["pe"].n] * 2
    for u in range(2):
        for dh in range(2):
            for bk in range(8):
                for dep in K.bank_war[bk]:
                    K.wait("pe", *dep)
                K.bank_war[bk] = []
            for ic in range(64):
                wpar = ic % 2
                wtile = sb["WT"][:, wpar * 1024:(wpar + 1) * 1024]
                htile = (sb["QST"] if ic % 2 == 0 else sb["QT"])
                K.wait("pool", "pe", p2_last[wpar])
                K.dma("pool", wtile,
                      t["l1d_t"][ic, :, dh * 1024:(dh + 1) * 1024])
                K.wait("sp", "pe", p2_last[wpar])
                K.dma("sp", htile[:, 0:512],
                      t["h_scr"][ic, :, u * 512:(u + 1) * 512])
                K.wait("pe", "gdma")
                K.wait("pe", "sdma")
                for tc4 in range(4):
                    for dco in range(2):
                        bank = tc4 * 2 + dco
                        l = htile[:, tc4 * 128:(tc4 + 1) * 128]
                        r = wtile[:, dco * 512:(dco + 1) * 512]
                        K.pe(lambda e, b=bank, l=l, r=r, s=(ic == 0),
                             tl=(ic == 63):
                             e.matmul(K.psum[b][:, :], l, r,
                                      start=s, stop=tl),
                             sem="pe" if (tc4 == 3 and dco == 1) else None)
                p2_last[ic % 2] = K.sem["pe"].n
            for tc4 in range(4):
                for dco in range(2):
                    K.bank_rdy[tc4 * 2 + dco] = K.sem["pe"].n
            npeall = K.sem["pe"].n
            K.wait("dve", "pe", npeall)
            K.wait("dve", "act")
            for tc4 in range(4):
                for dco in range(2):
                    bank = tc4 * 2 + dco
                    doff = dh * 1024 + dco * 512
                    dst = a4f[:, tc4 * 2048 + doff:tc4 * 2048 + doff + 512]
                    K.dve(lambda e, o=dst, p=K.psum[bank]:
                          e.tensor_copy(o, p[:, :]))
                    K.bank_war[bank].append(("dve", K.sem["dve"].n))
        nfa = K.sem["act"].n
        for tc4 in range(4):
            tc = u * 4 + tc4
            K.wait("sp", "dve")
            K.dma("sp", sb["EACC"][:, 0:2048],
                  t["e_scr"][tc * 128:(tc + 1) * 128, :])
            nel = K.sem["sdma"].n
            K.wait("dve", "sdma", nel)
            K.wait("dve", "dve")
            fap = a4f[:, tc4 * 2048:(tc4 + 1) * 2048]
            K.dve(lambda e, f=fap: e.tensor_add(sb["FTC"][:, :], f,
                                                sb["EACC"][:, 0:2048]))
            fap = sb["FTC"][:, :]
            if DEV_STAGE == 13:
                nf = K.sem["dve"].n
                K.wait("sp", "dve", nf)
                K.dma("sp", t["dbg_e"][tc * 128:(tc + 1) * 128, :], fap)
                continue
            emit_ln_residual_store(K, st, tc, fap, nfa, x_src, x_dst, final)
        if DEV_STAGE == 13 and u == 1:
            return



def kernel(**inputs):
    in_maps = stage_inputs(inputs)
    nc = build_program()
    res = run_bass_kernel_spmd(nc, in_maps, core_ids=list(range(8)))
    return unstage_output(res.results)


# version 06 (final)
# speedup vs baseline: 1.6407x; 1.2670x over previous
import numpy as np
import ml_dtypes
import concourse.bass as bass
import concourse.mybir as mybir
from concourse.bass_core_rust import IndirectOffsetOnAxis
from concourse.bass_utils import run_bass_kernel_spmd
from contextlib import ExitStack

F32 = mybir.dt.float32
BF16 = mybir.dt.bfloat16
U8 = mybir.dt.uint8
U32 = mybir.dt.uint32
I32 = mybir.dt.int32
AF = mybir.ActivationFunctionType
ALU = mybir.AluOpType
AXX = mybir.AxisListType.X

NH = 16
NTC = 8
NDC = 16
CEXP = 20.0
CINV = 1.0 / float(np.sqrt(128.0))
SLOPES = [1.0 / 2 ** (k / 2.0) for k in range(16)]
EPS = 1e-5
N_SUB = 4
DEV_STAGE = 0
DEV_KEYS = 0


def qbase(parity, slot):
    return 512 * parity if slot == 0 else 512 * (3 - parity)


def global_blocks(parity):
    return (0, 3) if parity == 0 else (1, 2)


def kc_owner(kc):
    g = kc // 4
    if g in (0, 3):
        return 0, (kc if g == 0 else kc - 8)
    return 1, kc - 4


def attn_chunks(layer):
    out = []
    for head in range(NH):
        slope = SLOPES[head]
        for slot in range(2):
            kcmax = 7 if slot == 0 else 15
            for kc in range(kcmax + 1):
                contrib = False
                full_both = True
                for p in (0, 1):
                    dq = qbase(p, slot) - 128 * kc
                    valid = dq >= -384
                    far = (layer == 0) and slope * max(0, dq - 127) > 120.0
                    if valid and not far:
                        contrib = True
                    if dq < 127:
                        full_both = False
                if contrib:
                    out.append((head, slot, kc, not full_both))
    return out



class Sem:
    def __init__(self, nc, name):
        self.h = nc.alloc_semaphore(name=name)
        self.n = 0


class KB:
    ENGS = ("pe", "act", "dve", "pool", "sp")

    def __init__(self, nc):
        self.nc = nc
        self.ops = {e: [] for e in self.ENGS}
        self.sem = {s: Sem(nc, "s_" + s)
                    for s in ("sdma", "gdma", "pe", "act", "dve", "pool", "cc",
                              "w0", "w1", "w2", "w3", "v0", "v1", "kv",
                              "h0", "h1")}
        self.ctx = ExitStack()
        self._sbuf_bytes = 0
        self.psum = [self.ctx.enter_context(
            nc.psum_tensor(f"pb{i}", [128, 512], F32)) for i in range(8)]
        self.bank_war = [[] for _ in range(8)]
        self.bank_rdy = [0] * 8

    def sb(self, name, shape, dtype):
        self._sbuf_bytes += int(np.prod(shape[1:])) * mybir.dt.size(dtype)
        return self.ctx.enter_context(self.nc.sbuf_tensor(name, shape, dtype))

    def _rec(self, eng, fn, sem=None, v=1):
        s = self.sem[sem] if sem else None
        if s is not None:
            s.n += v

            def f(e, fn=fn, s=s, v=v):
                fn(e).then_inc(s.h, v)
        else:
            def f(e, fn=fn):
                fn(e)
        self.ops[eng].append(f)
        return s.n if s is not None else None

    def pe(self, fn, sem=None, v=1):
        return self._rec("pe", fn, sem, v)

    def act(self, fn, sem="act", v=1):
        return self._rec("act", fn, sem, v)

    def dve(self, fn, sem="dve", v=1):
        return self._rec("dve", fn, sem, v)

    def pool(self, fn, sem="pool", v=1):
        return self._rec("pool", fn, sem, v)

    def sp(self, fn, sem=None, v=1):
        return self._rec("sp", fn, sem, v)

    def wait(self, eng, sem, n=None):
        s = self.sem[sem]
        n = s.n if n is None else n
        assert n <= s.n, f"wait on future count {sem} {n} > {s.n}"
        if n <= 0:
            return
        self.ops[eng].append(lambda e, s=s, n=n: e.wait_ge(s.h, n))

    def dma(self, q, out, in_, sem=None, **kw):
        if sem is None:
            sem = "sdma" if q == "sp" else "gdma"
        return self._rec(q, lambda e: e.dma_start(out=out, in_=in_, **kw), sem, 16)

    def mm_group(self, bank, mms, tail=None, out=None):
        for dep in self.bank_war[bank]:
            self.wait("pe", *dep)
        self.bank_war[bank] = []
        seq = list(mms) + list(tail or [])
        n = len(seq)
        pt = out if out is not None else self.psum[bank][:, :]
        for i, (l, r) in enumerate(seq):
            last = i == n - 1
            self.pe(lambda e, l=l, r=r, s=(i == 0), t=last, pt=pt:
                    e.matmul(pt, l, r, start=s, stop=t),
                    sem="pe" if last else None)
        self.bank_rdy[bank] = self.sem["pe"].n
        return self.bank_rdy[bank]

    def finalize(self, out_waits):
        nc, ops, sems = self.nc, self.ops, self.sem
        with nc.Block() as block:

            @block.tensor
            def _(e):
                for f in ops["pe"]:
                    f(e)

            @block.scalar
            def _(e):
                for f in ops["act"]:
                    f(e)

            @block.vector
            def _(e):
                for f in ops["dve"]:
                    f(e)

            @block.gpsimd
            def _(e):
                for f in ops["pool"]:
                    f(e)

            @block.sync
            def _(e):
                for f in ops["sp"]:
                    f(e)
                for s, n in out_waits:
                    e.wait_ge(sems[s].h, n)



def _tile_qk(wqkv):
    out = np.empty((32, 128, 2048), dtype=np.float32)
    w = wqkv.reshape(16, 128, 48, 128)
    for j in range(32):
        out[j] = w[:, :, j, :].transpose(1, 0, 2).reshape(128, 2048)
    return out


def _tile_v(wv):
    w = wv.reshape(16, 128, 4, 512)
    return np.ascontiguousarray(w.transpose(2, 1, 0, 3).reshape(4, 128, 8192))


def _tile_wo(wo):
    w = wo.reshape(16, 128, 4, 512)
    return np.ascontiguousarray(w.transpose(1, 0, 2, 3).reshape(128, 32768))


def _tile_lhs64(w, transpose):
    if transpose:
        v = w.reshape(64, 128, 16, 128)
        return np.ascontiguousarray(v.transpose(0, 3, 2, 1).reshape(64, 128, 2048))
    v = w.reshape(16, 128, 64, 128)
    return np.ascontiguousarray(v.transpose(2, 1, 0, 3).reshape(64, 128, 2048))


def _tile_down(w, transpose):
    if transpose:
        v = w.reshape(2048, 64, 128)
        return np.ascontiguousarray(v.transpose(1, 2, 0))
    return np.ascontiguousarray(w.reshape(64, 128, 2048))


def _int32_to_u8(w):
    return np.ascontiguousarray(w.view(np.uint8).reshape(*w.shape, 4)[..., 0])


def _w01_strip(parity, slot):
    kcmax = 7 if slot == 0 else 15
    k0 = qbase(parity, slot) - 128 * kcmax
    i = np.arange(128)[:, None]
    x = np.arange(1408)[None, :]
    return ((i - x) <= k0).astype(np.uint8)


def _l0_bias_table(parity, chunks):
    cols = []
    i = np.arange(128)
    for (head, slot, kc, _m) in chunks:
        s = SLOPES[head]
        cols.append(s * (128 * kc + i - qbase(parity, slot)) - CEXP)
    return np.stack(cols, axis=1).astype(np.float32)


def _routing_offsets(core):
    b, h = core // 2, core % 2
    offs = []
    for side in range(2):
        for blk in range(2):
            s0 = global_blocks(h)[blk] * 512
            n0 = b * 2048 + s0
            m0 = n0
            prod_batch = (m0 % 4096 * 0) + (m0 // 4096) + (2 if side == 1 else 0)
            sp = (m0 % 4096) // 2
            blk_p = sp // 512
            prod_parity = 1 if blk_p in (1, 2) else 0
            prod_core = 2 * (prod_batch % 2) + prod_parity if False else None
            pb_in_t = m0 // 4096
            prod_core = 2 * (pb_in_t + (2 if side == 1 else 0) * 0) + prod_parity
            prod_core = 2 * (pb_in_t + (2 if side == 1 else 0)) + prod_parity
            if prod_parity == 0:
                tloc = sp if blk_p == 0 else sp - 1024
            else:
                tloc = sp - 512
            offs.append((prod_core * 2048 + 2 * tloc) * 64)
    return np.array(offs + [0] * 4, dtype=np.uint32).reshape(1, 8)


def stage_inputs(inputs):
    f32 = np.float32
    g = {}
    wqkv = np.asarray(inputs["L0_attn_Wqkv"], f32)
    g["l0qk_t"] = _tile_qk(wqkv)
    g["l0v_t"] = _tile_v(np.ascontiguousarray(wqkv[:, 4096:6144]))
    g["l0wo_t"] = _tile_wo(np.asarray(inputs["L0_attn_Wo"], f32))
    bqkv = np.asarray(inputs["L0_attn_bqkv"], f32)
    g["l0bqk"] = np.ascontiguousarray(bqkv[:4096].reshape(32, 128).T)
    g["l0bv"] = np.ascontiguousarray(bqkv[4096:].reshape(1, 2048))
    g["l0bo"] = np.ascontiguousarray(
        np.asarray(inputs["L0_attn_bo"], f32).reshape(1, 2048))
    g["l0g_t"] = _tile_lhs64(_int32_to_u8(np.asarray(inputs["L0_ffn_gate_w"])), True)
    g["l0u_t"] = _tile_lhs64(_int32_to_u8(np.asarray(inputs["L0_ffn_up_w"])), True)
    g["l0d_t"] = _tile_down(_int32_to_u8(np.asarray(inputs["L0_ffn_down_w"])), True)
    g["l0gs"] = np.ascontiguousarray(
        np.asarray(inputs["L0_ffn_gate_scale"], f32).reshape(64, 128).T)
    g["l0us"] = np.ascontiguousarray(
        np.asarray(inputs["L0_ffn_up_scale"], f32).reshape(64, 128).T)
    g["l0ds"] = np.ascontiguousarray(
        np.asarray(inputs["L0_ffn_down_scale"], f32).reshape(1, 2048))

    wq = np.asarray(inputs["L1_attn_W_Q"], f32)
    wk = np.asarray(inputs["L1_attn_W_K"], f32)
    wv = np.asarray(inputs["L1_attn_W_V"], f32)
    g["l1q_t"] = np.ascontiguousarray(
        wq.transpose(0, 2, 1).reshape(16, 16, 128, 128)
        .transpose(0, 2, 1, 3).reshape(16, 128, 2048))
    g["l1k_t"] = np.ascontiguousarray(
        wk.transpose(0, 2, 1).reshape(16, 16, 128, 128)
        .transpose(0, 2, 1, 3).reshape(16, 128, 2048))
    wv2 = np.ascontiguousarray(wv.transpose(2, 0, 1).reshape(2048, 2048))
    g["l1v_t"] = _tile_v(wv2)
    g["l1wo_t"] = _tile_wo(
        np.ascontiguousarray(np.asarray(inputs["L1_attn_W_O"], f32).T))
    g["l1g_t"] = _tile_lhs64(np.asarray(inputs["L1_ffn_gate_w"], f32), False)
    g["l1u_t"] = _tile_lhs64(np.asarray(inputs["L1_ffn_up_w"], f32), False)
    g["l1d_t"] = _tile_down(np.asarray(inputs["L1_ffn_down_w"], f32), False)
    wqr = np.asarray(inputs["L1_ffn_W_q"], f32)
    g["l1wq_t"] = np.ascontiguousarray(
        wqr.reshape(16, 128, 128).transpose(1, 0, 2).reshape(128, 2048))

    g["l1de"] = np.ascontiguousarray(np.asarray(inputs["L1_ffn_down_embed"], f32))
    g["l1ue"] = np.ascontiguousarray(np.asarray(inputs["L1_ffn_up_embed"], f32))

    x = np.asarray(inputs["L0_attn_x"], f32)
    chunks0 = attn_chunks(0)
    in_maps = []
    for c in range(8):
        b, h = c // 2, c % 2
        m = dict(g)
        m["x0"] = np.ascontiguousarray(np.concatenate(
            [x[b, blk * 512:(blk + 1) * 512] for blk in global_blocks(h)], 0))
        m["w01a"] = np.ascontiguousarray(_w01_strip(h, 0))
        m["w01b"] = np.ascontiguousarray(_w01_strip(h, 1))
        m["l0bias"] = _l0_bias_table(h, chunks0)
        m["cidx"] = _routing_offsets(c)
        m["l1keys"] = np.ascontiguousarray(
            np.asarray(inputs["L1_ffn_keys"], np.float32)[c // 4])
        in_maps.append(m)
    return in_maps


def unstage_output(res_list):
    out = np.empty((4, 2048, 2048), dtype=np.float32)
    for c in range(8):
        b, h = c // 2, c % 2
        y = res_list[c]["out"]
        for i, blk in enumerate(global_blocks(h)):
            out[b, blk * 512:(blk + 1) * 512] = y[i * 512:(i + 1) * 512]
    return out



def build_program(n_sub=None):
    if n_sub is None:
        n_sub = N_SUB
    nc = bass.Bass("TRN2", target_bir_lowering=False)
    K = KB(nc)
    t = {}

    def din(name, shape, dtype):
        t[name] = nc.dram_tensor(name, list(shape), dtype, kind="ExternalInput")

    din("x0", (1024, 2048), F32)
    din("l0qk_t", (32, 128, 2048), F32)
    din("l0v_t", (4, 128, 8192), F32)
    din("l0wo_t", (128, 32768), F32)
    din("l0bqk", (128, 32), F32)
    din("l0bv", (1, 2048), F32)
    din("l0bo", (1, 2048), F32)
    din("l0g_t", (64, 128, 2048), U8)
    din("l0u_t", (64, 128, 2048), U8)
    din("l0d_t", (64, 128, 2048), U8)
    din("l0gs", (128, 64), F32)
    din("l0us", (128, 64), F32)
    din("l0ds", (1, 2048), F32)
    din("l1q_t", (16, 128, 2048), F32)
    din("l1k_t", (16, 128, 2048), F32)
    din("l1v_t", (4, 128, 8192), F32)
    din("l1wo_t", (128, 32768), F32)
    din("l1g_t", (64, 128, 2048), F32)
    din("l1u_t", (64, 128, 2048), F32)
    din("l1d_t", (64, 128, 2048), F32)
    din("l1wq_t", (128, 2048), F32)
    din("l1keys", (64, 64), F32)
    din("l1de", (4096, 2048), F32)
    din("l1ue", (4096, 2048), F32)
    din("w01a", (128, 1408), U8)
    din("w01b", (128, 1408), U8)
    din("l0bias", (128, len(attn_chunks(0))), F32)
    din("cidx", (1, 8), U32)
    t["out"] = nc.dram_tensor("out", [1024, 2048], F32, kind="ExternalOutput")

    dbg = "ExternalOutput" if DEV_STAGE == 1 else "Internal"
    if DEV_STAGE in (3, 4, 5, 6, 7, 8, 9, 10, 12, 13):
        t["dbg_rt"] = nc.dram_tensor("dbg_rt", [16384, 64], F32,
                                     kind="ExternalOutput")
        t["dbg_e"] = nc.dram_tensor("dbg_e", [1024, 2048], F32,
                                    kind="ExternalOutput")
    if DEV_STAGE > 0:
        t["dbg_z"] = nc.dram_tensor("dbg_z", [128, 16384], BF16,
                                    kind="ExternalOutput")
        t["dbg_ln"] = nc.dram_tensor("dbg_ln", [128, 16384], BF16,
                                     kind="ExternalOutput")
    t["x_scr"] = nc.dram_tensor("x_scr", [1024, 2048], F32)
    t["h_scr"] = nc.dram_tensor("h_scr", [64, 128, 1024], BF16)
    t["e_scr"] = nc.dram_tensor("e_scr", [1024, 2048], F32)
    t["qdram"] = nc.dram_tensor("qdram", [16, 2, 128, 512], BF16, kind=dbg)
    t["cc_kv"] = nc.dram_tensor("cc_kv", [256, 16384], BF16, kind=dbg)
    t["kv_all"] = nc.dram_tensor("kv_all", [512, 16384], BF16)
    t["cc_rt"] = nc.dram_tensor("cc_rt", [2048, 64], F32)
    t["rt_all"] = nc.dram_tensor("rt_all", [16384, 64], F32)

    sb = {}
    sb["A4L"] = K.sb("A4L", [128, 16384], BF16)
    sb["A8"] = K.sb("A8", [128, 32768], BF16)
    sb["XCH"] = K.sb("XCH", [128, 2 * 2048], F32)
    sb["LNTM"] = K.sb("LNTM", [128, 2048], BF16)
    sb["FTC"] = K.sb("FTC", [128, 2048], F32)
    sb["STAT"] = K.sb("STAT", [128, 64], F32)
    sb["CONSTS"] = K.sb("CONSTS", [128, 2], F32)
    sb["WT"] = K.sb("WT", [128, 4096], BF16)
    sb["WT2"] = K.sb("WT2", [128, 2 * 2048], BF16)
    sb["ZERO"] = K.sb("ZERO", [128, 512], BF16)
    sb["BQK"] = K.sb("BQK", [128, 32], F32)
    sb["ONEC"] = K.sb("ONEC", [128, 1], BF16)
    sb["IDN"] = K.sb("IDN", [128, 128], BF16)
    sb["GS"] = K.sb("GS", [128, 64], F32)
    sb["US"] = K.sb("US", [128, 64], F32)
    sb["KEYS"] = K.sb("KEYS", [128, 128], BF16)
    sb["IOTA8"] = K.sb("IOTA8", [128, 8], F32)
    sb["DEB"] = K.sb("DEB", [128, 2 * 2048], BF16)
    sb["UEB"] = K.sb("UEB", [128, 2 * 2048], BF16)
    sb["CIDX"] = K.sb("CIDX", [1, 8], U32)
    sb["QR"] = K.sb("QR", [64, 2048], BF16)
    rows1 = K.sb("ROWS1", [128, 2176], F32)
    sb["BVROW"] = rows1[0:1, 0:2048]
    sb["ONE_A"] = rows1[0:1, 2048:2176]
    sb["BOROW"] = rows1[32:33, 0:2048]
    sb["ONE_B"] = rows1[32:33, 2048:2176]
    sb["DSROW"] = rows1[64:65, 0:2048]
    sb["ONE_C"] = rows1[64:65, 2048:2176]
    rows2 = K.sb("ROWS2", [128, 2560], F32)
    sb["SLOPE"] = rows2[0:1, 0:2048]
    sb["ALIROW"] = rows2[0:1, 2048:2560]
    sb["REC"] = rows2[32:33, 0:1024]
    sb["DEN"] = rows2[32:33, 1024:2048]
    sb["ONE_D"] = rows2[32:33, 2048:2176]
    sb["RSX"] = rows2[64:65, 0:1024]
    sb["RSH"] = rows2[64:65, 1024:2048]
    sb["ONE_E"] = rows2[64:65, 2048:2560]
    at = K.sb("AT", [128, 10240], U8)
    at_bf, at_f = at.ap().bitcast(BF16), at.ap().bitcast(F32)
    sb["QST"] = at_bf[:, 0:1024]
    sb["QT"] = at_bf[:, 1024:2048]
    sb["EB"] = at_bf[:, 2048:3072]
    sb["EM"] = at_bf[:, 3072:4096]
    sb["RD"] = at_f[:, 2048:2560]
    sb["XBF"] = at_bf[:, 0:2048]
    sb["RT"] = at_f[:, 1024:2048]
    sb["TK"] = at_f[:, 2048:2560]
    aw = K.sb("AW", [128, 8192], U8)
    aw_bf, aw_f = aw.ap().bitcast(BF16), aw.ap().bitcast(F32)
    aw_u = aw.ap().bitcast(U32)
    sb["W01A"] = aw.ap()[:, 0:1408]
    sb["W01B"] = aw.ap()[:, 1408:2816]
    sb["BIAS0"] = aw_f[:, 1408:1408 + len(attn_chunks(0))]
    sb["TKU"] = aw_u[:, 512:576]
    sb["EWID"] = aw_u[:, 576:584]
    sb["EWW"] = aw_f[:, 584:592]
    sb["RS4"] = aw_f[:, 592:608]
    ae = K.sb("AE", [128, 8192], U8)
    sb["DSB"] = ae.ap().bitcast(F32)
    sb["EACC"] = ae.ap().bitcast(F32)

    st = {"sb": sb, "t": t}
    emit_prologue(K, st)
    emit_attention(K, st, layer=0, x_src=t["x0"], x_dst=t["x_scr"],
                   final=(n_sub == 1))
    if n_sub >= 2:
        emit_ffn_l0(K, st, x_src=t["x_scr"], x_dst=t["x_scr"],
                    final=(n_sub == 2))
    if n_sub >= 3:
        emit_attention(K, st, layer=1, x_src=t["x_scr"], x_dst=t["x_scr"],
                       final=(n_sub == 3))
    if n_sub >= 4:
        emit_ffn_l1(K, st, x_src=t["x_scr"], x_dst=t["x_scr"], final=True)

    K.finalize(out_waits=[("sdma", K.sem["sdma"].n), ("gdma", K.sem["gdma"].n)])
    return nc



def emit_prologue(K, st):
    sb, t = st["sb"], st["t"]
    K.dma("sp", sb["W01A"], t["w01a"][:, :])
    K.dma("sp", sb["W01B"], t["w01b"][:, :])
    K.dma("sp", sb["BIAS0"], t["l0bias"][:, :])
    K.dma("sp", sb["BQK"][:, :], t["l0bqk"][:, :])
    K.dma("sp", sb["BVROW"], t["l0bv"][0:1, :])
    K.dma("sp", sb["BOROW"], t["l0bo"][0:1, :])
    K.dma("sp", sb["DSROW"], t["l0ds"][0:1, :])
    K.dma("sp", sb["GS"][:, :], t["l0gs"][:, :])
    K.dma("sp", sb["US"][:, :], t["l0us"][:, :])
    K.dma("sp", sb["CIDX"][:, :], t["cidx"][:, :])
    st["n_pro_sdma"] = K.sem["sdma"].n
    K.dma("pool", sb["KEYS"][0:64, 0:64], t["l1keys"][:, :])
    K.pool(lambda e: e.iota(sb["ALIROW"].bitcast(I32), [[1, 512]], base=0, channel_multiplier=0))
    for nm in ("ONE_A", "ONE_B", "ONE_C", "ONE_D", "ONE_E"):
        K.pool(lambda e, nm=nm: e.memset(sb[nm], 1.0))
    K.pool(lambda e: e.memset(sb["CONSTS"][:, 0:1], EPS))
    K.pool(lambda e: e.memset(sb["CONSTS"][:, 1:2], -CEXP))
    K.pool(lambda e: e.memset(sb["ONEC"][:, :], 1.0))
    K.pool(lambda e: e.memset(sb["ZERO"][:, :], 0.0))
    K.pool(lambda e: e.memset(sb["IDN"][:, :], 1.0))
    for h in range(16):
        K.pool(lambda e, h=h: e.memset(
            sb["SLOPE"][0:1, h * 128:(h + 1) * 128], SLOPES[h]))
    K.pool(lambda e: e.iota(sb["IOTA8"][:, :].bitcast(I32), [[1, 8]], base=0, channel_multiplier=0))
    K.wait("pool", "pool")
    K.pool(lambda e: e.affine_select(sb["LNTM"][:, 0:128], sb["IDN"][:, :],
                                     [[-1, 128]], ALU.is_ge, 0.0,
                                     base=0, channel_multiplier=1))
    K.wait("pool", "pool")
    K.pool(lambda e: e.affine_select(sb["IDN"][:, :], sb["LNTM"][:, 0:128],
                                     [[1, 128]], ALU.is_ge, 0.0,
                                     base=0, channel_multiplier=-1))
    npool = K.sem["pool"].n
    K.wait("dve", "pool", npool)
    K.dve(lambda e: e.tensor_scalar_mul(sb["ALIROW"],
                                        sb["ALIROW"].bitcast(I32), -1.0 / CINV))
    K.dve(lambda e: e.tensor_copy(sb["IOTA8"][:, :],
                                  sb["IOTA8"][:, :].bitcast(I32)))
    st["n_pro_dve"] = K.sem["dve"].n
    st["n_pro_pool"] = npool



def emit_ln_chunk(K, st, tc, src_ap, out_ap, save_stats=None):
    sb = st["sb"]
    S = sb["STAT"]
    ssum, sqs = S[:, tc:tc + 1], S[:, 8 + tc:9 + tc]
    mean, var = S[:, 16 + tc:17 + tc], S[:, 24 + tc:25 + tc]
    sd, rr = S[:, 32 + tc:33 + tc], S[:, 40 + tc:41 + tc]
    nmr, msq = S[:, 48 + tc:49 + tc], S[:, 56 + tc:57 + tc]
    K.dve(lambda e: e.reduce_sum(ssum, src_ap, axis=AXX))
    K.act(lambda e: e.activation(out=sb["LNTM"][:, 0:2048], in_=src_ap,
                                 func=AF.Square, accum_out=sqs))
    K.wait("dve", "dve")
    K.dve(lambda e: e.tensor_scalar_mul(mean, ssum, 1.0 / 2048.0))
    K.wait("dve", "dve")
    K.dve(lambda e: e.tensor_mul(msq, mean, mean))
    K.wait("dve", "act")
    K.wait("dve", "dve")
    K.dve(lambda e: e.scalar_tensor_tensor(var, sqs, 1.0 / 2048.0, msq,
                                           ALU.mult, ALU.subtract))
    K.wait("act", "dve")
    K.act(lambda e: e.activation(out=sd, in_=var, func=AF.Sqrt,
                                 bias=sb["CONSTS"][:, 0:1]))
    K.wait("dve", "act")
    K.dve(lambda e: e.reciprocal(rr, sd))
    K.wait("dve", "dve")
    K.dve(lambda e: e.tensor_scalar(nmr, mean, rr, -1.0, ALU.mult, ALU.mult))
    K.wait("dve", "dve")
    K.dve(lambda e: e.tensor_scalar(out_ap, src_ap, rr, nmr, ALU.mult, ALU.add))
    if save_stats is not None:
        name, off = save_stats
        K.wait("dve", "dve")
        K.dve(lambda e: e.tensor_copy(sb[name][:, off + tc:off + tc + 1], rr))
        K.dve(lambda e: e.tensor_copy(sb[name][:, off + 8 + tc:off + 9 + tc], nmr))
    return K.sem["dve"].n


def emit_ln_transpose(K, st, x_src, save_stats=None):
    sb = st["sb"]
    K.wait("act", "pool", st["n_pro_pool"])
    K.wait("pe", "pool", st["n_pro_pool"])
    K.wait("dve", "pool", st["n_pro_pool"])
    for tc in range(NTC):
        par = tc % 2
        xc = sb["XCH"][:, par * 2048:(par + 1) * 2048]
        lc = sb["LNTM"][:, 0:2048]
        K.wait("sp", "dve")
        K.wait("sp", "pe")
        K.dma("sp", xc, x_src[tc * 128:(tc + 1) * 128, :])
        K.wait("dve", "sdma")
        K.wait("act", "sdma")
        K.wait("dve", "pe")
        ndve = emit_ln_chunk(K, st, tc, xc, lc, save_stats)
        K.wait("pe", "dve", ndve)
        for half in range(2):
            bank = 6 + half
            for dep in K.bank_war[bank]:
                K.wait("pe", *dep)
            K.bank_war[bank] = []
            pb = K.psum[bank].ap().bitcast(BF16)
            for k in range(8):
                dc = half * 8 + k
                src = sb["LNTM"][:, dc * 128:(dc + 1) * 128]
                K.pe(lambda e, o=pb[:, k * 128:(k + 1) * 128], s=src:
                     e.transpose(o, s, sb["IDN"][:, :]),
                     sem="pe" if k == 7 else None)
            npe = K.sem["pe"].n
            K.wait("dve", "pe", npe)
            dst = bass.AP(sb["A4L"], (half * 8) * 1024 + tc * 128,
                          [[16384, 128], [1024, 8], [1, 128]])
            K.dve(lambda e, d=dst, s=pb: e.tensor_copy(d, s))
            K.bank_war[bank].append(("dve", K.sem["dve"].n))



def emit_attention(K, st, layer, x_src, x_dst, final):
    sb, t = st["sb"], st["t"]
    A4L, A8 = sb["A4L"], sb["A8"]

    emit_ln_transpose(K, st, x_src)
    nln = K.sem["dve"].n
    K.wait("pe", "dve", nln)

    wt_last = [0, 0]
    qst_war = [0, 0]
    for j in range(32):
        par = j % 2
        wtile = sb["WT2"][:, par * 2048:(par + 1) * 2048]
        K.wait("pool", "pe", wt_last[par])
        wsem = "w0" if par == 0 else "w1"
        if layer == 0:
            K.dma("pool", wtile, t["l0qk_t"][j, :, :], sem=wsem)
        else:
            src = t["l1q_t"] if j < 16 else t["l1k_t"]
            K.dma("pool", wtile, src[j % 16, :, :], sem=wsem)
        K.wait("pe", wsem)
        for u in range(2):
            bank = (2 * j + u) % 4
            mms = [(wtile[:, dc * 128:(dc + 1) * 128],
                    A4L[:, dc * 1024 + u * 512:dc * 1024 + (u + 1) * 512])
                   for dc in range(NDC)]
            K.mm_group(bank, mms)
            K.wait("dve", "pe", K.bank_rdy[bank])
            if j < 16:
                par2 = (2 * j + u) % 2
                stg = sb["QST"][:, par2 * 512:(par2 + 1) * 512]
                K.wait("dve", "sdma", qst_war[par2])
                if layer == 0:
                    K.dve(lambda e, o=stg, p=K.psum[bank], bb=sb["BQK"][:, j:j + 1]:
                          e.tensor_scalar_add(o, p[:, :], bb))
                else:
                    K.dve(lambda e, o=stg, p=K.psum[bank]:
                          e.tensor_copy(o, p[:, :]))
                K.bank_war[bank].append(("dve", K.sem["dve"].n))
                K.wait("sp", "dve")
                K.dma("sp", t["qdram"][j, u, :, :], stg)
                qst_war[par2] = K.sem["sdma"].n
            else:
                jc = j - 16
                dst = bass.AP(A8, (4 * u) * 2048 + jc * 128,
                              [[32768, 128], [2048, 4], [1, 128]])
                if layer == 0:
                    K.dve(lambda e, o=dst, p=K.psum[bank], bb=sb["BQK"][:, j:j + 1]:
                          e.tensor_scalar_add(o, p[:, :], bb))
                else:
                    K.dve(lambda e, o=dst, p=K.psum[bank]:
                          e.tensor_copy(o, p[:, :]))
                K.bank_war[bank].append(("dve", K.sem["dve"].n))
        wt_last[par] = K.sem["pe"].n

    vsrc = t["l0v_t"] if layer == 0 else t["l1v_t"]
    v_last = [K.sem["pe"].n, K.sem["pe"].n]
    for jv in range(4):
        if layer == 0:
            K.wait("pe", "sdma", st["n_pro_sdma"])
        for half in range(2):
            vslot = (jv * 2 + half) % 2
            vbuf = sb["WT"] if vslot == 0 else sb["WT2"][:, 0:4096]
            K.wait("pool", "pe", v_last[vslot])
            vsem = "v0" if vslot == 0 else "v1"
            K.dma("pool", vbuf[:, :],
                  vsrc[jv, :, half * 4096:(half + 1) * 4096], sem=vsem)
            K.wait("pe", vsem)
            for tc in range(NTC):
                bank = tc
                if half == 0:
                    for dep in K.bank_war[bank]:
                        K.wait("pe", *dep)
                    K.bank_war[bank] = []
                pt = K.psum[bank][:, :]
                for dci in range(8):
                    dc = half * 8 + dci
                    l = A4L[:, dc * 1024 + tc * 128:dc * 1024 + (tc + 1) * 128]
                    r = vbuf[:, dci * 512:(dci + 1) * 512]
                    lastmm = (half == 1 and dci == 7 and layer != 0)
                    K.pe(lambda e, l=l, r=r, pt=pt,
                         st_=(half == 0 and dci == 0), tl=lastmm:
                         e.matmul(pt, l, r, start=st_, stop=tl),
                         sem="pe" if dci == 7 else None)
                if half == 1 and layer == 0:
                    K.pe(lambda e, pt=pt,
                         r=sb["BVROW"][0:1, jv * 512:(jv + 1) * 512]:
                         e.matmul(pt, sb["ONE_A"], r,
                                  start=False, stop=True), sem="pe")
                if half == 1:
                    K.bank_rdy[bank] = K.sem["pe"].n
                    dst = A8[:, 16384 + tc * 2048 + jv * 512:
                             16384 + tc * 2048 + (jv + 1) * 512]
                    K.wait("act", "pe", K.bank_rdy[bank])
                    K.act(lambda e, d=dst, p=K.psum[bank]:
                          e.activation(out=d, in_=p[:, :], func=AF.Copy))
                    K.bank_war[bank].append(("act", K.sem["act"].n))
            v_last[vslot] = K.sem["pe"].n

    K.wait("sp", "act")
    K.wait("sp", "dve")
    K.dma("sp", t["cc_kv"][0:128, :], A8[:, 0:16384])
    K.dma("sp", t["cc_kv"][128:256, :], A8[:, 16384:32768])
    nsd = K.sem["sdma"].n
    if DEV_STAGE == 1:
        K.wait("sp", "dve")
        K.dma("sp", t["dbg_ln"][:, :], A4L[:, :])
        return
    K.wait("pool", "sdma", nsd)
    K.pool(lambda e: e.collective_compute(
        "AllGather", ALU.bypass,
        replica_groups=[[0, 1], [2, 3], [4, 5], [6, 7]],
        ins=[t["cc_kv"].ap()],
        outs=[t["kv_all"].ap()]), sem="cc")
    ncc = K.sem["cc"].n

    chunks = attn_chunks(layer)
    zslots = {}
    for ch in chunks:
        zslots.setdefault((ch[0], ch[1]), []).append(ch)
    bias_col = {ch: i for i, ch in enumerate(attn_chunks(0))}
    ebuf_war = [0, 0]

    for hg in range(2):
        K.wait("sp", "cc", ncc)
        K.wait("sp", "pe")
        for kc in range(16):
            r, lt = kc_owner(kc)
            base = lt * 2048 + hg * 1024
            K.dma("sp", A8[:, kc * 1024:(kc + 1) * 1024],
                  t["kv_all"][r * 256:r * 256 + 128, base:base + 1024],
                  sem="kv")
            K.dma("sp", A8[:, 16384 + kc * 1024:16384 + (kc + 1) * 1024],
                  t["kv_all"][r * 256 + 128:r * 256 + 256, base:base + 1024],
                  sem="kv")
        K.wait("pe", "kv")

        for head in range(hg * 8, hg * 8 + 8):
            for slot in range(2):
                lst = zslots.get((head, slot))
                if not lst:
                    continue
                hpar = (head * 2 + slot) % 2
                zbank, dbank, rbank = 2 + hpar, 0 + hpar, 4 + hpar
                qt = sb["QT"][:, hpar * 512:(hpar + 1) * 512]
                K.wait("sp", "pe")
                K.dma("sp", qt, t["qdram"][head, slot, :, :])
                K.wait("pe", "sdma")
                for bk in (zbank, dbank):
                    for dep in K.bank_war[bk]:
                        K.wait("pe", *dep)
                    K.bank_war[bk] = []
                first = True
                nch = len(lst)

                def emit_scores(ch_i, ch):
                    _, _, kc, masked = ch
                    spar = ch_i % 2
                    sbank = 6 + spar
                    for dep in K.bank_war[sbank]:
                        K.wait("pe", *dep)
                    K.bank_war[sbank] = []
                    koff = kc * 1024 + (head - hg * 8) * 128
                    ktile = A8[:, koff:koff + 128]
                    if layer == 0:
                        K.pe(lambda e, b=sbank, l=ktile, r=qt:
                             e.matmul(K.psum[b][:, :], l, r,
                                      start=True, stop=False))
                        K.pe(lambda e, b=sbank,
                             l=sb["SLOPE"][0:1, head * 128:(head + 1) * 128]:
                             e.matmul(K.psum[b][:, :], l, sb["ALIROW"],
                                      start=False, stop=True), sem="pe")
                    else:
                        K.pe(lambda e, b=sbank, l=ktile, r=qt:
                             e.matmul(K.psum[b][:, :], l, r,
                                      start=True, stop=True), sem="pe")
                    nsc = K.sem["pe"].n
                    eb = sb["EB"][:, spar * 512:(spar + 1) * 512]
                    K.wait("act", "pe", nsc)
                    K.wait("act", "pe", ebuf_war[spar])
                    K.wait("act", "dve")
                    if layer == 0:
                        bias = sb["BIAS0"][:, bias_col[ch]:bias_col[ch] + 1]
                        K.act(lambda e, o=eb, p=K.psum[sbank], bb=bias:
                              e.activation(out=o, in_=p[:, :], func=AF.Exp,
                                           bias=bb, scale=CINV))
                    else:
                        K.act(lambda e, o=eb, p=K.psum[sbank]:
                              e.activation(out=o, in_=p[:, :], func=AF.Exp,
                                           bias=sb["CONSTS"][:, 1:2],
                                           scale=CINV))
                    K.bank_war[sbank].append(("act", K.sem["act"].n))
                    if masked:
                        em = sb["EM"][:, spar * 512:(spar + 1) * 512]
                        strip = sb["W01A"] if slot == 0 else sb["W01B"]
                        kcm = 7 if slot == 0 else 15
                        off = (kcm - kc) * 128
                        K.wait("dve", "act")
                        K.wait("dve", "pe", ebuf_war[spar])
                        if layer == 0:
                            K.dve(lambda e, o=em, m=strip[:, off:off + 512], a=eb:
                                  e.select(o, m, a, sb["ZERO"][:, :]))
                        else:
                            K.dve(lambda e, o=em, m=strip[:, off:off + 512], a=eb:
                                  e.tensor_mul(o, a, m))
                        K.bank_war[sbank].append(("dve", K.sem["dve"].n))
                        return em, ("dve", K.sem["dve"].n)
                    return eb, ("act", K.sem["act"].n)

                def emit_av(ch_i, etile, edep):
                    spar = ch_i % 2
                    K.wait("pe", edep[0], edep[1])
                    K.pe(lambda e, b=dbank, r=etile, s=(ch_i == 0),
                         tl=(ch_i == nch - 1):
                         e.matmul(K.psum[b][0:1, :], sb["ONEC"][:, 0:1], r,
                                  start=s, stop=tl),
                         sem="pe" if ch_i == nch - 1 else None)
                    koff2 = lst[ch_i][2] * 1024 + (head - hg * 8) * 128
                    vtile = A8[:, 16384 + koff2:16384 + koff2 + 128]
                    K.pe(lambda e, b=zbank, l=vtile, r=etile, s=(ch_i == 0),
                         tl=(ch_i == nch - 1):
                         e.matmul(K.psum[b][:, :], l, r, start=s, stop=tl),
                         sem="pe" if ch_i == nch - 1 else None)
                    ebuf_war[spar] = K.sem["pe"].n

                pend = None
                for ch_i, ch in enumerate(lst):
                    et, edep = emit_scores(ch_i, ch)
                    if pend is not None:
                        emit_av(pend[0], pend[1], pend[2])
                    pend = (ch_i, et, edep)
                emit_av(pend[0], pend[1], pend[2])
                nz = K.sem["pe"].n
                den = sb["DEN"][0:1, hpar * 512:(hpar + 1) * 512]
                rec = sb["REC"][0:1, hpar * 512:(hpar + 1) * 512]
                rdb = sb["RD"][:, 0:512]
                K.wait("dve", "pe", nz)
                K.dve(lambda e, o=den, p=K.psum[dbank]:
                      e.tensor_copy(o, p[0:1, :]))
                K.bank_war[dbank].append(("dve", K.sem["dve"].n))
                K.wait("dve", "dve")
                K.dve(lambda e, o=rec, i=den: e.reciprocal(o, i))
                nrec = K.sem["dve"].n
                for dep in K.bank_war[rbank]:
                    K.wait("pe", *dep)
                K.bank_war[rbank] = []
                K.wait("pe", "dve", nrec)
                K.pe(lambda e, b=rbank, r=rec:
                     e.matmul(K.psum[b][:, :], sb["ONE_D"], r,
                              start=True, stop=True), sem="pe")
                nrb = K.sem["pe"].n
                K.wait("act", "pe", nrb)
                K.wait("act", "dve")
                K.act(lambda e, o=rdb, p=K.psum[rbank]:
                      e.activation(out=o, in_=p[:, :], func=AF.Copy))
                K.bank_war[rbank].append(("act", K.sem["act"].n))
                zdst = A4L[:, head * 1024 + slot * 512:
                           head * 1024 + (slot + 1) * 512]
                K.wait("dve", "act")
                K.dve(lambda e, o=zdst, p=K.psum[zbank], r=rdb:
                      e.tensor_mul(o, p[:, :], r))
                K.bank_war[zbank].append(("dve", K.sem["dve"].n))

    if DEV_STAGE == 2:
        K.wait("sp", "dve")
        K.dma("sp", t["dbg_z"][:, :], A4L[:, :])
        K.wait("sp", "dve")
        K.dma("sp", t["dbg_ln"][:, :], A8[:, 0:16384])
        return
    K.wait("pool", "pe")
    wosrc = t["l0wo_t"] if layer == 0 else t["l1wo_t"]
    K.dma("pool", A8[:, :], wosrc[:, :], sem="kv")
    K.wait("pe", "kv")
    K.wait("pe", "dve")
    emit_proj_ln_residual(
        K, st, x_src=x_src, x_dst=x_dst, final=final,
        lhs_of=lambda k, tc: A4L[:, k * 1024 + (tc // 4) * 512
                                 + (tc % 4) * 128:k * 1024 + (tc // 4) * 512
                                 + (tc % 4) * 128 + 128],
        rhs_of=lambda k, dc: A8[:, k * 2048 + dc * 512:k * 2048 + (dc + 1) * 512],
        nk=16, bias_row=("BOROW", 0) if layer == 0 else None,
        scale_bcast=False, extra_add=None)


def emit_proj_ln_residual(K, st, x_src, x_dst, final, lhs_of, rhs_of,
                          nk, bias_row, scale_bcast, extra_add):
    sb, t = st["sb"], st["t"]
    prev_res = 0
    for tc in range(NTC):
        for dc in range(4):
            bank = (tc * 4 + dc) % 2
            mms = [(lhs_of(k, tc), rhs_of(k, dc)) for k in range(nk)]
            tail = None
            if bias_row is not None:
                name, off = bias_row
                tail = [(sb["ONE_B"],
                         sb[name][0:1, off + dc * 512:off + (dc + 1) * 512])]
            K.mm_group(bank, mms, tail=tail)
            dst = sb["FTC"][:, dc * 512:(dc + 1) * 512]
            if scale_bcast:
                K.wait("dve", "pe", K.bank_rdy[bank])
                K.wait("dve", "dve", prev_res)
                K.dve(lambda e, o=dst, p=K.psum[bank],
                      s=sb["DSB"][:, dc * 512:(dc + 1) * 512]:
                      e.tensor_mul(o, p[:, :], s))
                K.bank_war[bank].append(("dve", K.sem["dve"].n))
            else:
                K.wait("act", "pe", K.bank_rdy[bank])
                K.wait("act", "dve", prev_res)
                K.act(lambda e, o=dst, p=K.psum[bank]:
                      e.activation(out=o, in_=p[:, :], func=AF.Copy))
                K.bank_war[bank].append(("act", K.sem["act"].n))
        nfa = K.sem["act"].n
        if extra_add is not None:
            K.wait("dve", "act", nfa)
            K.wait("dve", "dve")
            K.wait("dve", "pool")
            K.dve(lambda e, a=extra_add(tc):
                  e.tensor_add(sb["FTC"][:, :], sb["FTC"][:, :], a))
        emit_ln_residual_store(K, st, tc, sb["FTC"][:, :], nfa,
                               x_src, x_dst, final)
        prev_res = K.sem["dve"].n



def emit_ln_residual_store(K, st, tc, F_ap, nfa, x_src, x_dst, final):
    sb, t = st["sb"], st["t"]
    par = tc % 2
    xc = sb["XCH"][:, par * 2048:(par + 1) * 2048]
    K.wait("sp", "dve")
    K.dma("sp", xc, x_src[tc * 128:(tc + 1) * 128, :])
    nx = K.sem["sdma"].n
    K.wait("dve", "act", nfa)
    K.wait("dve", "dve")
    K.wait("act", "act", nfa)
    K.wait("act", "dve", K.sem["dve"].n)
    emit_ln_chunk(K, st, tc, F_ap, F_ap)
    K.wait("dve", "sdma", nx)
    K.wait("dve", "dve")
    K.dve(lambda e, xc=xc, f=F_ap: e.tensor_add(xc, xc, f))
    nxn = K.sem["dve"].n
    K.wait("sp", "dve", nxn)
    dst = t["out"] if final else x_dst
    K.dma("sp", dst[tc * 128:(tc + 1) * 128, :], xc)


def emit_ffn_l0(K, st, x_src, x_dst, final):
    sb, t = st["sb"], st["t"]
    A4L, A8 = sb["A4L"], sb["A8"]
    a4f = A4L.ap().bitcast(F32)

    emit_ln_transpose(K, st, x_src)
    nln = K.sem["dve"].n
    K.wait("pe", "dve", nln)

    K.wait("pe", "sdma", st["n_pro_sdma"])
    for dc in range(4):
        bank = dc % 2
        K.mm_group(bank, [(sb["ONE_C"], sb["DSROW"][0:1, dc * 512:(dc + 1) * 512])])
        K.wait("act", "pe", K.bank_rdy[bank])
        K.act(lambda e, o=sb["DSB"][:, dc * 512:(dc + 1) * 512], p=K.psum[bank]:
              e.activation(out=o, in_=p[:, :], func=AF.Copy))
        K.bank_war[bank].append(("act", K.sem["act"].n))
    for u in range(2):
        bank = 4 + u
        mms = [(sb["ONEC"][:, 0:1],
                A4L[:, dc * 1024 + u * 512:dc * 1024 + (u + 1) * 512])
               for dc in range(NDC)]
        K.mm_group(bank, mms, out=K.psum[bank][0:1, :])
        K.wait("dve", "pe", K.bank_rdy[bank])
        K.dve(lambda e, o=sb["RSX"][0:1, u * 512:(u + 1) * 512], p=K.psum[bank]:
              e.tensor_scalar_mul(o, p[0:1, :], -128.0))
        K.bank_war[bank].append(("dve", K.sem["dve"].n))
    nrsx = K.sem["dve"].n
    K.wait("pe", "dve", nrsx)

    ga = sb["QT"]
    sgt = sb["QST"]
    ua = sb["EB"]
    hst = sb["EM"]
    hst_war = [0, 0]
    hst_sd = [0, 0]
    wslots = [sb["WT2"][:, 0:2048], sb["WT2"][:, 2048:4096],
              sb["WT"][:, 0:2048], sb["WT"][:, 2048:4096]]
    wslot_last = [K.sem["pe"].n] * 4
    for u in range(2):
        hbank = 2 + u
        for dep in K.bank_war[hbank]:
            K.wait("pe", *dep)
        K.bank_war[hbank] = []
        for ic in range(64):
            for gi, (wsrc, scl) in enumerate(
                    ((t["l0g_t"], sb["GS"]), (t["l0u_t"], sb["US"]))):
                si = (2 * ic + gi) % 4
                wtile = wslots[si]
                K.wait("pool", "pe", wslot_last[si])
                K.dma("pool", wtile, wsrc[ic, :, :], sem="w%d" % si)
                K.wait("pe", "w%d" % si)
                bank = gi
                mms = [(wtile[:, dc * 128:(dc + 1) * 128],
                        A4L[:, dc * 1024 + u * 512:dc * 1024 + (u + 1) * 512])
                       for dc in range(NDC)]
                tail = [(sb["ONE_E"][0:1, 0:128],
                         sb["RSX"][0:1, u * 512:(u + 1) * 512])]
                K.mm_group(bank, mms, tail=tail)
                K.wait("dve", "pe", K.bank_rdy[bank])
                dvedst = sgt if gi == 0 else ua
                K.wait("dve", "act")
                K.dve(lambda e, o=dvedst[:, 0:512], p=K.psum[bank],
                      s=scl[:, ic:ic + 1]:
                      e.tensor_scalar_mul(o, p[:, :], s))
                K.bank_war[bank].append(("dve", K.sem["dve"].n))
                wslot_last[si] = K.sem["pe"].n
            hp = ic % 2
            hsl = hst[:, hp * 512:(hp + 1) * 512]
            K.wait("act", "dve")
            K.act(lambda e: e.activation(out=ga[:, 0:512], in_=sgt[:, 0:512],
                                         func=AF.Silu))
            K.wait("dve", "act")
            K.wait("dve", "pe", hst_war[hp])
            K.wait("dve", "sdma", hst_sd[hp])
            K.dve(lambda e, o=hsl: e.tensor_mul(o, ga[:, 0:512], ua[:, 0:512]))
            nh = K.sem["dve"].n
            K.wait("pe", "dve", nh)
            K.pe(lambda e, b=hbank, s=(ic == 0), tl=(ic == 63), r=hsl:
                 e.matmul(K.psum[b][0:1, :], sb["ONEC"][:, 0:1], r,
                          start=s, stop=tl), sem="pe")
            hst_war[hp] = K.sem["pe"].n
            K.wait("sp", "dve", nh)
            K.dma("sp", t["h_scr"][ic, :, u * 512:(u + 1) * 512], hsl)
            hst_sd[hp] = K.sem["sdma"].n
        K.wait("dve", "pe", K.sem["pe"].n)
        K.dve(lambda e, o=sb["RSH"][0:1, u * 512:(u + 1) * 512], p=K.psum[hbank]:
              e.tensor_scalar_mul(o, p[0:1, :], -128.0))
        K.bank_war[hbank].append(("dve", K.sem["dve"].n))
    nrsh = K.sem["dve"].n
    K.wait("pe", "dve", nrsh)

    K.wait("sp", "dve")
    K.wait("sp", "act")
    K.wait("pool", "dve")
    K.wait("pool", "act")
    p2_last = [K.sem["pe"].n] * 2
    for u in range(2):
        for dh in range(2):
            for bk in range(8):
                for dep in K.bank_war[bk]:
                    K.wait("pe", *dep)
                K.bank_war[bk] = []
            for ic in range(64):
                wpar = ic % 2
                wtile = sb["WT"][:, wpar * 1024:(wpar + 1) * 1024]
                hpar = ic % 2
                htile = (sb["QST"] if hpar == 0 else sb["QT"])
                K.wait("pool", "pe", p2_last[wpar])
                K.dma("pool", wtile,
                      t["l0d_t"][ic, :, dh * 1024:(dh + 1) * 1024],
                      sem="w%d" % wpar)
                K.wait("sp", "pe", p2_last[hpar])
                K.dma("sp", htile[:, 0:512],
                      t["h_scr"][ic, :, u * 512:(u + 1) * 512],
                      sem="h%d" % hpar)
                K.wait("pe", "w%d" % wpar)
                K.wait("pe", "h%d" % hpar)
                for tc4 in range(4):
                    for dco in range(2):
                        bank = tc4 * 2 + dco
                        l = htile[:, tc4 * 128:(tc4 + 1) * 128]
                        r = wtile[:, dco * 512:(dco + 1) * 512]
                        K.pe(lambda e, b=bank, l=l, r=r, s=(ic == 0):
                             e.matmul(K.psum[b][:, :], l, r,
                                      start=s, stop=False),
                             sem="pe" if (tc4 == 3 and dco == 1) else None)
                p2_last[ic % 2] = K.sem["pe"].n
            for tc4 in range(4):
                for dco in range(2):
                    bank = tc4 * 2 + dco
                    l = sb["RSH"][0:1, u * 512 + tc4 * 128:
                                  u * 512 + (tc4 + 1) * 128]
                    K.pe(lambda e, b=bank, l=l:
                         e.matmul(K.psum[b][:, :], l, sb["ONE_E"],
                                  start=False, stop=True), sem="pe")
                    K.bank_rdy[bank] = K.sem["pe"].n
            npeall = K.sem["pe"].n
            K.wait("dve", "pe", npeall)
            K.wait("dve", "act")
            for tc4 in range(4):
                for dco in range(2):
                    bank = tc4 * 2 + dco
                    doff = dh * 1024 + dco * 512
                    dst = a4f[:, tc4 * 2048 + doff:tc4 * 2048 + doff + 512]
                    K.dve(lambda e, o=dst, p=K.psum[bank],
                          s=sb["DSB"][:, doff:doff + 512]:
                          e.tensor_mul(o, p[:, :], s))
                    K.bank_war[bank].append(("dve", K.sem["dve"].n))
        nfa = K.sem["act"].n
        for tc4 in range(4):
            tc = u * 4 + tc4
            emit_ln_residual_store(K, st, tc,
                                   a4f[:, tc4 * 2048:(tc4 + 1) * 2048],
                                   nfa, x_src, x_dst, final)


def emit_ffn_l1(K, st, x_src, x_dst, final):
    sb, t = st["sb"], st["t"]
    nc = K.nc
    A4L = sb["A4L"]
    a4f = A4L.ap().bitcast(F32)

    emit_ln_transpose(K, st, x_src, save_stats=("RS4", 0))
    nln = K.sem["dve"].n
    K.wait("pe", "dve", nln)
    if DEV_STAGE == 7:
        K.wait("sp", "dve")
        K.dma("sp", t["dbg_rt"][0:128, 0:16], sb["RS4"])
        K.dma("sp", t["dbg_e"][0:128, 0:512],
              sb["A4L"][:, 0:1024].bitcast(F32))
        return

    wq = sb["WT2"][:, 0:2048]
    K.wait("pool", "pe")
    K.dma("pool", wq, t["l1wq_t"][:, :])
    K.wait("pe", "gdma")
    for u in range(2):
        for half in range(2):
            bank = u
            mms = [(wq[:, dc * 128 + half * 64:dc * 128 + half * 64 + 64],
                    A4L[:, dc * 1024 + u * 512:dc * 1024 + (u + 1) * 512])
                   for dc in range(NDC)]
            K.mm_group(bank, mms, out=K.psum[bank][0:64, :])
            K.wait("dve", "pe", K.bank_rdy[bank])
            K.dve(lambda e, o=sb["QR"][0:64, half * 1024 + u * 512:
                                       half * 1024 + (u + 1) * 512],
                  p=K.psum[bank]:
                  e.tensor_copy(o, p[0:64, :]))
            K.bank_war[bank].append(("dve", K.sem["dve"].n))
    nqr = K.sem["dve"].n
    if DEV_STAGE == 8:
        K.wait("sp", "dve", nqr)
        K.dma("sp", t["dbg_e"][0:128, 0:512], sb["QR"][:, :].bitcast(F32))
        return
    K.wait("pe", "dve", nqr)
    K.wait("pe", "gdma")
    for tc in range(NTC):
        bank = 2 + (tc % 2)
        for dep in K.bank_war[bank]:
            K.wait("pe", *dep)
        K.bank_war[bank] = []
        u, w = tc // 4, tc % 4
        for half in range(2):
            l = sb["QR"][0:64, half * 1024 + tc * 128:
                         half * 1024 + (tc + 1) * 128]
            r = sb["KEYS"][0:64, 0:64]
            K.pe(lambda e, b=bank, l=l, r=r, h=half:
                 e.matmul(K.psum[b][:, h * 64:(h + 1) * 64], l, r,
                          start=True, stop=True), sem="pe")
        nsc = K.sem["pe"].n
        if DEV_STAGE == 10:
            K.bank_war[bank].append(("pe", nsc))
            continue
        K.wait("dve", "pe", nsc)
        K.wait("dve", "sdma")
        K.dve(lambda e, o=sb["RD"][:, 0:128], p=K.psum[bank]:
              e.tensor_copy(o, p[:, 0:128]))
        K.bank_war[bank].append(("dve", K.sem["dve"].n))
        if DEV_STAGE == 9:
            continue
        K.wait("sp", "dve")
        K.dma("sp", bass.AP(t["cc_rt"], tc * 16384, [[128, 128], [1, 128]]),
              sb["RD"][:, 0:128])
    nrt = K.sem["sdma"].n
    if DEV_STAGE == 10:
        K.wait("sp", "pe")
        K.dma("sp", t["dbg_e"][0:128, 0:512], sb["QR"][:, :].bitcast(F32))
        return
    if DEV_STAGE == 9:
        K.wait("sp", "dve")
        K.dma("sp", t["dbg_e"][0:128, 0:128], sb["RD"][:, 0:128])
        return
    if DEV_STAGE == 5:
        K.wait("sp", "sdma", nrt)
        K.dma("sp", t["dbg_rt"][0:2048, :], t["cc_rt"][:, :])
        return
    K.wait("pool", "sdma", nrt)
    K.pool(lambda e: e.collective_compute(
        "AllGather", ALU.bypass,
        replica_groups=[[0, 1, 2, 3, 4, 5, 6, 7]],
        ins=[t["cc_rt"].ap()],
        outs=[t["rt_all"].ap()]), sem="cc")
    ncc = K.sem["cc"].n
    if DEV_STAGE == 6:
        K.wait("sp", "cc", ncc)
        K.dma("sp", t["dbg_rt"][:, :], t["rt_all"][:, :])
        return

    regs = [nc.gpsimd.alloc_register(f"rtoff{i}") for i in range(4)]
    K.wait("pool", "cc", ncc)
    for side in range(2):
        for blk in range(2):
            i = side * 2 + blk
            K.pool(lambda e, r=regs[i], c=sb["CIDX"][0:1, i:i + 1]:
                   e.reg_load(r, c), sem=None)
            src = bass.AP(t["rt_all"], regs[i], [[64, 128], [8192, 4], [1, 64]])
            K.dma("pool", sb["RT"][:, i * 256:(i + 1) * 256], src)
    nrta = K.sem["gdma"].n

    if DEV_STAGE == 3:
        K.wait("sp", "sdma", nrta)
        K.wait("sp", "gdma")
        K.dma("sp", t["dbg_rt"][:, :], t["rt_all"][:, :])
        K.dma("sp", t["dbg_e"][0:128, 0:1024], sb["RT"][:, :])
        return
    TK, TKU = sb["TK"], sb["TKU"]
    IO8 = sb["IOTA8"][:, :]
    deb_war = [0, 0]
    for tc in range(NTC):
        blk, w = tc // 4, tc % 4
        t0 = sb["RT"][:, blk * 256 + w * 64:blk * 256 + (w + 1) * 64]
        t1 = sb["RT"][:, 512 + blk * 256 + w * 64:512 + blk * 256 + (w + 1) * 64]
        K.wait("dve", "gdma", nrta)
        sx, sy = TK[:, 0:8], TK[:, 8:16]
        ixf, iyf = TK[:, 16:24], TK[:, 24:32]
        ix64 = TK[:, 32:40]
        alls, alli = TK[:, 64:128], TK[:, 128:192]
        s8 = TK[:, 192:200]
        i_f, j_f = TK[:, 200:208], TK[:, 208:216]
        seli, selj = TK[:, 216:224], TK[:, 224:232]
        neg0, ssum, srec = TK[:, 232:233], TK[:, 233:234], TK[:, 234:235]
        ex8, wts, ew8 = TK[:, 240:248], TK[:, 248:256], TK[:, 256:264]
        ixu, iyu, pku = TKU[:, 0:8], TKU[:, 8:16], TKU[:, 16:24]
        iu, ju = TKU[:, 24:32], TKU[:, 32:40]
        K.dve(lambda e, o=sx, i=t0: e.max(out=o, in_=i))
        K.wait("dve", "dve")
        K.dve(lambda e, o=ixu, m=sx, i=t0: e.max_index(out=o, in_max=m, in_values=i))
        K.dve(lambda e, o=sy, i=t1: e.max(out=o, in_=i))
        K.wait("dve", "dve")
        K.dve(lambda e, o=iyu, m=sy, i=t1: e.max_index(out=o, in_max=m, in_values=i))
        K.wait("dve", "dve")
        K.dve(lambda e, o=ixf, i=ixu: e.tensor_copy(o, i))
        K.dve(lambda e, o=iyf, i=iyu: e.tensor_copy(o, i))
        K.wait("dve", "dve")
        K.dve(lambda e, o=ix64, i=ixf: e.tensor_scalar_mul(o, i, 64.0))
        K.wait("dve", "dve")
        for i8 in range(8):
            K.dve(lambda e, o=alls[:, i8 * 8:(i8 + 1) * 8], i=sy,
                  sc=sx[:, i8:i8 + 1]:
                  e.tensor_scalar_add(o, i, sc))
            K.dve(lambda e, o=alli[:, i8 * 8:(i8 + 1) * 8], i=iyf,
                  sc=ix64[:, i8:i8 + 1]:
                  e.tensor_scalar_add(o, i, sc))
        K.wait("dve", "dve")
        K.dve(lambda e, o=s8, i=alls: e.max(out=o, in_=i))
        K.wait("dve", "dve")
        K.dve(lambda e, o=pku, m=s8, i=alls: e.max_index(out=o, in_max=m, in_values=i))
        K.wait("dve", "dve")
        K.dve(lambda e, o=iu, i=pku: e.tensor_single_scalar(o, i, 3, ALU.logical_shift_right))
        K.dve(lambda e, o=ju, i=pku: e.tensor_single_scalar(o, i, 7, ALU.bitwise_and))
        K.wait("dve", "dve")
        K.dve(lambda e, o=i_f, i=iu: e.tensor_copy(o, i))
        K.dve(lambda e, o=j_f, i=ju: e.tensor_copy(o, i))
        K.wait("dve", "dve")
        for kk in range(8):
            K.dve(lambda e, o=TK[:, 448:456], i=IO8, sc=i_f[:, kk:kk + 1]:
                  e.tensor_single_scalar(o, i, sc, ALU.is_equal))
            K.wait("dve", "dve")
            K.dve(lambda e, m=TK[:, 448:456], v=ix64, o=TK[:, 456:464],
                  a=seli[:, kk:kk + 1]:
                  e.scalar_tensor_tensor(o, m, 1.0, v, ALU.bypass, ALU.mult,
                                         accum_out=a))
            K.dve(lambda e, o=TK[:, 448:456], i=IO8, sc=j_f[:, kk:kk + 1]:
                  e.tensor_single_scalar(o, i, sc, ALU.is_equal))
            K.wait("dve", "dve")
            K.dve(lambda e, m=TK[:, 448:456], v=iyf, o=TK[:, 456:464],
                  a=selj[:, kk:kk + 1]:
                  e.scalar_tensor_tensor(o, m, 1.0, v, ALU.bypass, ALU.mult,
                                         accum_out=a))
            K.wait("dve", "dve")
        K.dve(lambda e, o=TK[:, 264:272], a=seli, b=selj: e.tensor_add(o, a, b))
        K.wait("dve", "dve")
        K.dve(lambda e, o=sb["EWID"], i=TK[:, 264:272]: e.tensor_copy(o, i))
        nid = K.sem["dve"].n
        K.dve(lambda e, o=neg0, i=s8[:, 0:1]: e.tensor_scalar_mul(o, i, -1.0))
        K.wait("act", "dve")
        K.act(lambda e, o=ex8, i=s8, b=neg0: e.activation(
            out=o, in_=i, func=AF.Exp, bias=b))
        K.wait("dve", "act")
        K.dve(lambda e, o=ssum, i=ex8: e.reduce_sum(o, i, axis=AXX))
        K.wait("dve", "dve")
        K.dve(lambda e, o=srec, i=ssum: e.reciprocal(o, i))
        K.wait("dve", "dve")
        K.dve(lambda e, o=wts, i=ex8, sc=srec: e.tensor_scalar_mul(o, i, sc))
        par = tc % 2
        xc = sb["XCH"][:, par * 2048:(par + 1) * 2048]
        K.wait("sp", "dve")
        K.dma("sp", xc, x_src[tc * 128:(tc + 1) * 128, :])
        nx = K.sem["sdma"].n
        K.wait("dve", "sdma", nx)
        K.dve(lambda e, xc=xc, r=sb["RS4"][:, tc:tc + 1],
              m=sb["RS4"][:, 8 + tc:9 + tc]:
              e.tensor_scalar(sb["XBF"], xc, r, m, ALU.mult, ALU.add))
        nxbf = K.sem["dve"].n
        K.wait("pool", "dve", nid)
        for k8 in range(8):
            gpar = k8 % 2
            de = sb["DEB"][:, gpar * 2048:(gpar + 1) * 2048]
            ue = sb["UEB"][:, gpar * 2048:(gpar + 1) * 2048]
            K.wait("pool", "dve", deb_war[gpar])
            K.pool(lambda e, o=de, ix=sb["EWID"][:, k8:k8 + 1]:
                   e.indirect_dma_start(o, None, t["l1de"][:, :],
                                        IndirectOffsetOnAxis(ap=ix, axis=0)),
                   sem="gdma", v=16)
            K.pool(lambda e, o=ue, ix=sb["EWID"][:, k8:k8 + 1]:
                   e.indirect_dma_start(o, None, t["l1ue"][:, :],
                                        IndirectOffsetOnAxis(ap=ix, axis=0)),
                   sem="gdma", v=16)
            ng = K.sem["gdma"].n
            K.wait("dve", "gdma", ng)
            K.wait("dve", "dve", nxbf)
            K.dve(lambda e, de=de, a=ew8[:, k8:k8 + 1]:
                  e.scalar_tensor_tensor(sb["LNTM"][:, 0:2048], de, 1.0,
                                         sb["XBF"], ALU.bypass, ALU.mult,
                                         accum_out=a))
            K.wait("act", "dve")
            K.act(lambda e, o=TK[:, 272 + k8:273 + k8], i=ew8[:, k8:k8 + 1]:
                  e.activation(out=o, in_=i, func=AF.Silu))
            K.wait("dve", "act")
            K.dve(lambda e, o=TK[:, 280 + k8:281 + k8],
                  a=TK[:, 272 + k8:273 + k8], b=wts[:, k8:k8 + 1]:
                  e.tensor_mul(o, a, b))
            K.wait("dve", "dve")
            if k8 == 0:
                K.dve(lambda e, ue=ue, sc=TK[:, 280:281]:
                      e.tensor_scalar_mul(sb["EACC"][:, 0:2048], ue, sc))
            else:
                K.dve(lambda e, ue=ue, sc=TK[:, 280 + k8:281 + k8]:
                      e.scalar_tensor_tensor(sb["EACC"][:, 0:2048], ue, sc,
                                             sb["EACC"][:, 0:2048], ALU.mult,
                                             ALU.add))
            deb_war[gpar] = K.sem["dve"].n
        nea = K.sem["dve"].n
        K.wait("sp", "dve", nea)
        K.dma("sp", t["e_scr"][tc * 128:(tc + 1) * 128, :], sb["EACC"][:, 0:2048])

    if DEV_STAGE == 4:
        K.wait("sp", "sdma")
        K.dma("sp", t["dbg_e"][:, :], t["e_scr"][:, :])
        K.dma("sp", t["dbg_rt"][:, :], t["rt_all"][:, :])
        return
    hst_sd = [0, 0]
    wslots = [sb["WT2"][:, 0:2048], sb["WT2"][:, 2048:4096],
              sb["WT"][:, 0:2048], sb["WT"][:, 2048:4096]]
    wslot_last = [K.sem["pe"].n] * 4
    for u in range(2):
        for ic in range(64):
            for gi, wsrc in enumerate((t["l1g_t"], t["l1u_t"])):
                si = (2 * ic + gi) % 4
                wtile = wslots[si]
                K.wait("pool", "pe", wslot_last[si])
                K.dma("pool", wtile, wsrc[ic, :, :], sem="w%d" % si)
                K.wait("pe", "w%d" % si)
                bank = gi
                mms = [(wtile[:, dc * 128:(dc + 1) * 128],
                        A4L[:, dc * 1024 + u * 512:dc * 1024 + (u + 1) * 512])
                       for dc in range(NDC)]
                K.mm_group(bank, mms)
                K.wait("dve", "pe", K.bank_rdy[bank])
                dvedst = sb["QST"] if gi == 0 else sb["EB"]
                K.wait("dve", "act")
                K.dve(lambda e, o=dvedst[:, 0:512], p=K.psum[bank]:
                      e.tensor_copy(o, p[:, :]))
                K.bank_war[bank].append(("dve", K.sem["dve"].n))
                wslot_last[si] = K.sem["pe"].n
            hp = ic % 2
            hsl = sb["EM"][:, hp * 512:(hp + 1) * 512]
            K.wait("act", "dve")
            K.act(lambda e: e.activation(out=sb["QT"][:, 0:512],
                                         in_=sb["QST"][:, 0:512], func=AF.Silu))
            K.wait("dve", "act")
            K.wait("dve", "sdma", hst_sd[hp])
            K.dve(lambda e, o=hsl: e.tensor_mul(o, sb["QT"][:, 0:512],
                                                sb["EB"][:, 0:512]))
            nh = K.sem["dve"].n
            K.wait("sp", "dve", nh)
            K.dma("sp", t["h_scr"][ic, :, u * 512:(u + 1) * 512], hsl)
            hst_sd[hp] = K.sem["sdma"].n

    if DEV_STAGE == 12:
        K.wait("sp", "sdma")
        K.wait("sp", "gdma")
        K.dma("sp", t["dbg_e"].ap().bitcast(BF16)[0:1024, 0:4096],
              t["h_scr"][0:32, :, :])
        return
    K.wait("sp", "dve")
    K.wait("sp", "act")
    K.wait("pool", "dve")
    K.wait("pool", "act")
    p2_last = [K.sem["pe"].n] * 2
    for u in range(2):
        for dh in range(2):
            for bk in range(8):
                for dep in K.bank_war[bk]:
                    K.wait("pe", *dep)
                K.bank_war[bk] = []
            for ic in range(64):
                wpar = ic % 2
                wtile = sb["WT"][:, wpar * 1024:(wpar + 1) * 1024]
                htile = (sb["QST"] if ic % 2 == 0 else sb["QT"])
                K.wait("pool", "pe", p2_last[wpar])
                K.dma("pool", wtile,
                      t["l1d_t"][ic, :, dh * 1024:(dh + 1) * 1024],
                      sem="w%d" % wpar)
                K.wait("sp", "pe", p2_last[wpar])
                K.dma("sp", htile[:, 0:512],
                      t["h_scr"][ic, :, u * 512:(u + 1) * 512],
                      sem="h%d" % wpar)
                K.wait("pe", "w%d" % wpar)
                K.wait("pe", "h%d" % wpar)
                for tc4 in range(4):
                    for dco in range(2):
                        bank = tc4 * 2 + dco
                        l = htile[:, tc4 * 128:(tc4 + 1) * 128]
                        r = wtile[:, dco * 512:(dco + 1) * 512]
                        K.pe(lambda e, b=bank, l=l, r=r, s=(ic == 0),
                             tl=(ic == 63):
                             e.matmul(K.psum[b][:, :], l, r,
                                      start=s, stop=tl),
                             sem="pe" if (tc4 == 3 and dco == 1) else None)
                p2_last[ic % 2] = K.sem["pe"].n
            for tc4 in range(4):
                for dco in range(2):
                    K.bank_rdy[tc4 * 2 + dco] = K.sem["pe"].n
            npeall = K.sem["pe"].n
            K.wait("dve", "pe", npeall)
            K.wait("dve", "act")
            for tc4 in range(4):
                for dco in range(2):
                    bank = tc4 * 2 + dco
                    doff = dh * 1024 + dco * 512
                    dst = a4f[:, tc4 * 2048 + doff:tc4 * 2048 + doff + 512]
                    K.dve(lambda e, o=dst, p=K.psum[bank]:
                          e.tensor_copy(o, p[:, :]))
                    K.bank_war[bank].append(("dve", K.sem["dve"].n))
        nfa = K.sem["act"].n
        for tc4 in range(4):
            tc = u * 4 + tc4
            K.wait("sp", "dve")
            K.dma("sp", sb["EACC"][:, 0:2048],
                  t["e_scr"][tc * 128:(tc + 1) * 128, :])
            nel = K.sem["sdma"].n
            K.wait("dve", "sdma", nel)
            K.wait("dve", "dve")
            fap = a4f[:, tc4 * 2048:(tc4 + 1) * 2048]
            K.dve(lambda e, f=fap: e.tensor_add(sb["FTC"][:, :], f,
                                                sb["EACC"][:, 0:2048]))
            fap = sb["FTC"][:, :]
            if DEV_STAGE == 13:
                nf = K.sem["dve"].n
                K.wait("sp", "dve", nf)
                K.dma("sp", t["dbg_e"][tc * 128:(tc + 1) * 128, :], fap)
                continue
            emit_ln_residual_store(K, st, tc, fap, nfa, x_src, x_dst, final)
        if DEV_STAGE == 13 and u == 1:
            return



def kernel(**inputs):
    in_maps = stage_inputs(inputs)
    nc = build_program()
    res = run_bass_kernel_spmd(nc, in_maps, core_ids=list(range(8)))
    return unstage_output(res.results)


# version 07
# speedup vs baseline: 1.6467x; 1.0036x over previous
import numpy as np
import ml_dtypes
import concourse.bass as bass
import concourse.mybir as mybir
from concourse.bass_core_rust import IndirectOffsetOnAxis
from concourse.bass_utils import run_bass_kernel_spmd
from contextlib import ExitStack

F32 = mybir.dt.float32
BF16 = mybir.dt.bfloat16
U8 = mybir.dt.uint8
U32 = mybir.dt.uint32
I32 = mybir.dt.int32
AF = mybir.ActivationFunctionType
ALU = mybir.AluOpType
AXX = mybir.AxisListType.X

NH = 16
NTC = 8
NDC = 16
CEXP = 20.0
CINV = 1.0 / float(np.sqrt(128.0))
SLOPES = [1.0 / 2 ** (k / 2.0) for k in range(16)]
EPS = 1e-5
N_SUB = 4
DEV_STAGE = 0
DEV_KEYS = 0


def qbase(parity, slot):
    return 512 * parity if slot == 0 else 512 * (3 - parity)


def global_blocks(parity):
    return (0, 3) if parity == 0 else (1, 2)


def kc_owner(kc):
    g = kc // 4
    if g in (0, 3):
        return 0, (kc if g == 0 else kc - 8)
    return 1, kc - 4


def attn_chunks(layer):
    out = []
    for head in range(NH):
        slope = SLOPES[head]
        for slot in range(2):
            kcmax = 7 if slot == 0 else 15
            for kc in range(kcmax + 1):
                contrib = False
                full_both = True
                for p in (0, 1):
                    dq = qbase(p, slot) - 128 * kc
                    valid = dq >= -384
                    far = (layer == 0) and slope * max(0, dq - 127) > 120.0
                    if valid and not far:
                        contrib = True
                    if dq < 127:
                        full_both = False
                if contrib:
                    out.append((head, slot, kc, not full_both))
    return out



class Sem:
    def __init__(self, nc, name):
        self.h = nc.alloc_semaphore(name=name)
        self.n = 0


class KB:
    ENGS = ("pe", "act", "dve", "pool", "sp")

    def __init__(self, nc):
        self.nc = nc
        self.ops = {e: [] for e in self.ENGS}
        self.sem = {s: Sem(nc, "s_" + s)
                    for s in ("sdma", "gdma", "pe", "act", "dve", "pool", "cc",
                              "w0", "w1", "w2", "w3", "v0", "v1", "kv",
                              "h0", "h1", "t0", "t1")}
        self.ctx = ExitStack()
        self._sbuf_bytes = 0
        self.psum = [self.ctx.enter_context(
            nc.psum_tensor(f"pb{i}", [128, 512], F32)) for i in range(8)]
        self.bank_war = [[] for _ in range(8)]
        self.bank_rdy = [0] * 8

    def sb(self, name, shape, dtype):
        self._sbuf_bytes += int(np.prod(shape[1:])) * mybir.dt.size(dtype)
        return self.ctx.enter_context(self.nc.sbuf_tensor(name, shape, dtype))

    def _rec(self, eng, fn, sem=None, v=1):
        s = self.sem[sem] if sem else None
        if s is not None:
            s.n += v

            def f(e, fn=fn, s=s, v=v):
                fn(e).then_inc(s.h, v)
        else:
            def f(e, fn=fn):
                fn(e)
        self.ops[eng].append(f)
        return s.n if s is not None else None

    def pe(self, fn, sem=None, v=1):
        return self._rec("pe", fn, sem, v)

    def act(self, fn, sem="act", v=1):
        return self._rec("act", fn, sem, v)

    def dve(self, fn, sem="dve", v=1):
        return self._rec("dve", fn, sem, v)

    def pool(self, fn, sem="pool", v=1):
        return self._rec("pool", fn, sem, v)

    def sp(self, fn, sem=None, v=1):
        return self._rec("sp", fn, sem, v)

    def wait(self, eng, sem, n=None):
        s = self.sem[sem]
        n = s.n if n is None else n
        assert n <= s.n, f"wait on future count {sem} {n} > {s.n}"
        if n <= 0:
            return
        self.ops[eng].append(lambda e, s=s, n=n: e.wait_ge(s.h, n))

    def dma(self, q, out, in_, sem=None, **kw):
        if sem is None:
            sem = "sdma" if q == "sp" else "gdma"
        return self._rec(q, lambda e: e.dma_start(out=out, in_=in_, **kw), sem, 16)

    def mm_group(self, bank, mms, tail=None, out=None):
        for dep in self.bank_war[bank]:
            self.wait("pe", *dep)
        self.bank_war[bank] = []
        seq = list(mms) + list(tail or [])
        n = len(seq)
        pt = out if out is not None else self.psum[bank][:, :]
        for i, (l, r) in enumerate(seq):
            last = i == n - 1
            self.pe(lambda e, l=l, r=r, s=(i == 0), t=last, pt=pt:
                    e.matmul(pt, l, r, start=s, stop=t),
                    sem="pe" if last else None)
        self.bank_rdy[bank] = self.sem["pe"].n
        return self.bank_rdy[bank]

    def finalize(self, out_waits):
        nc, ops, sems = self.nc, self.ops, self.sem
        with nc.Block() as block:

            @block.tensor
            def _(e):
                for f in ops["pe"]:
                    f(e)

            @block.scalar
            def _(e):
                for f in ops["act"]:
                    f(e)

            @block.vector
            def _(e):
                for f in ops["dve"]:
                    f(e)

            @block.gpsimd
            def _(e):
                for f in ops["pool"]:
                    f(e)

            @block.sync
            def _(e):
                for f in ops["sp"]:
                    f(e)
                for s, n in out_waits:
                    e.wait_ge(sems[s].h, n)



def _tile_qk(wqkv):
    out = np.empty((32, 128, 2048), dtype=np.float32)
    w = wqkv.reshape(16, 128, 48, 128)
    for j in range(32):
        out[j] = w[:, :, j, :].transpose(1, 0, 2).reshape(128, 2048)
    return out


def _tile_v(wv):
    w = wv.reshape(16, 128, 4, 512)
    return np.ascontiguousarray(w.transpose(2, 1, 0, 3).reshape(4, 128, 8192))


def _tile_wo(wo):
    w = wo.reshape(16, 128, 4, 512)
    return np.ascontiguousarray(w.transpose(1, 0, 2, 3).reshape(128, 32768))


def _tile_lhs64(w, transpose):
    if transpose:
        v = w.reshape(64, 128, 16, 128)
        return np.ascontiguousarray(v.transpose(0, 3, 2, 1).reshape(64, 128, 2048))
    v = w.reshape(16, 128, 64, 128)
    return np.ascontiguousarray(v.transpose(2, 1, 0, 3).reshape(64, 128, 2048))


def _tile_down(w, transpose):
    if transpose:
        v = w.reshape(2048, 64, 128)
        return np.ascontiguousarray(v.transpose(1, 2, 0))
    return np.ascontiguousarray(w.reshape(64, 128, 2048))


def _int32_to_u8(w):
    return np.ascontiguousarray(w.view(np.uint8).reshape(*w.shape, 4)[..., 0])


def _w01_strip(parity, slot):
    kcmax = 7 if slot == 0 else 15
    k0 = qbase(parity, slot) - 128 * kcmax
    i = np.arange(128)[:, None]
    x = np.arange(1408)[None, :]
    return ((i - x) <= k0).astype(np.uint8)


def _l0_bias_table(parity, chunks):
    cols = []
    i = np.arange(128)
    for (head, slot, kc, _m) in chunks:
        s = SLOPES[head]
        cols.append(s * (128 * kc + i - qbase(parity, slot)) - CEXP)
    return np.stack(cols, axis=1).astype(np.float32)


def _routing_offsets(core):
    b, h = core // 2, core % 2
    offs = []
    for side in range(2):
        for blk in range(2):
            s0 = global_blocks(h)[blk] * 512
            n0 = b * 2048 + s0
            m0 = n0
            prod_batch = (m0 % 4096 * 0) + (m0 // 4096) + (2 if side == 1 else 0)
            sp = (m0 % 4096) // 2
            blk_p = sp // 512
            prod_parity = 1 if blk_p in (1, 2) else 0
            prod_core = 2 * (prod_batch % 2) + prod_parity if False else None
            pb_in_t = m0 // 4096
            prod_core = 2 * (pb_in_t + (2 if side == 1 else 0) * 0) + prod_parity
            prod_core = 2 * (pb_in_t + (2 if side == 1 else 0)) + prod_parity
            if prod_parity == 0:
                tloc = sp if blk_p == 0 else sp - 1024
            else:
                tloc = sp - 512
            offs.append((prod_core * 2048 + 2 * tloc) * 64)
    return np.array(offs + [0] * 4, dtype=np.uint32).reshape(1, 8)


def stage_inputs(inputs):
    f32 = np.float32
    g = {}
    wqkv = np.asarray(inputs["L0_attn_Wqkv"], f32)
    g["l0qk_t"] = _tile_qk(wqkv)
    g["l0v_t"] = _tile_v(np.ascontiguousarray(wqkv[:, 4096:6144]))
    g["l0wo_t"] = _tile_wo(np.asarray(inputs["L0_attn_Wo"], f32))
    bqkv = np.asarray(inputs["L0_attn_bqkv"], f32)
    g["l0bqk"] = np.ascontiguousarray(bqkv[:4096].reshape(32, 128).T)
    g["l0bv"] = np.ascontiguousarray(bqkv[4096:].reshape(1, 2048))
    g["l0bo"] = np.ascontiguousarray(
        np.asarray(inputs["L0_attn_bo"], f32).reshape(1, 2048))
    g["l0g_t"] = _tile_lhs64(_int32_to_u8(np.asarray(inputs["L0_ffn_gate_w"])), True)
    g["l0u_t"] = _tile_lhs64(_int32_to_u8(np.asarray(inputs["L0_ffn_up_w"])), True)
    g["l0d_t"] = _tile_down(_int32_to_u8(np.asarray(inputs["L0_ffn_down_w"])), True)
    g["l0gs"] = np.ascontiguousarray(
        np.asarray(inputs["L0_ffn_gate_scale"], f32).reshape(64, 128).T)
    g["l0us"] = np.ascontiguousarray(
        np.asarray(inputs["L0_ffn_up_scale"], f32).reshape(64, 128).T)
    g["l0ds"] = np.ascontiguousarray(
        np.asarray(inputs["L0_ffn_down_scale"], f32).reshape(1, 2048))

    wq = np.asarray(inputs["L1_attn_W_Q"], f32)
    wk = np.asarray(inputs["L1_attn_W_K"], f32)
    wv = np.asarray(inputs["L1_attn_W_V"], f32)
    g["l1q_t"] = np.ascontiguousarray(
        wq.transpose(0, 2, 1).reshape(16, 16, 128, 128)
        .transpose(0, 2, 1, 3).reshape(16, 128, 2048))
    g["l1k_t"] = np.ascontiguousarray(
        wk.transpose(0, 2, 1).reshape(16, 16, 128, 128)
        .transpose(0, 2, 1, 3).reshape(16, 128, 2048))
    wv2 = np.ascontiguousarray(wv.transpose(2, 0, 1).reshape(2048, 2048))
    g["l1v_t"] = _tile_v(wv2)
    g["l1wo_t"] = _tile_wo(
        np.ascontiguousarray(np.asarray(inputs["L1_attn_W_O"], f32).T))
    g["l1g_t"] = _tile_lhs64(np.asarray(inputs["L1_ffn_gate_w"], f32), False)
    g["l1u_t"] = _tile_lhs64(np.asarray(inputs["L1_ffn_up_w"], f32), False)
    g["l1d_t"] = _tile_down(np.asarray(inputs["L1_ffn_down_w"], f32), False)
    wqr = np.asarray(inputs["L1_ffn_W_q"], f32)
    g["l1wq_t"] = np.ascontiguousarray(
        wqr.reshape(16, 128, 128).transpose(1, 0, 2).reshape(128, 2048))

    g["l1de"] = np.ascontiguousarray(np.asarray(inputs["L1_ffn_down_embed"], f32))
    g["l1ue"] = np.ascontiguousarray(np.asarray(inputs["L1_ffn_up_embed"], f32))

    x = np.asarray(inputs["L0_attn_x"], f32)
    chunks0 = attn_chunks(0)
    in_maps = []
    for c in range(8):
        b, h = c // 2, c % 2
        m = dict(g)
        m["x0"] = np.ascontiguousarray(np.concatenate(
            [x[b, blk * 512:(blk + 1) * 512] for blk in global_blocks(h)], 0))
        m["w01a"] = np.ascontiguousarray(_w01_strip(h, 0))
        m["w01b"] = np.ascontiguousarray(_w01_strip(h, 1))
        m["l0bias"] = _l0_bias_table(h, chunks0)
        m["cidx"] = _routing_offsets(c)
        m["l1keys"] = np.ascontiguousarray(
            np.asarray(inputs["L1_ffn_keys"], np.float32)[c // 4])
        in_maps.append(m)
    return in_maps


def unstage_output(res_list):
    out = np.empty((4, 2048, 2048), dtype=np.float32)
    for c in range(8):
        b, h = c // 2, c % 2
        y = res_list[c]["out"]
        for i, blk in enumerate(global_blocks(h)):
            out[b, blk * 512:(blk + 1) * 512] = y[i * 512:(i + 1) * 512]
    return out



def build_program(n_sub=None):
    if n_sub is None:
        n_sub = N_SUB
    nc = bass.Bass("TRN2", target_bir_lowering=False)
    K = KB(nc)
    t = {}

    def din(name, shape, dtype):
        t[name] = nc.dram_tensor(name, list(shape), dtype, kind="ExternalInput")

    din("x0", (1024, 2048), F32)
    din("l0qk_t", (32, 128, 2048), F32)
    din("l0v_t", (4, 128, 8192), F32)
    din("l0wo_t", (128, 32768), F32)
    din("l0bqk", (128, 32), F32)
    din("l0bv", (1, 2048), F32)
    din("l0bo", (1, 2048), F32)
    din("l0g_t", (64, 128, 2048), U8)
    din("l0u_t", (64, 128, 2048), U8)
    din("l0d_t", (64, 128, 2048), U8)
    din("l0gs", (128, 64), F32)
    din("l0us", (128, 64), F32)
    din("l0ds", (1, 2048), F32)
    din("l1q_t", (16, 128, 2048), F32)
    din("l1k_t", (16, 128, 2048), F32)
    din("l1v_t", (4, 128, 8192), F32)
    din("l1wo_t", (128, 32768), F32)
    din("l1g_t", (64, 128, 2048), F32)
    din("l1u_t", (64, 128, 2048), F32)
    din("l1d_t", (64, 128, 2048), F32)
    din("l1wq_t", (128, 2048), F32)
    din("l1keys", (64, 64), F32)
    din("l1de", (4096, 2048), F32)
    din("l1ue", (4096, 2048), F32)
    din("w01a", (128, 1408), U8)
    din("w01b", (128, 1408), U8)
    din("l0bias", (128, len(attn_chunks(0))), F32)
    din("cidx", (1, 8), U32)
    t["out"] = nc.dram_tensor("out", [1024, 2048], F32, kind="ExternalOutput")

    dbg = "ExternalOutput" if DEV_STAGE == 1 else "Internal"
    if DEV_STAGE in (3, 4, 5, 6, 7, 8, 9, 10, 12, 13):
        t["dbg_rt"] = nc.dram_tensor("dbg_rt", [16384, 64], F32,
                                     kind="ExternalOutput")
        t["dbg_e"] = nc.dram_tensor("dbg_e", [1024, 2048], F32,
                                    kind="ExternalOutput")
    if DEV_STAGE > 0:
        t["dbg_z"] = nc.dram_tensor("dbg_z", [128, 16384], BF16,
                                    kind="ExternalOutput")
        t["dbg_ln"] = nc.dram_tensor("dbg_ln", [128, 16384], BF16,
                                     kind="ExternalOutput")
    t["x_scr"] = nc.dram_tensor("x_scr", [1024, 2048], F32)
    t["h_scr"] = nc.dram_tensor("h_scr", [64, 128, 1024], BF16)
    t["e_scr"] = nc.dram_tensor("e_scr", [1024, 2048], F32)
    t["qdram"] = nc.dram_tensor("qdram", [16, 2, 128, 512], BF16, kind=dbg)
    t["cc_kv"] = nc.dram_tensor("cc_kv", [256, 16384], BF16, kind=dbg)
    t["kv_all"] = nc.dram_tensor("kv_all", [512, 16384], BF16)
    t["cc_rt"] = nc.dram_tensor("cc_rt", [2048, 64], F32)
    t["rt_all"] = nc.dram_tensor("rt_all", [16384, 64], F32)

    sb = {}
    sb["A4L"] = K.sb("A4L", [128, 16384], BF16)
    sb["A8"] = K.sb("A8", [128, 32768], BF16)
    sb["XCH"] = K.sb("XCH", [128, 2 * 2048], F32)
    sb["LNTM"] = K.sb("LNTM", [128, 2048], BF16)
    sb["FTC"] = K.sb("FTC", [128, 2048], F32)
    sb["STAT"] = K.sb("STAT", [128, 64], F32)
    sb["CONSTS"] = K.sb("CONSTS", [128, 2], F32)
    sb["WT"] = K.sb("WT", [128, 4096], BF16)
    sb["WT2"] = K.sb("WT2", [128, 2 * 2048], BF16)
    sb["ZERO"] = K.sb("ZERO", [128, 512], BF16)
    sb["BQK"] = K.sb("BQK", [128, 32], F32)
    sb["ONEC"] = K.sb("ONEC", [128, 1], BF16)
    sb["IDN"] = K.sb("IDN", [128, 128], BF16)
    sb["GS"] = K.sb("GS", [128, 64], F32)
    sb["US"] = K.sb("US", [128, 64], F32)
    sb["KEYS"] = K.sb("KEYS", [128, 128], BF16)
    sb["IOTA8"] = K.sb("IOTA8", [128, 8], F32)
    sb["DEB"] = K.sb("DEB", [128, 2 * 2048], BF16)
    sb["UEB"] = K.sb("UEB", [128, 2 * 2048], BF16)
    sb["CIDX"] = K.sb("CIDX", [1, 8], U32)
    sb["QR"] = K.sb("QR", [64, 2048], BF16)
    rows1 = K.sb("ROWS1", [128, 2176], F32)
    sb["BVROW"] = rows1[0:1, 0:2048]
    sb["ONE_A"] = rows1[0:1, 2048:2176]
    sb["BOROW"] = rows1[32:33, 0:2048]
    sb["ONE_B"] = rows1[32:33, 2048:2176]
    sb["DSROW"] = rows1[64:65, 0:2048]
    sb["ONE_C"] = rows1[64:65, 2048:2176]
    rows2 = K.sb("ROWS2", [128, 2560], F32)
    sb["SLOPE"] = rows2[0:1, 0:2048]
    sb["ALIROW"] = rows2[0:1, 2048:2560]
    sb["REC"] = rows2[32:33, 0:1024]
    sb["DEN"] = rows2[32:33, 1024:2048]
    sb["ONE_D"] = rows2[32:33, 2048:2176]
    sb["RSX"] = rows2[64:65, 0:1024]
    sb["RSH"] = rows2[64:65, 1024:2048]
    sb["ONE_E"] = rows2[64:65, 2048:2560]
    at = K.sb("AT", [128, 10240], U8)
    at_bf, at_f = at.ap().bitcast(BF16), at.ap().bitcast(F32)
    sb["QST"] = at_bf[:, 0:1024]
    sb["QT"] = at_bf[:, 1024:2048]
    sb["EB"] = at_bf[:, 2048:3072]
    sb["EM"] = at_bf[:, 3072:4096]
    sb["RD"] = at_f[:, 2048:2560]
    sb["XBF"] = at_bf[:, 0:2048]
    sb["RT"] = at_f[:, 1024:2048]
    sb["TK"] = at_f[:, 2048:2560]
    aw = K.sb("AW", [128, 8192], U8)
    aw_bf, aw_f = aw.ap().bitcast(BF16), aw.ap().bitcast(F32)
    aw_u = aw.ap().bitcast(U32)
    sb["W01A"] = aw.ap()[:, 0:1408]
    sb["W01B"] = aw.ap()[:, 1408:2816]
    sb["BIAS0"] = aw_f[:, 1408:1408 + len(attn_chunks(0))]
    sb["TKU"] = aw_u[:, 512:576]
    sb["EWID"] = aw_u[:, 576:584]
    sb["EWW"] = aw_f[:, 584:592]
    sb["RS4"] = aw_f[:, 592:608]
    ae = K.sb("AE", [128, 8192], U8)
    sb["DSB"] = ae.ap().bitcast(F32)
    sb["EACC"] = ae.ap().bitcast(F32)

    st = {"sb": sb, "t": t}
    emit_prologue(K, st)
    emit_attention(K, st, layer=0, x_src=t["x0"], x_dst=t["x_scr"],
                   final=(n_sub == 1))
    if n_sub >= 2:
        emit_ffn_l0(K, st, x_src=t["x_scr"], x_dst=t["x_scr"],
                    final=(n_sub == 2))
    if n_sub >= 3:
        emit_attention(K, st, layer=1, x_src=t["x_scr"], x_dst=t["x_scr"],
                       final=(n_sub == 3))
    if n_sub >= 4:
        emit_ffn_l1(K, st, x_src=t["x_scr"], x_dst=t["x_scr"], final=True)

    K.finalize(out_waits=[("sdma", K.sem["sdma"].n), ("gdma", K.sem["gdma"].n)])
    return nc



def emit_prologue(K, st):
    sb, t = st["sb"], st["t"]
    K.dma("sp", sb["W01A"], t["w01a"][:, :])
    K.dma("sp", sb["W01B"], t["w01b"][:, :])
    K.dma("sp", sb["BIAS0"], t["l0bias"][:, :])
    K.dma("sp", sb["BQK"][:, :], t["l0bqk"][:, :])
    K.dma("sp", sb["BVROW"], t["l0bv"][0:1, :])
    K.dma("sp", sb["BOROW"], t["l0bo"][0:1, :])
    K.dma("sp", sb["DSROW"], t["l0ds"][0:1, :])
    K.dma("sp", sb["GS"][:, :], t["l0gs"][:, :])
    K.dma("sp", sb["US"][:, :], t["l0us"][:, :])
    K.dma("sp", sb["CIDX"][:, :], t["cidx"][:, :])
    st["n_pro_sdma"] = K.sem["sdma"].n
    K.dma("pool", sb["KEYS"][0:64, 0:64], t["l1keys"][:, :])
    K.pool(lambda e: e.iota(sb["ALIROW"].bitcast(I32), [[1, 512]], base=0, channel_multiplier=0))
    for nm in ("ONE_A", "ONE_B", "ONE_C", "ONE_D", "ONE_E"):
        K.pool(lambda e, nm=nm: e.memset(sb[nm], 1.0))
    K.pool(lambda e: e.memset(sb["CONSTS"][:, 0:1], EPS))
    K.pool(lambda e: e.memset(sb["CONSTS"][:, 1:2], -CEXP))
    K.pool(lambda e: e.memset(sb["ONEC"][:, :], 1.0))
    K.pool(lambda e: e.memset(sb["ZERO"][:, :], 0.0))
    K.pool(lambda e: e.memset(sb["IDN"][:, :], 1.0))
    for h in range(16):
        K.pool(lambda e, h=h: e.memset(
            sb["SLOPE"][0:1, h * 128:(h + 1) * 128], SLOPES[h]))
    K.pool(lambda e: e.iota(sb["IOTA8"][:, :].bitcast(I32), [[1, 8]], base=0, channel_multiplier=0))
    K.wait("pool", "pool")
    K.pool(lambda e: e.affine_select(sb["LNTM"][:, 0:128], sb["IDN"][:, :],
                                     [[-1, 128]], ALU.is_ge, 0.0,
                                     base=0, channel_multiplier=1))
    K.wait("pool", "pool")
    K.pool(lambda e: e.affine_select(sb["IDN"][:, :], sb["LNTM"][:, 0:128],
                                     [[1, 128]], ALU.is_ge, 0.0,
                                     base=0, channel_multiplier=-1))
    npool = K.sem["pool"].n
    K.wait("dve", "pool", npool)
    K.dve(lambda e: e.tensor_scalar_mul(sb["ALIROW"],
                                        sb["ALIROW"].bitcast(I32), -1.0 / CINV))
    K.dve(lambda e: e.tensor_copy(sb["IOTA8"][:, :],
                                  sb["IOTA8"][:, :].bitcast(I32)))
    st["n_pro_dve"] = K.sem["dve"].n
    st["n_pro_pool"] = npool



def emit_ln_chunk(K, st, tc, src_ap, out_ap, save_stats=None):
    sb = st["sb"]
    S = sb["STAT"]
    ssum, sqs = S[:, tc:tc + 1], S[:, 8 + tc:9 + tc]
    mean, var = S[:, 16 + tc:17 + tc], S[:, 24 + tc:25 + tc]
    sd, rr = S[:, 32 + tc:33 + tc], S[:, 40 + tc:41 + tc]
    nmr, msq = S[:, 48 + tc:49 + tc], S[:, 56 + tc:57 + tc]
    K.dve(lambda e: e.reduce_sum(ssum, src_ap, axis=AXX))
    K.act(lambda e: e.activation(out=sb["LNTM"][:, 0:2048], in_=src_ap,
                                 func=AF.Square, accum_out=sqs))
    K.wait("dve", "dve")
    K.dve(lambda e: e.tensor_scalar_mul(mean, ssum, 1.0 / 2048.0))
    K.wait("dve", "dve")
    K.dve(lambda e: e.tensor_mul(msq, mean, mean))
    K.wait("dve", "act")
    K.wait("dve", "dve")
    K.dve(lambda e: e.scalar_tensor_tensor(var, sqs, 1.0 / 2048.0, msq,
                                           ALU.mult, ALU.subtract))
    K.wait("act", "dve")
    K.act(lambda e: e.activation(out=sd, in_=var, func=AF.Sqrt,
                                 bias=sb["CONSTS"][:, 0:1]))
    K.wait("dve", "act")
    K.dve(lambda e: e.reciprocal(rr, sd))
    K.wait("dve", "dve")
    K.dve(lambda e: e.tensor_scalar(nmr, mean, rr, -1.0, ALU.mult, ALU.mult))
    K.wait("dve", "dve")
    K.dve(lambda e: e.tensor_scalar(out_ap, src_ap, rr, nmr, ALU.mult, ALU.add))
    if save_stats is not None:
        name, off = save_stats
        K.wait("dve", "dve")
        K.dve(lambda e: e.tensor_copy(sb[name][:, off + tc:off + tc + 1], rr))
        K.dve(lambda e: e.tensor_copy(sb[name][:, off + 8 + tc:off + 9 + tc], nmr))
    return K.sem["dve"].n


def emit_ln_transpose(K, st, x_src, save_stats=None):
    sb = st["sb"]
    K.wait("act", "pool", st["n_pro_pool"])
    K.wait("pe", "pool", st["n_pro_pool"])
    K.wait("dve", "pool", st["n_pro_pool"])
    for tc in range(NTC):
        par = tc % 2
        xc = sb["XCH"][:, par * 2048:(par + 1) * 2048]
        lc = sb["LNTM"][:, 0:2048]
        K.wait("sp", "dve")
        K.wait("sp", "pe")
        K.dma("sp", xc, x_src[tc * 128:(tc + 1) * 128, :])
        K.wait("dve", "sdma")
        K.wait("act", "sdma")
        K.wait("dve", "pe")
        ndve = emit_ln_chunk(K, st, tc, xc, lc, save_stats)
        K.wait("pe", "dve", ndve)
        for half in range(2):
            bank = 6 + half
            for dep in K.bank_war[bank]:
                K.wait("pe", *dep)
            K.bank_war[bank] = []
            pb = K.psum[bank].ap().bitcast(BF16)
            for k in range(8):
                dc = half * 8 + k
                src = sb["LNTM"][:, dc * 128:(dc + 1) * 128]
                K.pe(lambda e, o=pb[:, k * 128:(k + 1) * 128], s=src:
                     e.transpose(o, s, sb["IDN"][:, :]),
                     sem="pe" if k == 7 else None)
            npe = K.sem["pe"].n
            K.wait("dve", "pe", npe)
            dst = bass.AP(sb["A4L"], (half * 8) * 1024 + tc * 128,
                          [[16384, 128], [1024, 8], [1, 128]])
            K.dve(lambda e, d=dst, s=pb: e.tensor_copy(d, s))
            K.bank_war[bank].append(("dve", K.sem["dve"].n))



def emit_attention(K, st, layer, x_src, x_dst, final):
    sb, t = st["sb"], st["t"]
    A4L, A8 = sb["A4L"], sb["A8"]

    emit_ln_transpose(K, st, x_src)
    nln = K.sem["dve"].n
    K.wait("pe", "dve", nln)

    wt_last = [0, 0]
    qst_war = [0, 0]
    for j in range(32):
        par = j % 2
        wtile = sb["WT2"][:, par * 2048:(par + 1) * 2048]
        K.wait("pool", "pe", wt_last[par])
        wsem = "w0" if par == 0 else "w1"
        if layer == 0:
            K.dma("pool", wtile, t["l0qk_t"][j, :, :], sem=wsem)
        else:
            src = t["l1q_t"] if j < 16 else t["l1k_t"]
            K.dma("pool", wtile, src[j % 16, :, :], sem=wsem)
        K.wait("pe", wsem)
        for u in range(2):
            bank = (2 * j + u) % 4
            mms = [(wtile[:, dc * 128:(dc + 1) * 128],
                    A4L[:, dc * 1024 + u * 512:dc * 1024 + (u + 1) * 512])
                   for dc in range(NDC)]
            K.mm_group(bank, mms)
            K.wait("dve", "pe", K.bank_rdy[bank])
            if j < 16:
                par2 = (2 * j + u) % 2
                stg = sb["QST"][:, par2 * 512:(par2 + 1) * 512]
                K.wait("dve", "sdma", qst_war[par2])
                if layer == 0:
                    K.dve(lambda e, o=stg, p=K.psum[bank], bb=sb["BQK"][:, j:j + 1]:
                          e.tensor_scalar_add(o, p[:, :], bb))
                else:
                    K.dve(lambda e, o=stg, p=K.psum[bank]:
                          e.tensor_copy(o, p[:, :]))
                K.bank_war[bank].append(("dve", K.sem["dve"].n))
                K.wait("sp", "dve")
                K.dma("sp", t["qdram"][j, u, :, :], stg)
                qst_war[par2] = K.sem["sdma"].n
            else:
                jc = j - 16
                dst = bass.AP(A8, (4 * u) * 2048 + jc * 128,
                              [[32768, 128], [2048, 4], [1, 128]])
                if layer == 0:
                    K.dve(lambda e, o=dst, p=K.psum[bank], bb=sb["BQK"][:, j:j + 1]:
                          e.tensor_scalar_add(o, p[:, :], bb))
                else:
                    K.dve(lambda e, o=dst, p=K.psum[bank]:
                          e.tensor_copy(o, p[:, :]))
                K.bank_war[bank].append(("dve", K.sem["dve"].n))
        wt_last[par] = K.sem["pe"].n

    vsrc = t["l0v_t"] if layer == 0 else t["l1v_t"]
    v_last = [K.sem["pe"].n, K.sem["pe"].n]
    for jv in range(4):
        if layer == 0:
            K.wait("pe", "sdma", st["n_pro_sdma"])
        for half in range(2):
            vslot = (jv * 2 + half) % 2
            vbuf = sb["WT"] if vslot == 0 else sb["WT2"][:, 0:4096]
            K.wait("pool", "pe", v_last[vslot])
            vsem = "v0" if vslot == 0 else "v1"
            K.dma("pool", vbuf[:, :],
                  vsrc[jv, :, half * 4096:(half + 1) * 4096], sem=vsem)
            K.wait("pe", vsem)
            for tc in range(NTC):
                bank = tc
                if half == 0:
                    for dep in K.bank_war[bank]:
                        K.wait("pe", *dep)
                    K.bank_war[bank] = []
                pt = K.psum[bank][:, :]
                for dci in range(8):
                    dc = half * 8 + dci
                    l = A4L[:, dc * 1024 + tc * 128:dc * 1024 + (tc + 1) * 128]
                    r = vbuf[:, dci * 512:(dci + 1) * 512]
                    lastmm = (half == 1 and dci == 7 and layer != 0)
                    K.pe(lambda e, l=l, r=r, pt=pt,
                         st_=(half == 0 and dci == 0), tl=lastmm:
                         e.matmul(pt, l, r, start=st_, stop=tl),
                         sem="pe" if dci == 7 else None)
                if half == 1 and layer == 0:
                    K.pe(lambda e, pt=pt,
                         r=sb["BVROW"][0:1, jv * 512:(jv + 1) * 512]:
                         e.matmul(pt, sb["ONE_A"], r,
                                  start=False, stop=True), sem="pe")
                if half == 1:
                    K.bank_rdy[bank] = K.sem["pe"].n
                    dst = A8[:, 16384 + tc * 2048 + jv * 512:
                             16384 + tc * 2048 + (jv + 1) * 512]
                    K.wait("act", "pe", K.bank_rdy[bank])
                    K.act(lambda e, d=dst, p=K.psum[bank]:
                          e.activation(out=d, in_=p[:, :], func=AF.Copy))
                    K.bank_war[bank].append(("act", K.sem["act"].n))
            v_last[vslot] = K.sem["pe"].n

    K.wait("sp", "act")
    K.wait("sp", "dve")
    K.dma("sp", t["cc_kv"][0:128, :], A8[:, 0:16384])
    K.dma("sp", t["cc_kv"][128:256, :], A8[:, 16384:32768])
    nsd = K.sem["sdma"].n
    if DEV_STAGE == 1:
        K.wait("sp", "dve")
        K.dma("sp", t["dbg_ln"][:, :], A4L[:, :])
        return
    K.wait("pool", "sdma", nsd)
    K.pool(lambda e: e.collective_compute(
        "AllGather", ALU.bypass,
        replica_groups=[[0, 1], [2, 3], [4, 5], [6, 7]],
        ins=[t["cc_kv"].ap()],
        outs=[t["kv_all"].ap()]), sem="cc")
    ncc = K.sem["cc"].n

    chunks = attn_chunks(layer)
    zslots = {}
    for ch in chunks:
        zslots.setdefault((ch[0], ch[1]), []).append(ch)
    bias_col = {ch: i for i, ch in enumerate(attn_chunks(0))}
    ebuf_war = [0, 0]
    qt_war = [0, 0]

    for hg in range(2):
        K.wait("sp", "cc", ncc)
        K.wait("sp", "pe")
        for kc in range(16):
            r, lt = kc_owner(kc)
            base = lt * 2048 + hg * 1024
            K.dma("sp", A8[:, kc * 1024:(kc + 1) * 1024],
                  t["kv_all"][r * 256:r * 256 + 128, base:base + 1024],
                  sem="kv")
            K.dma("sp", A8[:, 16384 + kc * 1024:16384 + (kc + 1) * 1024],
                  t["kv_all"][r * 256 + 128:r * 256 + 256, base:base + 1024],
                  sem="kv")
        K.wait("pe", "kv")

        for head in range(hg * 8, hg * 8 + 8):
            for slot in range(2):
                lst = zslots.get((head, slot))
                if not lst:
                    continue
                hpar = (head * 2 + slot) % 2
                zbank, dbank, rbank = 2 + hpar, 0 + hpar, 4 + hpar
                qt = sb["QT"][:, hpar * 512:(hpar + 1) * 512]
                K.wait("sp", "pe", qt_war[hpar])
                K.dma("sp", qt, t["qdram"][head, slot, :, :],
                      sem="t%d" % hpar)
                K.wait("pe", "t%d" % hpar)
                for bk in (zbank, dbank):
                    for dep in K.bank_war[bk]:
                        K.wait("pe", *dep)
                    K.bank_war[bk] = []
                first = True
                nch = len(lst)

                def emit_scores(ch_i, ch):
                    _, _, kc, masked = ch
                    spar = ch_i % 2
                    sbank = 6 + spar
                    for dep in K.bank_war[sbank]:
                        K.wait("pe", *dep)
                    K.bank_war[sbank] = []
                    koff = kc * 1024 + (head - hg * 8) * 128
                    ktile = A8[:, koff:koff + 128]
                    if layer == 0:
                        K.pe(lambda e, b=sbank, l=ktile, r=qt:
                             e.matmul(K.psum[b][:, :], l, r,
                                      start=True, stop=False))
                        K.pe(lambda e, b=sbank,
                             l=sb["SLOPE"][0:1, head * 128:(head + 1) * 128]:
                             e.matmul(K.psum[b][:, :], l, sb["ALIROW"],
                                      start=False, stop=True), sem="pe")
                    else:
                        K.pe(lambda e, b=sbank, l=ktile, r=qt:
                             e.matmul(K.psum[b][:, :], l, r,
                                      start=True, stop=True), sem="pe")
                    nsc = K.sem["pe"].n
                    eb = sb["EB"][:, spar * 512:(spar + 1) * 512]
                    K.wait("act", "pe", nsc)
                    K.wait("act", "pe", ebuf_war[spar])
                    K.wait("act", "dve")
                    if layer == 0:
                        bias = sb["BIAS0"][:, bias_col[ch]:bias_col[ch] + 1]
                        K.act(lambda e, o=eb, p=K.psum[sbank], bb=bias:
                              e.activation(out=o, in_=p[:, :], func=AF.Exp,
                                           bias=bb, scale=CINV))
                    else:
                        K.act(lambda e, o=eb, p=K.psum[sbank]:
                              e.activation(out=o, in_=p[:, :], func=AF.Exp,
                                           bias=sb["CONSTS"][:, 1:2],
                                           scale=CINV))
                    K.bank_war[sbank].append(("act", K.sem["act"].n))
                    if masked:
                        em = sb["EM"][:, spar * 512:(spar + 1) * 512]
                        strip = sb["W01A"] if slot == 0 else sb["W01B"]
                        kcm = 7 if slot == 0 else 15
                        off = (kcm - kc) * 128
                        K.wait("dve", "act")
                        K.wait("dve", "pe", ebuf_war[spar])
                        if layer == 0:
                            K.dve(lambda e, o=em, m=strip[:, off:off + 512], a=eb:
                                  e.select(o, m, a, sb["ZERO"][:, :]))
                        else:
                            K.dve(lambda e, o=em, m=strip[:, off:off + 512], a=eb:
                                  e.tensor_mul(o, a, m))
                        K.bank_war[sbank].append(("dve", K.sem["dve"].n))
                        return em, ("dve", K.sem["dve"].n)
                    return eb, ("act", K.sem["act"].n)

                def emit_av(ch_i, etile, edep):
                    spar = ch_i % 2
                    K.wait("pe", edep[0], edep[1])
                    K.pe(lambda e, b=dbank, r=etile, s=(ch_i == 0),
                         tl=(ch_i == nch - 1):
                         e.matmul(K.psum[b][0:1, :], sb["ONEC"][:, 0:1], r,
                                  start=s, stop=tl),
                         sem="pe" if ch_i == nch - 1 else None)
                    koff2 = lst[ch_i][2] * 1024 + (head - hg * 8) * 128
                    vtile = A8[:, 16384 + koff2:16384 + koff2 + 128]
                    K.pe(lambda e, b=zbank, l=vtile, r=etile, s=(ch_i == 0),
                         tl=(ch_i == nch - 1):
                         e.matmul(K.psum[b][:, :], l, r, start=s, stop=tl),
                         sem="pe" if ch_i == nch - 1 else None)
                    ebuf_war[spar] = K.sem["pe"].n

                pend = None
                for ch_i, ch in enumerate(lst):
                    et, edep = emit_scores(ch_i, ch)
                    if pend is not None:
                        emit_av(pend[0], pend[1], pend[2])
                    pend = (ch_i, et, edep)
                emit_av(pend[0], pend[1], pend[2])
                nz = K.sem["pe"].n
                den = sb["DEN"][0:1, hpar * 512:(hpar + 1) * 512]
                rec = sb["REC"][0:1, hpar * 512:(hpar + 1) * 512]
                rdb = sb["RD"][:, 0:512]
                K.wait("dve", "pe", nz)
                K.dve(lambda e, o=den, p=K.psum[dbank]:
                      e.tensor_copy(o, p[0:1, :]))
                K.bank_war[dbank].append(("dve", K.sem["dve"].n))
                K.wait("dve", "dve")
                K.dve(lambda e, o=rec, i=den: e.reciprocal(o, i))
                nrec = K.sem["dve"].n
                for dep in K.bank_war[rbank]:
                    K.wait("pe", *dep)
                K.bank_war[rbank] = []
                K.wait("pe", "dve", nrec)
                K.pe(lambda e, b=rbank, r=rec:
                     e.matmul(K.psum[b][:, :], sb["ONE_D"], r,
                              start=True, stop=True), sem="pe")
                nrb = K.sem["pe"].n
                K.wait("act", "pe", nrb)
                K.wait("act", "dve")
                K.act(lambda e, o=rdb, p=K.psum[rbank]:
                      e.activation(out=o, in_=p[:, :], func=AF.Copy))
                K.bank_war[rbank].append(("act", K.sem["act"].n))
                zdst = A4L[:, head * 1024 + slot * 512:
                           head * 1024 + (slot + 1) * 512]
                K.wait("dve", "act")
                K.dve(lambda e, o=zdst, p=K.psum[zbank], r=rdb:
                      e.tensor_mul(o, p[:, :], r))
                K.bank_war[zbank].append(("dve", K.sem["dve"].n))
                qt_war[hpar] = K.sem["pe"].n

    if DEV_STAGE == 2:
        K.wait("sp", "dve")
        K.dma("sp", t["dbg_z"][:, :], A4L[:, :])
        K.wait("sp", "dve")
        K.dma("sp", t["dbg_ln"][:, :], A8[:, 0:16384])
        return
    K.wait("pool", "pe")
    wosrc = t["l0wo_t"] if layer == 0 else t["l1wo_t"]
    K.dma("pool", A8[:, :], wosrc[:, :], sem="kv")
    K.wait("pe", "kv")
    K.wait("pe", "dve")
    emit_proj_ln_residual(
        K, st, x_src=x_src, x_dst=x_dst, final=final,
        lhs_of=lambda k, tc: A4L[:, k * 1024 + (tc // 4) * 512
                                 + (tc % 4) * 128:k * 1024 + (tc // 4) * 512
                                 + (tc % 4) * 128 + 128],
        rhs_of=lambda k, dc: A8[:, k * 2048 + dc * 512:k * 2048 + (dc + 1) * 512],
        nk=16, bias_row=("BOROW", 0) if layer == 0 else None,
        scale_bcast=False, extra_add=None)


def emit_proj_ln_residual(K, st, x_src, x_dst, final, lhs_of, rhs_of,
                          nk, bias_row, scale_bcast, extra_add):
    sb, t = st["sb"], st["t"]
    prev_res = 0
    for tc in range(NTC):
        for dc in range(4):
            bank = (tc * 4 + dc) % 2
            mms = [(lhs_of(k, tc), rhs_of(k, dc)) for k in range(nk)]
            tail = None
            if bias_row is not None:
                name, off = bias_row
                tail = [(sb["ONE_B"],
                         sb[name][0:1, off + dc * 512:off + (dc + 1) * 512])]
            K.mm_group(bank, mms, tail=tail)
            dst = sb["FTC"][:, dc * 512:(dc + 1) * 512]
            if scale_bcast:
                K.wait("dve", "pe", K.bank_rdy[bank])
                K.wait("dve", "dve", prev_res)
                K.dve(lambda e, o=dst, p=K.psum[bank],
                      s=sb["DSB"][:, dc * 512:(dc + 1) * 512]:
                      e.tensor_mul(o, p[:, :], s))
                K.bank_war[bank].append(("dve", K.sem["dve"].n))
            else:
                K.wait("act", "pe", K.bank_rdy[bank])
                K.wait("act", "dve", prev_res)
                K.act(lambda e, o=dst, p=K.psum[bank]:
                      e.activation(out=o, in_=p[:, :], func=AF.Copy))
                K.bank_war[bank].append(("act", K.sem["act"].n))
        nfa = K.sem["act"].n
        if extra_add is not None:
            K.wait("dve", "act", nfa)
            K.wait("dve", "dve")
            K.wait("dve", "pool")
            K.dve(lambda e, a=extra_add(tc):
                  e.tensor_add(sb["FTC"][:, :], sb["FTC"][:, :], a))
        emit_ln_residual_store(K, st, tc, sb["FTC"][:, :], nfa,
                               x_src, x_dst, final)
        prev_res = K.sem["dve"].n



def emit_ln_residual_store(K, st, tc, F_ap, nfa, x_src, x_dst, final):
    sb, t = st["sb"], st["t"]
    par = tc % 2
    xc = sb["XCH"][:, par * 2048:(par + 1) * 2048]
    K.wait("sp", "dve")
    K.dma("sp", xc, x_src[tc * 128:(tc + 1) * 128, :])
    nx = K.sem["sdma"].n
    K.wait("dve", "act", nfa)
    K.wait("dve", "dve")
    K.wait("act", "act", nfa)
    K.wait("act", "dve", K.sem["dve"].n)
    emit_ln_chunk(K, st, tc, F_ap, F_ap)
    K.wait("dve", "sdma", nx)
    K.wait("dve", "dve")
    K.dve(lambda e, xc=xc, f=F_ap: e.tensor_add(xc, xc, f))
    nxn = K.sem["dve"].n
    K.wait("sp", "dve", nxn)
    dst = t["out"] if final else x_dst
    K.dma("sp", dst[tc * 128:(tc + 1) * 128, :], xc)


def emit_ffn_l0(K, st, x_src, x_dst, final):
    sb, t = st["sb"], st["t"]
    A4L, A8 = sb["A4L"], sb["A8"]
    a4f = A4L.ap().bitcast(F32)

    emit_ln_transpose(K, st, x_src)
    nln = K.sem["dve"].n
    K.wait("pe", "dve", nln)

    K.wait("pe", "sdma", st["n_pro_sdma"])
    for dc in range(4):
        bank = dc % 2
        K.mm_group(bank, [(sb["ONE_C"], sb["DSROW"][0:1, dc * 512:(dc + 1) * 512])])
        K.wait("act", "pe", K.bank_rdy[bank])
        K.act(lambda e, o=sb["DSB"][:, dc * 512:(dc + 1) * 512], p=K.psum[bank]:
              e.activation(out=o, in_=p[:, :], func=AF.Copy))
        K.bank_war[bank].append(("act", K.sem["act"].n))
    for u in range(2):
        bank = 4 + u
        mms = [(sb["ONEC"][:, 0:1],
                A4L[:, dc * 1024 + u * 512:dc * 1024 + (u + 1) * 512])
               for dc in range(NDC)]
        K.mm_group(bank, mms, out=K.psum[bank][0:1, :])
        K.wait("dve", "pe", K.bank_rdy[bank])
        K.dve(lambda e, o=sb["RSX"][0:1, u * 512:(u + 1) * 512], p=K.psum[bank]:
              e.tensor_scalar_mul(o, p[0:1, :], -128.0))
        K.bank_war[bank].append(("dve", K.sem["dve"].n))
    nrsx = K.sem["dve"].n
    K.wait("pe", "dve", nrsx)

    ga = sb["QT"]
    sgt = sb["QST"]
    ua = sb["EB"]
    hst = sb["EM"]
    hst_war = [0, 0]
    hst_sd = [0, 0]
    wslots = [sb["WT2"][:, 0:2048], sb["WT2"][:, 2048:4096],
              sb["WT"][:, 0:2048], sb["WT"][:, 2048:4096]]
    wslot_last = [K.sem["pe"].n] * 4
    for u in range(2):
        hbank = 2 + u
        for dep in K.bank_war[hbank]:
            K.wait("pe", *dep)
        K.bank_war[hbank] = []
        for ic in range(64):
            for gi, (wsrc, scl) in enumerate(
                    ((t["l0g_t"], sb["GS"]), (t["l0u_t"], sb["US"]))):
                si = (2 * ic + gi) % 4
                wtile = wslots[si]
                K.wait("pool", "pe", wslot_last[si])
                K.dma("pool", wtile, wsrc[ic, :, :], sem="w%d" % si)
                K.wait("pe", "w%d" % si)
                bank = gi
                mms = [(wtile[:, dc * 128:(dc + 1) * 128],
                        A4L[:, dc * 1024 + u * 512:dc * 1024 + (u + 1) * 512])
                       for dc in range(NDC)]
                tail = [(sb["ONE_E"][0:1, 0:128],
                         sb["RSX"][0:1, u * 512:(u + 1) * 512])]
                K.mm_group(bank, mms, tail=tail)
                K.wait("dve", "pe", K.bank_rdy[bank])
                dvedst = sgt if gi == 0 else ua
                K.wait("dve", "act")
                K.dve(lambda e, o=dvedst[:, 0:512], p=K.psum[bank],
                      s=scl[:, ic:ic + 1]:
                      e.tensor_scalar_mul(o, p[:, :], s))
                K.bank_war[bank].append(("dve", K.sem["dve"].n))
                wslot_last[si] = K.sem["pe"].n
            hp = ic % 2
            hsl = hst[:, hp * 512:(hp + 1) * 512]
            K.wait("act", "dve")
            K.act(lambda e: e.activation(out=ga[:, 0:512], in_=sgt[:, 0:512],
                                         func=AF.Silu))
            K.wait("dve", "act")
            K.wait("dve", "pe", hst_war[hp])
            K.wait("dve", "sdma", hst_sd[hp])
            K.dve(lambda e, o=hsl: e.tensor_mul(o, ga[:, 0:512], ua[:, 0:512]))
            nh = K.sem["dve"].n
            K.wait("pe", "dve", nh)
            K.pe(lambda e, b=hbank, s=(ic == 0), tl=(ic == 63), r=hsl:
                 e.matmul(K.psum[b][0:1, :], sb["ONEC"][:, 0:1], r,
                          start=s, stop=tl), sem="pe")
            hst_war[hp] = K.sem["pe"].n
            K.wait("sp", "dve", nh)
            K.dma("sp", t["h_scr"][ic, :, u * 512:(u + 1) * 512], hsl)
            hst_sd[hp] = K.sem["sdma"].n
        K.wait("dve", "pe", K.sem["pe"].n)
        K.dve(lambda e, o=sb["RSH"][0:1, u * 512:(u + 1) * 512], p=K.psum[hbank]:
              e.tensor_scalar_mul(o, p[0:1, :], -128.0))
        K.bank_war[hbank].append(("dve", K.sem["dve"].n))
    nrsh = K.sem["dve"].n
    K.wait("pe", "dve", nrsh)

    K.wait("sp", "dve")
    K.wait("sp", "act")
    K.wait("pool", "dve")
    K.wait("pool", "act")
    p2_last = [K.sem["pe"].n] * 2
    for u in range(2):
        for dh in range(2):
            for bk in range(8):
                for dep in K.bank_war[bk]:
                    K.wait("pe", *dep)
                K.bank_war[bk] = []
            for ic in range(64):
                wpar = ic % 2
                wtile = sb["WT"][:, wpar * 1024:(wpar + 1) * 1024]
                hpar = ic % 2
                htile = (sb["QST"] if hpar == 0 else sb["QT"])
                K.wait("pool", "pe", p2_last[wpar])
                K.dma("pool", wtile,
                      t["l0d_t"][ic, :, dh * 1024:(dh + 1) * 1024],
                      sem="w%d" % wpar)
                K.wait("sp", "pe", p2_last[hpar])
                K.dma("sp", htile[:, 0:512],
                      t["h_scr"][ic, :, u * 512:(u + 1) * 512],
                      sem="h%d" % hpar)
                K.wait("pe", "w%d" % wpar)
                K.wait("pe", "h%d" % hpar)
                for tc4 in range(4):
                    for dco in range(2):
                        bank = tc4 * 2 + dco
                        l = htile[:, tc4 * 128:(tc4 + 1) * 128]
                        r = wtile[:, dco * 512:(dco + 1) * 512]
                        K.pe(lambda e, b=bank, l=l, r=r, s=(ic == 0):
                             e.matmul(K.psum[b][:, :], l, r,
                                      start=s, stop=False),
                             sem="pe" if (tc4 == 3 and dco == 1) else None)
                p2_last[ic % 2] = K.sem["pe"].n
            for tc4 in range(4):
                for dco in range(2):
                    bank = tc4 * 2 + dco
                    l = sb["RSH"][0:1, u * 512 + tc4 * 128:
                                  u * 512 + (tc4 + 1) * 128]
                    K.pe(lambda e, b=bank, l=l:
                         e.matmul(K.psum[b][:, :], l, sb["ONE_E"],
                                  start=False, stop=True), sem="pe")
                    K.bank_rdy[bank] = K.sem["pe"].n
            npeall = K.sem["pe"].n
            K.wait("dve", "pe", npeall)
            K.wait("dve", "act")
            for tc4 in range(4):
                for dco in range(2):
                    bank = tc4 * 2 + dco
                    doff = dh * 1024 + dco * 512
                    dst = a4f[:, tc4 * 2048 + doff:tc4 * 2048 + doff + 512]
                    K.dve(lambda e, o=dst, p=K.psum[bank],
                          s=sb["DSB"][:, doff:doff + 512]:
                          e.tensor_mul(o, p[:, :], s))
                    K.bank_war[bank].append(("dve", K.sem["dve"].n))
        nfa = K.sem["act"].n
        for tc4 in range(4):
            tc = u * 4 + tc4
            emit_ln_residual_store(K, st, tc,
                                   a4f[:, tc4 * 2048:(tc4 + 1) * 2048],
                                   nfa, x_src, x_dst, final)


def emit_ffn_l1(K, st, x_src, x_dst, final):
    sb, t = st["sb"], st["t"]
    nc = K.nc
    A4L = sb["A4L"]
    a4f = A4L.ap().bitcast(F32)

    emit_ln_transpose(K, st, x_src, save_stats=("RS4", 0))
    nln = K.sem["dve"].n
    K.wait("pe", "dve", nln)
    if DEV_STAGE == 7:
        K.wait("sp", "dve")
        K.dma("sp", t["dbg_rt"][0:128, 0:16], sb["RS4"])
        K.dma("sp", t["dbg_e"][0:128, 0:512],
              sb["A4L"][:, 0:1024].bitcast(F32))
        return

    wq = sb["WT2"][:, 0:2048]
    K.wait("pool", "pe")
    K.dma("pool", wq, t["l1wq_t"][:, :])
    K.wait("pe", "gdma")
    for u in range(2):
        for half in range(2):
            bank = u
            mms = [(wq[:, dc * 128 + half * 64:dc * 128 + half * 64 + 64],
                    A4L[:, dc * 1024 + u * 512:dc * 1024 + (u + 1) * 512])
                   for dc in range(NDC)]
            K.mm_group(bank, mms, out=K.psum[bank][0:64, :])
            K.wait("dve", "pe", K.bank_rdy[bank])
            K.dve(lambda e, o=sb["QR"][0:64, half * 1024 + u * 512:
                                       half * 1024 + (u + 1) * 512],
                  p=K.psum[bank]:
                  e.tensor_copy(o, p[0:64, :]))
            K.bank_war[bank].append(("dve", K.sem["dve"].n))
    nqr = K.sem["dve"].n
    if DEV_STAGE == 8:
        K.wait("sp", "dve", nqr)
        K.dma("sp", t["dbg_e"][0:128, 0:512], sb["QR"][:, :].bitcast(F32))
        return
    K.wait("pe", "dve", nqr)
    K.wait("pe", "gdma")
    for tc in range(NTC):
        bank = 2 + (tc % 2)
        for dep in K.bank_war[bank]:
            K.wait("pe", *dep)
        K.bank_war[bank] = []
        u, w = tc // 4, tc % 4
        for half in range(2):
            l = sb["QR"][0:64, half * 1024 + tc * 128:
                         half * 1024 + (tc + 1) * 128]
            r = sb["KEYS"][0:64, 0:64]
            K.pe(lambda e, b=bank, l=l, r=r, h=half:
                 e.matmul(K.psum[b][:, h * 64:(h + 1) * 64], l, r,
                          start=True, stop=True), sem="pe")
        nsc = K.sem["pe"].n
        if DEV_STAGE == 10:
            K.bank_war[bank].append(("pe", nsc))
            continue
        K.wait("dve", "pe", nsc)
        K.wait("dve", "sdma")
        K.dve(lambda e, o=sb["RD"][:, 0:128], p=K.psum[bank]:
              e.tensor_copy(o, p[:, 0:128]))
        K.bank_war[bank].append(("dve", K.sem["dve"].n))
        if DEV_STAGE == 9:
            continue
        K.wait("sp", "dve")
        K.dma("sp", bass.AP(t["cc_rt"], tc * 16384, [[128, 128], [1, 128]]),
              sb["RD"][:, 0:128])
    nrt = K.sem["sdma"].n
    if DEV_STAGE == 10:
        K.wait("sp", "pe")
        K.dma("sp", t["dbg_e"][0:128, 0:512], sb["QR"][:, :].bitcast(F32))
        return
    if DEV_STAGE == 9:
        K.wait("sp", "dve")
        K.dma("sp", t["dbg_e"][0:128, 0:128], sb["RD"][:, 0:128])
        return
    if DEV_STAGE == 5:
        K.wait("sp", "sdma", nrt)
        K.dma("sp", t["dbg_rt"][0:2048, :], t["cc_rt"][:, :])
        return
    K.wait("pool", "sdma", nrt)
    K.pool(lambda e: e.collective_compute(
        "AllGather", ALU.bypass,
        replica_groups=[[0, 1, 2, 3, 4, 5, 6, 7]],
        ins=[t["cc_rt"].ap()],
        outs=[t["rt_all"].ap()]), sem="cc")
    ncc = K.sem["cc"].n
    if DEV_STAGE == 6:
        K.wait("sp", "cc", ncc)
        K.dma("sp", t["dbg_rt"][:, :], t["rt_all"][:, :])
        return

    regs = [nc.gpsimd.alloc_register(f"rtoff{i}") for i in range(4)]
    K.wait("pool", "cc", ncc)
    for side in range(2):
        for blk in range(2):
            i = side * 2 + blk
            K.pool(lambda e, r=regs[i], c=sb["CIDX"][0:1, i:i + 1]:
                   e.reg_load(r, c), sem=None)
            src = bass.AP(t["rt_all"], regs[i], [[64, 128], [8192, 4], [1, 64]])
            K.dma("pool", sb["RT"][:, i * 256:(i + 1) * 256], src)
    nrta = K.sem["gdma"].n

    if DEV_STAGE == 3:
        K.wait("sp", "sdma", nrta)
        K.wait("sp", "gdma")
        K.dma("sp", t["dbg_rt"][:, :], t["rt_all"][:, :])
        K.dma("sp", t["dbg_e"][0:128, 0:1024], sb["RT"][:, :])
        return
    TK, TKU = sb["TK"], sb["TKU"]
    IO8 = sb["IOTA8"][:, :]
    deb_war = [0, 0]
    for tc in range(NTC):
        blk, w = tc // 4, tc % 4
        t0 = sb["RT"][:, blk * 256 + w * 64:blk * 256 + (w + 1) * 64]
        t1 = sb["RT"][:, 512 + blk * 256 + w * 64:512 + blk * 256 + (w + 1) * 64]
        K.wait("dve", "gdma", nrta)
        sx, sy = TK[:, 0:8], TK[:, 8:16]
        ixf, iyf = TK[:, 16:24], TK[:, 24:32]
        ix64 = TK[:, 32:40]
        alls, alli = TK[:, 64:128], TK[:, 128:192]
        s8 = TK[:, 192:200]
        i_f, j_f = TK[:, 200:208], TK[:, 208:216]
        seli, selj = TK[:, 216:224], TK[:, 224:232]
        neg0, ssum, srec = TK[:, 232:233], TK[:, 233:234], TK[:, 234:235]
        ex8, wts, ew8 = TK[:, 240:248], TK[:, 248:256], TK[:, 256:264]
        ixu, iyu, pku = TKU[:, 0:8], TKU[:, 8:16], TKU[:, 16:24]
        iu, ju = TKU[:, 24:32], TKU[:, 32:40]
        K.dve(lambda e, o=sx, i=t0: e.max(out=o, in_=i))
        K.wait("dve", "dve")
        K.dve(lambda e, o=ixu, m=sx, i=t0: e.max_index(out=o, in_max=m, in_values=i))
        K.dve(lambda e, o=sy, i=t1: e.max(out=o, in_=i))
        K.wait("dve", "dve")
        K.dve(lambda e, o=iyu, m=sy, i=t1: e.max_index(out=o, in_max=m, in_values=i))
        K.wait("dve", "dve")
        K.dve(lambda e, o=ixf, i=ixu: e.tensor_copy(o, i))
        K.dve(lambda e, o=iyf, i=iyu: e.tensor_copy(o, i))
        K.wait("dve", "dve")
        K.dve(lambda e, o=ix64, i=ixf: e.tensor_scalar_mul(o, i, 64.0))
        K.wait("dve", "dve")
        for i8 in range(8):
            K.dve(lambda e, o=alls[:, i8 * 8:(i8 + 1) * 8], i=sy,
                  sc=sx[:, i8:i8 + 1]:
                  e.tensor_scalar_add(o, i, sc))
            K.dve(lambda e, o=alli[:, i8 * 8:(i8 + 1) * 8], i=iyf,
                  sc=ix64[:, i8:i8 + 1]:
                  e.tensor_scalar_add(o, i, sc))
        K.wait("dve", "dve")
        K.dve(lambda e, o=s8, i=alls: e.max(out=o, in_=i))
        K.wait("dve", "dve")
        K.dve(lambda e, o=pku, m=s8, i=alls: e.max_index(out=o, in_max=m, in_values=i))
        K.wait("dve", "dve")
        K.dve(lambda e, o=iu, i=pku: e.tensor_single_scalar(o, i, 3, ALU.logical_shift_right))
        K.dve(lambda e, o=ju, i=pku: e.tensor_single_scalar(o, i, 7, ALU.bitwise_and))
        K.wait("dve", "dve")
        K.dve(lambda e, o=i_f, i=iu: e.tensor_copy(o, i))
        K.dve(lambda e, o=j_f, i=ju: e.tensor_copy(o, i))
        K.wait("dve", "dve")
        for kk in range(8):
            K.dve(lambda e, o=TK[:, 448:456], i=IO8, sc=i_f[:, kk:kk + 1]:
                  e.tensor_single_scalar(o, i, sc, ALU.is_equal))
            K.wait("dve", "dve")
            K.dve(lambda e, m=TK[:, 448:456], v=ix64, o=TK[:, 456:464],
                  a=seli[:, kk:kk + 1]:
                  e.scalar_tensor_tensor(o, m, 1.0, v, ALU.bypass, ALU.mult,
                                         accum_out=a))
            K.dve(lambda e, o=TK[:, 448:456], i=IO8, sc=j_f[:, kk:kk + 1]:
                  e.tensor_single_scalar(o, i, sc, ALU.is_equal))
            K.wait("dve", "dve")
            K.dve(lambda e, m=TK[:, 448:456], v=iyf, o=TK[:, 456:464],
                  a=selj[:, kk:kk + 1]:
                  e.scalar_tensor_tensor(o, m, 1.0, v, ALU.bypass, ALU.mult,
                                         accum_out=a))
            K.wait("dve", "dve")
        K.dve(lambda e, o=TK[:, 264:272], a=seli, b=selj: e.tensor_add(o, a, b))
        K.wait("dve", "dve")
        K.dve(lambda e, o=sb["EWID"], i=TK[:, 264:272]: e.tensor_copy(o, i))
        nid = K.sem["dve"].n
        K.dve(lambda e, o=neg0, i=s8[:, 0:1]: e.tensor_scalar_mul(o, i, -1.0))
        K.wait("act", "dve")
        K.act(lambda e, o=ex8, i=s8, b=neg0: e.activation(
            out=o, in_=i, func=AF.Exp, bias=b))
        K.wait("dve", "act")
        K.dve(lambda e, o=ssum, i=ex8: e.reduce_sum(o, i, axis=AXX))
        K.wait("dve", "dve")
        K.dve(lambda e, o=srec, i=ssum: e.reciprocal(o, i))
        K.wait("dve", "dve")
        K.dve(lambda e, o=wts, i=ex8, sc=srec: e.tensor_scalar_mul(o, i, sc))
        par = tc % 2
        xc = sb["XCH"][:, par * 2048:(par + 1) * 2048]
        K.wait("sp", "dve")
        K.dma("sp", xc, x_src[tc * 128:(tc + 1) * 128, :])
        nx = K.sem["sdma"].n
        K.wait("dve", "sdma", nx)
        K.dve(lambda e, xc=xc, r=sb["RS4"][:, tc:tc + 1],
              m=sb["RS4"][:, 8 + tc:9 + tc]:
              e.tensor_scalar(sb["XBF"], xc, r, m, ALU.mult, ALU.add))
        nxbf = K.sem["dve"].n
        K.wait("pool", "dve", nid)
        for k8 in range(8):
            gpar = k8 % 2
            de = sb["DEB"][:, gpar * 2048:(gpar + 1) * 2048]
            ue = sb["UEB"][:, gpar * 2048:(gpar + 1) * 2048]
            K.wait("pool", "dve", deb_war[gpar])
            K.pool(lambda e, o=de, ix=sb["EWID"][:, k8:k8 + 1]:
                   e.indirect_dma_start(o, None, t["l1de"][:, :],
                                        IndirectOffsetOnAxis(ap=ix, axis=0)),
                   sem="gdma", v=16)
            K.pool(lambda e, o=ue, ix=sb["EWID"][:, k8:k8 + 1]:
                   e.indirect_dma_start(o, None, t["l1ue"][:, :],
                                        IndirectOffsetOnAxis(ap=ix, axis=0)),
                   sem="gdma", v=16)
            ng = K.sem["gdma"].n
            K.wait("dve", "gdma", ng)
            K.wait("dve", "dve", nxbf)
            K.dve(lambda e, de=de, a=ew8[:, k8:k8 + 1]:
                  e.scalar_tensor_tensor(sb["LNTM"][:, 0:2048], de, 1.0,
                                         sb["XBF"], ALU.bypass, ALU.mult,
                                         accum_out=a))
            K.wait("act", "dve")
            K.act(lambda e, o=TK[:, 272 + k8:273 + k8], i=ew8[:, k8:k8 + 1]:
                  e.activation(out=o, in_=i, func=AF.Silu))
            K.wait("dve", "act")
            K.dve(lambda e, o=TK[:, 280 + k8:281 + k8],
                  a=TK[:, 272 + k8:273 + k8], b=wts[:, k8:k8 + 1]:
                  e.tensor_mul(o, a, b))
            K.wait("dve", "dve")
            if k8 == 0:
                K.dve(lambda e, ue=ue, sc=TK[:, 280:281]:
                      e.tensor_scalar_mul(sb["EACC"][:, 0:2048], ue, sc))
            else:
                K.dve(lambda e, ue=ue, sc=TK[:, 280 + k8:281 + k8]:
                      e.scalar_tensor_tensor(sb["EACC"][:, 0:2048], ue, sc,
                                             sb["EACC"][:, 0:2048], ALU.mult,
                                             ALU.add))
            deb_war[gpar] = K.sem["dve"].n
        nea = K.sem["dve"].n
        K.wait("sp", "dve", nea)
        K.dma("sp", t["e_scr"][tc * 128:(tc + 1) * 128, :], sb["EACC"][:, 0:2048])

    if DEV_STAGE == 4:
        K.wait("sp", "sdma")
        K.dma("sp", t["dbg_e"][:, :], t["e_scr"][:, :])
        K.dma("sp", t["dbg_rt"][:, :], t["rt_all"][:, :])
        return
    hst_sd = [0, 0]
    wslots = [sb["WT2"][:, 0:2048], sb["WT2"][:, 2048:4096],
              sb["WT"][:, 0:2048], sb["WT"][:, 2048:4096]]
    wslot_last = [K.sem["pe"].n] * 4
    for u in range(2):
        for ic in range(64):
            for gi, wsrc in enumerate((t["l1g_t"], t["l1u_t"])):
                si = (2 * ic + gi) % 4
                wtile = wslots[si]
                K.wait("pool", "pe", wslot_last[si])
                K.dma("pool", wtile, wsrc[ic, :, :], sem="w%d" % si)
                K.wait("pe", "w%d" % si)
                bank = gi
                mms = [(wtile[:, dc * 128:(dc + 1) * 128],
                        A4L[:, dc * 1024 + u * 512:dc * 1024 + (u + 1) * 512])
                       for dc in range(NDC)]
                K.mm_group(bank, mms)
                K.wait("dve", "pe", K.bank_rdy[bank])
                dvedst = sb["QST"] if gi == 0 else sb["EB"]
                K.wait("dve", "act")
                K.dve(lambda e, o=dvedst[:, 0:512], p=K.psum[bank]:
                      e.tensor_copy(o, p[:, :]))
                K.bank_war[bank].append(("dve", K.sem["dve"].n))
                wslot_last[si] = K.sem["pe"].n
            hp = ic % 2
            hsl = sb["EM"][:, hp * 512:(hp + 1) * 512]
            K.wait("act", "dve")
            K.act(lambda e: e.activation(out=sb["QT"][:, 0:512],
                                         in_=sb["QST"][:, 0:512], func=AF.Silu))
            K.wait("dve", "act")
            K.wait("dve", "sdma", hst_sd[hp])
            K.dve(lambda e, o=hsl: e.tensor_mul(o, sb["QT"][:, 0:512],
                                                sb["EB"][:, 0:512]))
            nh = K.sem["dve"].n
            K.wait("sp", "dve", nh)
            K.dma("sp", t["h_scr"][ic, :, u * 512:(u + 1) * 512], hsl)
            hst_sd[hp] = K.sem["sdma"].n

    if DEV_STAGE == 12:
        K.wait("sp", "sdma")
        K.wait("sp", "gdma")
        K.dma("sp", t["dbg_e"].ap().bitcast(BF16)[0:1024, 0:4096],
              t["h_scr"][0:32, :, :])
        return
    K.wait("sp", "dve")
    K.wait("sp", "act")
    K.wait("pool", "dve")
    K.wait("pool", "act")
    p2_last = [K.sem["pe"].n] * 2
    for u in range(2):
        for dh in range(2):
            for bk in range(8):
                for dep in K.bank_war[bk]:
                    K.wait("pe", *dep)
                K.bank_war[bk] = []
            for ic in range(64):
                wpar = ic % 2
                wtile = sb["WT"][:, wpar * 1024:(wpar + 1) * 1024]
                htile = (sb["QST"] if ic % 2 == 0 else sb["QT"])
                K.wait("pool", "pe", p2_last[wpar])
                K.dma("pool", wtile,
                      t["l1d_t"][ic, :, dh * 1024:(dh + 1) * 1024],
                      sem="w%d" % wpar)
                K.wait("sp", "pe", p2_last[wpar])
                K.dma("sp", htile[:, 0:512],
                      t["h_scr"][ic, :, u * 512:(u + 1) * 512],
                      sem="h%d" % wpar)
                K.wait("pe", "w%d" % wpar)
                K.wait("pe", "h%d" % wpar)
                for tc4 in range(4):
                    for dco in range(2):
                        bank = tc4 * 2 + dco
                        l = htile[:, tc4 * 128:(tc4 + 1) * 128]
                        r = wtile[:, dco * 512:(dco + 1) * 512]
                        K.pe(lambda e, b=bank, l=l, r=r, s=(ic == 0),
                             tl=(ic == 63):
                             e.matmul(K.psum[b][:, :], l, r,
                                      start=s, stop=tl),
                             sem="pe" if (tc4 == 3 and dco == 1) else None)
                p2_last[ic % 2] = K.sem["pe"].n
            for tc4 in range(4):
                for dco in range(2):
                    K.bank_rdy[tc4 * 2 + dco] = K.sem["pe"].n
            npeall = K.sem["pe"].n
            K.wait("dve", "pe", npeall)
            K.wait("dve", "act")
            for tc4 in range(4):
                for dco in range(2):
                    bank = tc4 * 2 + dco
                    doff = dh * 1024 + dco * 512
                    dst = a4f[:, tc4 * 2048 + doff:tc4 * 2048 + doff + 512]
                    K.dve(lambda e, o=dst, p=K.psum[bank]:
                          e.tensor_copy(o, p[:, :]))
                    K.bank_war[bank].append(("dve", K.sem["dve"].n))
        nfa = K.sem["act"].n
        for tc4 in range(4):
            tc = u * 4 + tc4
            K.wait("sp", "dve")
            K.dma("sp", sb["EACC"][:, 0:2048],
                  t["e_scr"][tc * 128:(tc + 1) * 128, :])
            nel = K.sem["sdma"].n
            K.wait("dve", "sdma", nel)
            K.wait("dve", "dve")
            fap = a4f[:, tc4 * 2048:(tc4 + 1) * 2048]
            K.dve(lambda e, f=fap: e.tensor_add(sb["FTC"][:, :], f,
                                                sb["EACC"][:, 0:2048]))
            fap = sb["FTC"][:, :]
            if DEV_STAGE == 13:
                nf = K.sem["dve"].n
                K.wait("sp", "dve", nf)
                K.dma("sp", t["dbg_e"][tc * 128:(tc + 1) * 128, :], fap)
                continue
            emit_ln_residual_store(K, st, tc, fap, nfa, x_src, x_dst, final)
        if DEV_STAGE == 13 and u == 1:
            return



def kernel(**inputs):
    in_maps = stage_inputs(inputs)
    nc = build_program()
    res = run_bass_kernel_spmd(nc, in_maps, core_ids=list(range(8)))
    return unstage_output(res.results)
